# version 42
# baseline (speedup 1.0000x reference)
import math
import numpy as np
import ml_dtypes
import concourse.bass as bass
import concourse.mybir as mybir
from concourse.bass_utils import run_bass_kernel_spmd

F32 = mybir.dt.float32
BF16 = mybir.dt.bfloat16
AF = mybir.ActivationFunctionType
ALU = mybir.AluOpType
AX = mybir.AxisListType

N_CORES = 8
T = 4096
D = 1024
NT = T // 128
BF = ml_dtypes.bfloat16

ENGINES = ("sync", "tensor", "vector", "scalar", "gpsimd")
DMA_QUEUES = ("sync", "gpsimd", "scalar")

LAUNCH_PLAN = [["A", "B", "C1", "C1b", "C3",
                "D0", "D1", "D2", "D3", "E"]]

PH_IO = {
    "A": (("x0",), ()),
    "B": (("x0",), ("x1",)),
    "C1": (("x1",), ()),
    "C1b": (("x1",), ("x2",)),
    "C3": (("x2",), ("x2",)),
    "D0": ((), ()),
    "D1": (("x2",), ()),
    "D2": ((), ()),
    "D3": (("x2",), ("x3",)),
    "E": (("x3",), ("out",)),
}


class Phase:
    def __init__(self, prog, name):
        self.prog = prog
        self.nc = prog.nc
        self.name = name
        self.steps = []
        self._ctxs = []

    def sb(self, name, shape, dtype):
        cm = self.nc.sbuf_tensor(f"{self.name}_{name}", shape, dtype)
        t = cm.__enter__()
        self._ctxs.append(cm)
        return t

    def ps(self, name, shape, dtype=F32):
        cm = self.nc.psum_tensor(f"{self.name}_{name}", shape, dtype)
        t = cm.__enter__()
        self._ctxs.append(cm)
        return t

    def step(self, eng, fn, reads=(), writes=(), extra_waits=()):
        assert eng in ENGINES
        self.steps.append((eng, "c", fn, tuple(reads), tuple(writes), None,
                           tuple(extra_waits)))

    def dma(self, eng, fn, reads=(), writes=(), extra_waits=()):
        assert eng in DMA_QUEUES
        self.steps.append((eng, "d", fn, tuple(reads), tuple(writes), None,
                           tuple(extra_waits)))

    def collective(self, name, fn, reads=(), writes=()):
        cm = self.nc.semaphore(f"cc_{name}")
        sem = cm.__enter__()
        self.prog._sem_ctxs.append(cm)
        self.steps.append(("gpsimd", "cc", fn, tuple(reads), tuple(writes), sem, ()))
        return sem


DMA_POOL = 8
PRUNE_SAME_ENGINE = False


class Prog:
    def __init__(self, nc):
        self.nc = nc
        self.phase_idx = 0
        self._sem_ctxs = []
        self._prog_ctxs = []
        self.st = {}
        self.barrier = self._mksem("barrier")
        self.psem = {e: self._mksem(f"p_{e}") for e in ENGINES}
        self.dpool = {q: [self._mksem(f"d_{q}_{i}") for i in range(DMA_POOL)]
                      for q in DMA_QUEUES}
        self.cnt = {e: 0 for e in ENGINES}
        self.duse = {q: [0] * DMA_POOL for q in DMA_QUEUES}

    def _mksem(self, name):
        cm = self.nc.semaphore(name)
        s = cm.__enter__()
        self._sem_ctxs.append(cm)
        return s

    def sb(self, name, shape, dtype):
        cm = self.nc.sbuf_tensor(f"g_{name}", shape, dtype)
        t = cm.__enter__()
        self._prog_ctxs.append(cm)
        return t

    def mark(self):
        return len(self._prog_ctxs)

    def release(self, idx):
        while len(self._prog_ctxs) > idx:
            self._prog_ctxs.pop().__exit__(None, None, None)

    def close(self):
        self.release(0)

    def flush_phase(self, ph):
        nc = self.nc
        pidx = self.phase_idx
        psem, dpool = self.psem, self.dpool
        cnt, duse = self.cnt, self.duse

        last_write = {}
        readers = {}

        emit = {e: [] for e in ENGINES}

        for eng, kind, fn, reads, writes, cc_sem, extra_waits in ph.steps:
            deps = []
            for tok in reads:
                d = last_write.get(tok)
                if d:
                    deps.append(d)
            for tok in writes:
                d = last_write.get(tok)
                if d:
                    deps.append(d)
                deps.extend(readers.get(tok, ()))
            waits = {}
            for d in deps:
                dk, dsrc, dval = d
                if dk == "c" and dsrc == eng and kind == "c" and (
                        PRUNE_SAME_ENGINE or eng == "tensor"):
                    continue
                if dk == "c":
                    sem = psem[dsrc]
                elif dk == "d":
                    sem = dpool[dsrc[0]][dsrc[1]]
                else:
                    sem = dsrc
                key = id(sem)
                if key not in waits or waits[key][1] < dval:
                    waits[key] = (sem, dval)
            if kind == "c":
                cnt[eng] += 1
                mydep = ("c", eng, cnt[eng])
                inc_sem, inc_val = psem[eng], 1
            elif kind == "d":
                n = sum(duse[eng])
                slot = n % DMA_POOL
                prev = duse[eng][slot]
                if prev:
                    sem = dpool[eng][slot]
                    k = id(sem)
                    if k not in waits or waits[k][1] < 16 * prev:
                        waits[k] = (sem, 16 * prev)
                duse[eng][slot] += 1
                mydep = ("d", (eng, slot), 16 * duse[eng][slot])
                inc_sem, inc_val = dpool[eng][slot], 16
            else:
                mydep = ("cc", cc_sem, 1)
                inc_sem, inc_val = cc_sem, 1
            for tok in writes:
                last_write[tok] = mydep
                readers[tok] = []
            for tok in reads:
                readers.setdefault(tok, []).append(mydep)
            for sem, val in extra_waits:
                k = id(sem)
                if k not in waits or waits[k][1] < val:
                    waits[k] = (sem, val)
            emit[eng].append((list(waits.values()), fn, inc_sem, inc_val))

        bar_target = 5 * pidx

        with nc.Block(f"ph{pidx}_{ph.name}") as block:
            def make(e):
                cnt_e = cnt[e]
                duse_e = {q: list(duse[q]) for q in DMA_QUEUES}

                def body(engobj):
                    if pidx > 0:
                        engobj.wait_ge(self.barrier, bar_target)
                    hw = {}
                    for waits, fn, inc_sem, inc_val in emit[e]:
                        for sem, val in waits:
                            k = id(sem)
                            if hw.get(k, -1) >= val:
                                continue
                            engobj.wait_ge(sem, val)
                            hw[k] = val
                        inst = fn()
                        inst.then_inc(inc_sem, inc_val)
                    if cnt_e:
                        engobj.wait_ge(psem[e], cnt_e)
                    if e in DMA_QUEUES:
                        for i in range(DMA_POOL):
                            if duse_e[e][i]:
                                engobj.wait_ge(dpool[e][i], 16 * duse_e[e][i])
                    engobj.sem_inc(self.barrier, 1)
                body.__name__ = f"b{pidx}_{e}"
                return body
            for e in ENGINES:
                getattr(block, e)(make(e))

        for cm in reversed(ph._ctxs):
            cm.__exit__(None, None, None)
        self.phase_idx += 1



def host_constants():
    C = {}
    C["ident_bf"] = np.eye(128, dtype=BF)
    C["ident_f32"] = np.eye(128, dtype=np.float32)
    NB, MAXD = 32, 128
    deltas = np.arange(-255, 256)
    n = deltas
    ret = (n < 0).astype(np.int64) * (NB // 2)
    na = np.abs(n)
    max_exact = NB // 4
    is_small = na < max_exact
    with np.errstate(divide="ignore"):
        val_large = max_exact + (
            np.log(np.maximum(na, 1).astype(np.float32) / max_exact)
            / math.log(MAXD / max_exact) * (NB // 2 - max_exact)
        ).astype(np.int64)
    val_large = np.minimum(val_large, NB // 2 - 1)
    bucket = ret + np.where(is_small, na, val_large)
    oh = np.zeros((NB, 511), dtype=np.float32)
    oh[bucket, np.arange(511)] = 1.0
    C["OH511"] = oh
    hd = 64
    inv_freq = 1.0 / (10000.0 ** (np.arange(0, hd, 2, dtype=np.float32) / hd))
    freqs = np.outer(np.arange(T, dtype=np.float32), inv_freq)
    cos = np.cos(freqs).astype(BF).astype(np.float32)
    sin = np.sin(freqs).astype(BF).astype(np.float32)
    C["cosrep"] = np.tile(cos, (1, 16))
    C["sinrep"] = np.tile(sin, (1, 16))
    C["T8"] = np.triu(np.ones((8, 8), dtype=np.float32))
    return C



class HbmReg:
    def __init__(self, nc, ext_in, ext_out):
        self.nc = nc
        self.ext_in = set(ext_in)
        self.ext_out = set(ext_out)
        self.map = {}
        self.shapes = {}

    def get(self, name, shape, dtype):
        if name in self.map:
            return self.map[name]
        if name in self.ext_in:
            kind = "ExternalInput"
        elif name in self.ext_out:
            kind = "ExternalOutput"
        else:
            kind = "Internal"
        ap = self.nc.dram_tensor(name, list(shape), dtype, kind=kind).ap()
        self.map[name] = ap
        self.shapes[name] = (tuple(shape), dtype)
        return ap


STREAMS = {
    "x1": ((T, D), F32),
    "x2": ((T, D), F32),
    "x3": ((T, D), F32),
}


PH_GROUP = {"A": "AB", "B": "AB",
            "C1": "C", "C1b": "C", "C3": "C",
            "D0": "D", "D1": "D", "D2": "D", "D3": "D",
            "E": "E"}


def build_program(phases, ext_in_names, ext_out_names):
    nc = bass.Bass("TRN2", target_bir_lowering=False, debug=False)
    reg = HbmReg(nc, ext_in_names, ext_out_names)
    prog = Prog(nc)
    cur_group, group_mark = None, 0
    for name in phases:
        g = PH_GROUP[name]
        if g != cur_group:
            if cur_group is not None:
                prog.release(group_mark)
            cur_group, group_mark = g, prog.mark()
        ph = Phase(prog, name)
        PHASE_EMITTERS[name](ph, reg)
        prog.flush_phase(ph)
    prog.close()
    return nc, reg, prog



def ln_tiles(ph, nc, src_ap_fn, ntiles, out_fn, name, eng_stats="vector"):
    raise NotImplementedError



class LNHelper:

    def __init__(self, ph, tag, nslots=2):
        self.ph = ph
        self.nc = ph.nc
        self.stats = [ph.sb(f"{tag}_stats{i}", [128, 12], F32) for i in range(nslots)]
        self.mv = [ph.sb(f"{tag}_mv{i}", [128, 2], F32) for i in range(nslots)]
        self.rstd = [ph.sb(f"{tag}_rstd{i}", [128, 1], F32) for i in range(nslots)]
        self.tag = tag

    def emit(self, slot, x_ap, x_tok, y_ap, y_tok):
        ph, nc = self.ph, self.nc
        st, mv, rs = self.stats[slot], self.mv[slot], self.rstd[slot]
        ST, MV, RS = (self.tag, "st", slot), (self.tag, "mv", slot), (self.tag, "rs", slot)
        ph.step("vector", lambda: nc.vector.bn_stats(st[:, 0:6], x_ap[:, 0:512]),
                reads=[x_tok], writes=[ST])
        ph.step("vector", lambda: nc.vector.bn_stats(st[:, 6:12], x_ap[:, 512:1024]),
                reads=[x_tok], writes=[ST])
        ph.step("vector", lambda: nc.vector.bn_aggr(mv[:], st[:]), reads=[ST], writes=[MV])
        ph.step("vector", lambda: nc.vector.tensor_scalar(rs[:], mv[:, 1:2], 1e-5, None, ALU.add),
                reads=[MV], writes=[RS])
        ph.step("scalar", lambda: nc.scalar.activation(rs[:], rs[:], AF.Sqrt),
                reads=[RS], writes=[RS])
        ph.step("vector", lambda: nc.vector.reciprocal(rs[:], rs[:]), reads=[RS], writes=[RS])
        ph.step("vector", lambda: nc.vector.tensor_scalar(y_ap[:], x_ap[:], mv[:, 0:1], rs[:],
                                                          ALU.subtract, ALU.mult),
                reads=[x_tok, MV, RS], writes=[y_tok])



def emit_A(ph, reg):
    nc, prog = ph.nc, ph.prog
    st = prog.st
    x0 = reg.get("x0", (T, D), F32)
    v0 = reg.get("v0_hbm", (T, 2048), BF16)
    Wh = reg.get("L0_attn_W_h", (D, 4096), F32)
    Wqk = reg.get("L0_attn_W_qk", (D, 128), F32)
    Wo = reg.get("L0_attn_W_o", (2048, D), F32)
    bh = reg.get("L0_attn_b_h", (1, 4096), F32)
    bqk = reg.get("L0_attn_b_qk", (128, 1), F32)
    bo = reg.get("L0_attn_b_o", (1, D), F32)
    ng = reg.get("L0_attn_norm_g", (1, D), F32)
    nb = reg.get("L0_attn_norm_b", (1, D), F32)
    gam = reg.get("L0_attn_qk_gamma", (4, 128), F32)
    bet = reg.get("L0_attn_qk_beta", (4, 128), F32)
    rel = reg.get("L0_attn_rel_emb", (32, 1), F32)
    identb = reg.get("ident_bf", (128, 128), BF16)
    identf = reg.get("ident_f32", (128, 128), F32)
    ohrev = reg.get("OH511", (32, 511), F32)
    browrep = reg.get("browrep_hbm", (128, 511), F32)

    st["idb"] = prog.sb("idb", [128, 128], BF16)
    st["idf"] = prog.sb("idf", [128, 128], F32)
    st["Wqk_fm"] = prog.sb("Wqk_fm", [128, 8 * 128], BF16)
    st["Whg"] = prog.sb("Whg", [128, 8 * 2048], BF16)
    st["Wo_sb"] = prog.sb("Wo_sb", [128, 16 * D], BF16)
    st["gbc"] = prog.sb("gbc", [128, D], F32)
    st["bbc"] = prog.sb("bbc", [128, D], F32)
    st["bh_row"] = prog.sb("bh_row", [1, 4096], BF16)
    st["bo_row"] = prog.sb("bo_row", [1, D], BF16)
    st["bqkT"] = prog.sb("bqkT", [128, 1], F32)
    st["bhgT"] = prog.sb("bhgT", [128, 16], F32)
    st["gbT"] = prog.sb("gbT", [128, 8], F32)
    st["gq"] = prog.sb("gq", [128, 2], F32)
    st["ones_b"] = prog.sb("ones_b", [1, 128], BF16)
    st["ones_f"] = prog.sb("ones_f", [1, 128], F32)
    st["biasT"] = prog.sb("biasT", [128, 2 * 256], F32)
    st["linkv"] = prog.sb("linkv", [128, 2048], BF16)

    ln1 = LNHelper(ph, "ln1", 2)
    ln2 = LNHelper(ph, "ln2", 2)
    xg = [ph.sb(f"xg{j}", [128, D], F32) for j in range(2)]
    yb = [ph.sb(f"yb{j}", [128, D], F32) for j in range(2)]
    nrm = [ph.sb(f"nrm{j}", [128, D], F32) for j in range(2)]
    nrmb = [ph.sb(f"nrmb{j}", [128, D], BF16) for j in range(2)]
    nT2 = [ph.sb(f"nT{i}", [128, 8 * 256], BF16) for i in range(2)]
    qk2 = [ph.sb(f"qk{i}", [128, 256], F32) for i in range(2)]
    lkf2 = [ph.sb(f"lkf{i}", [128, 256], BF16) for i in range(2)]
    lktm2 = [[ph.sb(f"lktm{i}_{j}", [128, 128], BF16) for j in range(2)] for i in range(2)]
    vbf2 = [[ph.sb(f"vbf{i}_{j}", [128, 2048], BF16) for j in range(2)] for i in range(2)]
    st["Whv"] = ph.sb("Whv", [128, 8 * 2048], BF16)
    pqk = ph.ps("pqk", [128, 256], F32)
    ptA = ph.ps("ptA", [128, 128], BF16)
    pv = [ph.ps(f"pv{j}", [128, 512], F32) for j in range(2)]
    pkv = ph.ps("pkv", [128, 2048], F32)


    ph.step("gpsimd", lambda: nc.gpsimd.memset(st["ones_b"][:], 1.0), writes=["ones_b"])
    ph.step("gpsimd", lambda: nc.gpsimd.memset(st["ones_f"][:], 1.0), writes=["ones_f"])
    ph.dma("sync", lambda: nc.sync.dma_start(out=st["idb"][:], in_=identb), writes=["idb"])
    ph.dma("sync", lambda: nc.sync.dma_start(out=st["idf"][:], in_=identf), writes=["idf"])

    for dc in range(8):
        ph.dma("gpsimd", lambda dc=dc: nc.gpsimd.dma_start(
            out=st["Wqk_fm"][:, 128 * dc:128 * (dc + 1)], in_=Wqk[128 * dc:128 * (dc + 1), :]),
            writes=["Wqk_fm"])
    ph.dma("gpsimd", lambda: nc.gpsimd.dma_start(out=st["bh_row"][:], in_=bh), writes=["bh_row"])
    ph.dma("gpsimd", lambda: nc.gpsimd.dma_start(out=st["bo_row"][:], in_=bo), writes=["bo_row"])
    ph.dma("sync", lambda: nc.sync.dma_start(out=st["bqkT"][:], in_=bqk), writes=["bqkT"])
    ph.dma("gpsimd", lambda: nc.gpsimd.dma_start(
        out=st["bhgT"][:], in_=bh[0:1, 2048:4096].rearrange("o (c p) -> (o p) c", p=128),
        allow_slow_non_contiguous=True), writes=["bhgT"])

    rowbuf = ph.sb("rowbuf", [1, D], F32)
    pbc = pv[0]
    for nm, src, dst in (("g", ng, "gbc"), ("b", nb, "bbc")):
        ph.dma("sync", lambda src=src: nc.sync.dma_start(out=rowbuf[:], in_=src),
               writes=["rowbuf"])
        for h in range(2):
            ph.step("tensor", lambda h=h: nc.tensor.matmul(
                pbc[:], st["ones_f"][:], rowbuf[0:1, 512 * h:512 * (h + 1)], start=True, stop=True),
                reads=["rowbuf", "ones_f"], writes=[("pv", 0)])
            ph.step("vector", lambda h=h, dst=dst: nc.vector.tensor_copy(
                st[dst][:, 512 * h:512 * (h + 1)], pbc[:]),
                reads=[("pv", 0)], writes=[dst])

    gbrows = ph.sb("gbrows", [8, 128], F32)
    pgb = pv[1][:, 0:8]
    ph.dma("sync", lambda: nc.sync.dma_start(out=gbrows[0:4, :], in_=gam), writes=["gbrows"])
    ph.dma("sync", lambda: nc.sync.dma_start(out=gbrows[4:8, :], in_=bet), writes=["gbrows"])
    ph.step("tensor", lambda: nc.tensor.transpose(pgb, gbrows[:], st["idf"][0:8, 0:8]),
            reads=["gbrows", "idf"], writes=[("pv", 1)])
    ph.step("vector", lambda: nc.vector.tensor_copy(st["gbT"][:], pgb), reads=[("pv", 1)], writes=["gbT"])
    ph.step("vector", lambda: nc.vector.tensor_scalar(st["gq"][:, 0:1], st["gbT"][:, 0:1],
                                                      1.0 / 256.0, None, ALU.mult),
            reads=["gbT"], writes=["gq"])
    ph.step("vector", lambda: nc.vector.tensor_scalar(st["gq"][:, 1:2], st["gbT"][:, 4:5],
                                                      1.0 / 256.0, None, ALU.mult),
            reads=["gbT"], writes=["gq"])

    relsb = ph.sb("relsb", [32, 1], F32)
    ohsb = ph.sb("ohsb", [32, 511], F32)
    brow = ph.sb("brow", [1, 511], F32)
    browb = ph.sb("browb", [128, 511], F32)
    pb1 = pkv[0:1, 0:511]
    pb2 = pkv[:, 512:1023]
    ph.dma("sync", lambda: nc.sync.dma_start(out=relsb[:], in_=rel), writes=["relsb"])
    ph.dma("sync", lambda: nc.sync.dma_start(out=ohsb[:], in_=ohrev), writes=["ohsb"])
    ph.step("vector", lambda: nc.vector.tensor_scalar(relsb[:], relsb[:], float(math.sqrt(128.0)),
                                                      None, ALU.mult),
            reads=["relsb"], writes=["relsb"])
    ph.step("tensor", lambda: nc.tensor.matmul(pb1, relsb[:], ohsb[:], start=True, stop=True),
            reads=["relsb", "ohsb"], writes=[("pkv", 0)])
    ph.step("vector", lambda: nc.vector.tensor_copy(brow[:], pb1), reads=[("pkv", 0)], writes=["brow"])
    ph.step("tensor", lambda: nc.tensor.matmul(pb2, st["ones_f"][:], brow[:], start=True, stop=True),
            reads=["brow", "ones_f"], writes=[("pkv", 1)])
    ph.step("vector", lambda: nc.vector.tensor_copy(browb[:], pb2), reads=[("pkv", 1)], writes=["browb"])
    ph.dma("sync", lambda: nc.sync.dma_start(out=browrep, in_=browb[:]), reads=["browb"], writes=["browrep"])
    for jt in range(2):
        def mk(jt=jt):
            src = browrep.copy()
            pair_t = type(src.ap)
            src.ap = pair_t([(510, 128), (1, 256)])
            src.offset = 255 - 128 * jt
            return nc.sync.dma_start(out=st["biasT"][:, 256 * jt:256 * (jt + 1)], in_=src)
        ph.dma("sync", mk, reads=["browrep"], writes=["biasT"])

    for dc in range(8):
        ph.dma("gpsimd", lambda dc=dc: nc.gpsimd.dma_start(
            out=st["Whv"][:, 2048 * dc:2048 * (dc + 1)],
            in_=Wh[128 * dc:128 * (dc + 1), 0:2048]),
            writes=["Whv"])

    for g in range(16):
        G = g % 2
        nT, qk, lkf, lktm, vbf = nT2[G], qk2[G], lkf2[G], lktm2[G], vbf2[G]
        ptA2 = pqk.bitcast(BF16)[:, 0:128]
        if g < 8:
            ph.dma("gpsimd", lambda g=g: nc.gpsimd.dma_start(
                out=st["Whg"][:, 2048 * g:2048 * (g + 1)], in_=Wh[128 * g:128 * (g + 1), 2048:4096]),
                writes=["Whg"])
        for ec in (2 * g, 2 * g + 1):
            if ec < 16:
                ph.dma("gpsimd", lambda ec=ec: nc.gpsimd.dma_start(
                    out=st["Wo_sb"][:, D * ec:D * (ec + 1)], in_=Wo[128 * ec:128 * (ec + 1), :]),
                    writes=["Wo_sb"])
        for j in range(2):
            t = 2 * g + j
            XG, YB, NRM, NRMB = ("xg", j), ("yb", j), ("nrm", j), ("nrmb", j)
            ph.dma("sync", lambda t=t, j=j: nc.sync.dma_start(
                out=xg[j][:], in_=x0[128 * t:128 * (t + 1), :]), writes=[XG])
            ln1.emit(j, xg[j], XG, yb[j], YB)
            ln2.emit(j, yb[j], YB, nrm[j], NRM)
            ph.step("vector", lambda j=j: nc.vector.tensor_tensor(nrm[j][:], nrm[j][:], st["gbc"][:], ALU.mult),
                    reads=[NRM, "gbc"], writes=[NRM])
            ph.step("vector", lambda j=j: nc.vector.tensor_tensor(nrmb[j][:], nrm[j][:], st["bbc"][:], ALU.add),
                    reads=[NRM, "bbc"], writes=[NRMB])
            for dc in range(8):
                pt = ptA if dc % 2 == 0 else ptA2
                PTK = "ptA" if dc % 2 == 0 else "pqk"
                ph.step("tensor", lambda j=j, dc=dc, pt=pt: nc.tensor.transpose(
                    pt[:], nrmb[j][:, 128 * dc:128 * (dc + 1)], st["idb"][:]),
                    reads=[NRMB, "idb"], writes=[PTK])
                if dc % 2 == 0:
                    ph.step("scalar", lambda j=j, dc=dc, nT=nT, pt=pt: nc.scalar.copy(
                        nT[:, 256 * dc + 128 * j:256 * dc + 128 * (j + 1)], pt[:]),
                        reads=[PTK], writes=[("nT", G, j, dc)])
                else:
                    ph.step("vector", lambda j=j, dc=dc, nT=nT, pt=pt: nc.vector.tensor_copy(
                        nT[:, 256 * dc + 128 * j:256 * dc + 128 * (j + 1)], pt[:]),
                        reads=[PTK], writes=[("nT", G, j, dc)])
        for dc in range(8):
            ph.step("tensor", lambda dc=dc, nT=nT: nc.tensor.matmul(
                pqk[:], st["Wqk_fm"][:, 128 * dc:128 * (dc + 1)], nT[:, 256 * dc:256 * (dc + 1)],
                start=(dc == 0), stop=(dc == 7)),
                reads=[("nT", G, 0, dc), ("nT", G, 1, dc), "Wqk_fm"], writes=["pqk"])
        ph.step("scalar", lambda qk=qk: nc.scalar.activation(qk[:], pqk[:], AF.Silu, bias=st["bqkT"][:]),
                reads=["pqk", "bqkT"], writes=[("qk", G)])
        ph.step("vector", lambda qk=qk, lkf=lkf: nc.vector.tensor_scalar(lkf[:], qk[:], st["gbT"][:, 3:4],
                                                          st["gbT"][:, 7:8], ALU.mult, ALU.add),
                reads=[("qk", G), "gbT"], writes=[("lkf", G)])
        for j in range(2):
            ph.step("tensor", lambda j=j, lkf=lkf: nc.tensor.transpose(
                ptA[:], lkf[:, 128 * j:128 * (j + 1)], st["idb"][:]),
                reads=[("lkf", G), "idb"], writes=["ptA"])
            ph.step("vector", lambda j=j, lktm=lktm: nc.vector.tensor_copy(lktm[j][:], ptA[:]),
                    reads=["ptA"], writes=[("lktm", G, j)])
        for j in range(2):
            for hc in range(4):
                pvb = pv[hc % 2]
                PV = ("pv", hc % 2)
                for dc in range(8):
                    ph.step("tensor", lambda j=j, hc=hc, dc=dc, pvb=pvb, nT=nT: nc.tensor.matmul(
                        pvb[:], nT[:, 256 * dc + 128 * j:256 * dc + 128 * (j + 1)],
                        st["Whv"][:, 2048 * dc + 512 * hc:2048 * dc + 512 * (hc + 1)],
                        start=(dc == 0), stop=False),
                        reads=[("nT", G, j, dc), "Whv"], writes=[PV])
                ph.step("tensor", lambda hc=hc, pvb=pvb: nc.tensor.matmul(
                    pvb[:], st["ones_b"][:], st["bh_row"][0:1, 512 * hc:512 * (hc + 1)],
                    start=False, stop=True),
                    reads=["ones_b", "bh_row"], writes=[PV])
                ph.step("scalar", lambda j=j, hc=hc, pvb=pvb, vbf=vbf: nc.scalar.activation(
                    vbf[j][:, 512 * hc:512 * (hc + 1)], pvb[:], AF.Silu),
                    reads=[PV], writes=[("vbf", G, j, hc)])
            ph.dma("gpsimd", lambda g=g, j=j, vbf=vbf: nc.gpsimd.dma_start(
                out=v0[256 * g + 128 * j:256 * g + 128 * (j + 1), :], in_=vbf[j][:]),
                reads=[("vbf", G, j, 0), ("vbf", G, j, 1), ("vbf", G, j, 2), ("vbf", G, j, 3)],
                writes=[("v0h", g, j)])
            for hc in range(4):
                ph.step("tensor", lambda g=g, j=j, hc=hc, lktm=lktm, vbf=vbf: nc.tensor.matmul(
                    pkv[:, 512 * hc:512 * (hc + 1)], lktm[j][:],
                    vbf[j][:, 512 * hc:512 * (hc + 1)],
                    start=(g == 0 and j == 0), stop=(g == 15 and j == 1),
                    skip_group_check=True),
                    reads=[("lktm", G, j), ("vbf", G, j, hc)], writes=[("pkv", hc)])
    for hc in range(4):
        ph.step("vector", lambda hc=hc: nc.vector.tensor_scalar(
            st["linkv"][:, 512 * hc:512 * (hc + 1)], pkv[:, 512 * hc:512 * (hc + 1)],
            1.0 / T, None, ALU.mult),
            reads=[("pkv", hc)], writes=["linkv"])



def emit_B(ph, reg):
    nc, prog = ph.nc, ph.prog
    st = prog.st
    x0 = reg.get("x0", (T, D), F32)
    v0 = reg.get("v0_hbm", (T, 2048), BF16)
    x1 = reg.get("x1", *STREAMS["x1"])

    ln1 = LNHelper(ph, "ln1", 2)
    ln2 = LNHelper(ph, "ln2", 2)
    xg = [ph.sb(f"xg{j}", [128, D], F32) for j in range(2)]
    yb = [ph.sb(f"yb{j}", [128, D], F32) for j in range(2)]
    rsum = [ph.sb(f"rsum{j}", [128, D], F32) for j in range(8)]
    nrm = [ph.sb(f"nrm{j}", [128, D], F32) for j in range(1)]
    nrmb = [ph.sb(f"nrmb{j}", [128, D], BF16) for j in range(2)]
    nT2 = [ph.sb(f"nT{i}", [128, 8 * 512], BF16) for i in range(2)]
    qk = ph.sb("qk", [128, 512], F32)
    qqb = ph.sb("qqb", [128, 512], BF16)
    qkb = ph.sb("qkb", [128, 512], BF16)
    lqb = ph.sb("lqb", [128, 512], BF16)
    gate = ph.sb("gate", [128, 16 * 512], BF16)
    s1 = ph.sb("s1", [128, 256], F32)
    attnT = [ph.sb(f"attnT{jh}", [128, 256], BF16) for jh in range(2)]
    vin = [ph.sb(f"vin{j}", [128, 2048], BF16) for j in range(4)]
    x1t = [ph.sb(f"x1t{j}", [128, D], F32) for j in range(1)]
    pqk = ph.ps("pqk", [128, 512], F32)
    ptB = ph.ps("ptB", [128, 128], BF16)
    pg = [ph.ps(f"pg{i}", [128, 512], F32) for i in range(2)]
    ps_s = pqk[:, 0:256]
    pe = [ph.ps(f"pe{i}", [128, 256], F32) for i in range(2)]
    po = ph.ps("po", [128, D], F32)

    ptB2 = pg[1].bitcast(BF16)[:, 0:128]

    def prep(p):
        P = p % 2
        nT = nT2[P]
        for jj in range(4):
            t = 4 * p + jj
            b = jj % 2
            s8 = t % 8
            XG, YB, RS = ("xg", b), ("yb", b), ("rs", s8)
            NRM, NRMB = ("nrm", 0), ("nrmb", b)
            ph.dma("sync", lambda t=t, b=b: nc.sync.dma_start(
                out=xg[b][:], in_=x0[128 * t:128 * (t + 1), :]), writes=[XG])
            ln1.emit(b, xg[b], XG, yb[b], YB)
            ln2.emit(b, yb[b], YB, nrm[0], NRM)
            ph.step("vector", lambda b=b, s8=s8: nc.vector.tensor_tensor(
                rsum[s8][:], xg[b][:], yb[b][:], ALU.add),
                reads=[XG, YB], writes=[RS])
            ph.step("vector", lambda: nc.vector.tensor_tensor(
                nrm[0][:], nrm[0][:], st["gbc"][:], ALU.mult),
                reads=[NRM, "gbc"], writes=[NRM])
            ph.step("vector", lambda b=b: nc.vector.tensor_tensor(
                nrmb[b][:], nrm[0][:], st["bbc"][:], ALU.add),
                reads=[NRM, "bbc"], writes=[NRMB])
            for dc in range(8):
                pt = ptB if dc % 2 == 0 else ptB2
                PTK = "ptB" if dc % 2 == 0 else ("pg", 1)
                ph.step("tensor", lambda b=b, dc=dc, jj=jj, pt=pt: nc.tensor.transpose(
                    pt[:], nrmb[b][:, 128 * dc:128 * (dc + 1)], st["idb"][:]),
                    reads=[NRMB, "idb"], writes=[PTK])
                if dc % 2 == 0:
                    ph.step("scalar", lambda dc=dc, jj=jj, nT=nT, pt=pt: nc.scalar.copy(
                        nT[:, 512 * dc + 128 * jj:512 * dc + 128 * (jj + 1)], pt[:]),
                        reads=[PTK], writes=[("nT", P, jj, dc)])
                else:
                    ph.step("vector", lambda dc=dc, jj=jj, nT=nT, pt=pt: nc.vector.tensor_copy(
                        nT[:, 512 * dc + 128 * jj:512 * dc + 128 * (jj + 1)], pt[:]),
                        reads=[PTK], writes=[("nT", P, jj, dc)])

    prep(0)
    for p in range(8):
        P = p % 2
        nT = nT2[P]
        nt_toks = lambda dc, P=P: [("nT", P, jj, dc) for jj in range(4)]
        for jj in range(4):
            t = 4 * p + jj
            vs = t % 4
            ph.dma("gpsimd", lambda t=t, vs=vs: nc.gpsimd.dma_start(
                out=vin[vs][:], in_=v0[128 * t:128 * (t + 1), :]),
                writes=[("vin", vs)])
        for dc in range(8):
            ph.step("tensor", lambda dc=dc, nT=nT: nc.tensor.matmul(
                pqk[:], st["Wqk_fm"][:, 128 * dc:128 * (dc + 1)],
                nT[:, 512 * dc:512 * (dc + 1)], start=(dc == 0), stop=(dc == 7)),
                reads=nt_toks(dc) + ["Wqk_fm"], writes=["pqk"])
        ph.step("scalar", lambda: nc.scalar.activation(qk[:], pqk[:], AF.Silu, bias=st["bqkT"][:]),
                reads=["pqk", "bqkT"], writes=["qk"])
        ph.step("vector", lambda: nc.vector.tensor_scalar(qqb[:], qk[:], st["gq"][:, 0:1],
                                                          st["gq"][:, 1:2], ALU.mult, ALU.add),
                reads=["qk", "gq"], writes=["qqb"])
        ph.step("vector", lambda: nc.vector.tensor_scalar(qkb[:], qk[:], st["gbT"][:, 2:3],
                                                          st["gbT"][:, 6:7], ALU.mult, ALU.add),
                reads=["qk", "gbT"], writes=["qkb"])
        ph.step("vector", lambda: nc.vector.tensor_scalar(lqb[:], qk[:], st["gbT"][:, 1:2],
                                                          st["gbT"][:, 5:6], ALU.mult, ALU.add),
                reads=["qk", "gbT"], writes=["lqb"])
        for hc in range(16):
            pgb = pg[hc % 2]
            PG = ("pg", hc % 2)
            for dc in range(8):
                ph.step("tensor", lambda hc=hc, dc=dc, pgb=pgb, nT=nT: nc.tensor.matmul(
                    pgb[:], st["Whg"][:, 2048 * dc + 128 * hc:2048 * dc + 128 * (hc + 1)],
                    nT[:, 512 * dc:512 * (dc + 1)], start=(dc == 0), stop=(dc == 7)),
                    reads=nt_toks(dc) + ["Whg"], writes=[PG])
            ph.step("scalar", lambda hc=hc, pgb=pgb: nc.scalar.activation(
                gate[:, 512 * hc:512 * (hc + 1)], pgb[:], AF.Silu, bias=st["bhgT"][:, hc:hc + 1]),
                reads=[PG, "bhgT"], writes=[("gate", hc)])
        if p + 1 < 8:
            prep(p + 1)
        for gg in range(2):
            g = 2 * p + gg
            qoff = 256 * gg
            for jh in range(2):
                ph.step("tensor", lambda jh=jh, qoff=qoff: nc.tensor.matmul(
                    ps_s, qkb[:, qoff + 128 * jh:qoff + 128 * (jh + 1)],
                    qqb[:, qoff:qoff + 256], start=True, stop=True),
                    reads=["qkb", "qqb"], writes=["pqk"])
                ph.step("vector", lambda jh=jh: nc.vector.tensor_tensor(
                    s1[:], ps_s, st["biasT"][:, 256 * jh:256 * (jh + 1)], ALU.add),
                    reads=["pqk", "biasT"], writes=["s1"])
                ph.step("vector", lambda: nc.vector.tensor_scalar(s1[:], s1[:], 0.0, None, ALU.max),
                        reads=["s1"], writes=["s1"])
                ph.step("vector", lambda jh=jh: nc.vector.tensor_tensor(
                    attnT[jh][:], s1[:], s1[:], ALU.mult),
                    reads=["s1"], writes=[("attnT", jh)])
            for ec in range(16):
                peb = pe[ec % 2]
                PE_ = ("pe", ec % 2)
                for j in range(2):
                    vs = (4 * p + 2 * gg + j) % 4
                    ph.step("tensor", lambda ec=ec, j=j, vs=vs, peb=peb: nc.tensor.matmul(
                        peb[:], vin[vs][:, 128 * ec:128 * (ec + 1)], attnT[j][:],
                        start=(j == 0), stop=False),
                        reads=[("vin", vs), ("attnT", j)], writes=[PE_])
                ph.step("tensor", lambda ec=ec, qoff=qoff, peb=peb: nc.tensor.matmul(
                    peb[:], st["linkv"][:, 128 * ec:128 * (ec + 1)], lqb[:, qoff:qoff + 256],
                    start=False, stop=True),
                    reads=["linkv", "lqb"], writes=[PE_])
                ph.step("vector", lambda ec=ec, qoff=qoff, peb=peb: nc.vector.tensor_tensor(
                    gate[:, 512 * ec + qoff:512 * ec + qoff + 256], peb[:],
                    gate[:, 512 * ec + qoff:512 * ec + qoff + 256], ALU.mult),
                    reads=[PE_, ("gate", ec)], writes=[("gated", ec, gg)])
            for j in range(2):
                jj = 2 * gg + j
                t = 4 * p + jj
                for ec in range(16):
                    for h in range(2):
                        ph.step("tensor", lambda ec=ec, h=h, qoff=qoff, j=j: nc.tensor.matmul(
                            po[:, 512 * h:512 * (h + 1)],
                            gate[:, 512 * ec + qoff + 128 * j:512 * ec + qoff + 128 * (j + 1)],
                            st["Wo_sb"][:, D * ec + 512 * h:D * ec + 512 * (h + 1)],
                            start=(ec == 0), stop=False),
                            reads=[("gated", ec, gg), "Wo_sb"], writes=[("po", h)])
                for h in range(2):
                    ph.step("tensor", lambda h=h: nc.tensor.matmul(
                        po[:, 512 * h:512 * (h + 1)], st["ones_b"][:],
                        st["bo_row"][0:1, 512 * h:512 * (h + 1)], start=False, stop=True),
                        reads=["ones_b", "bo_row"], writes=[("po", h)])
                xb = 0
                s8 = t % 8
                ph.step("vector", lambda s8=s8, xb=xb: nc.vector.tensor_tensor(
                    x1t[xb][:], po[:], rsum[s8][:], ALU.add),
                    reads=[("po", 0), ("po", 1), ("rs", s8)], writes=[("x1t", xb)])
                ph.dma("sync", lambda t=t, xb=xb: nc.sync.dma_start(
                    out=x1[128 * t:128 * (t + 1), :], in_=x1t[xb][:]),
                    reads=[("x1t", xb)], writes=[("x1h", t)])



def emit_C1(ph, reg):
    nc, prog = ph.nc, ph.prog
    st = prog.st
    x1 = reg.get("x1", *STREAMS["x1"])
    cc1_in = reg.get("cc1_in", (9, 1024), F32)
    cc1_out = reg.get("cc1_out", (9, 1024), F32)
    sw1 = reg.get("L0_ffn_sw1", (D, D), F32)
    sw2 = reg.get("L0_ffn_sw2", (D, D), F32)
    sw3 = reg.get("L0_ffn_sw3", (D, D), F32)
    wg = reg.get("L0_ffn_w_gate", (8, D), F32)
    eb = reg.get("L0_ffn_expert_bias", (1, 8), F32)
    identb = reg.get("ident_bf", (128, 128), BF16)
    maskin = reg.get("mask_core", (8, 1), F32)
    slotin = reg.get("slot_onehot", (1, 8), F32)
    posin = reg.get("posbase", (1, 1), F32)
    t8in = reg.get("T8", (8, 8), F32)

    st["idbC"] = prog.sb("idbC", [128, 128], BF16)
    st["sw1T"] = prog.sb("sw1T", [128, 8 * D], BF16)
    st["sw2T"] = prog.sb("sw2T", [128, 8 * D], BF16)
    st["sw3T"] = prog.sb("sw3T", [128, 8 * D], BF16)
    st["wgT"] = prog.sb("wgT", [128, 64], BF16)
    st["y2T"] = prog.sb("y2T", [128, 8 * T], BF16)
    st["ebbem"] = prog.sb("ebbem", [128, 256], F32)
    st["oh0"] = prog.sb("oh0", [128, 256], F32)
    st["oh1"] = prog.sb("oh1", [128, 256], F32)
    st["w0"] = prog.sb("w0", [128, 32], F32)
    st["w1s"] = prog.sb("w1s", [128, 32], F32)
    st["onesf"] = prog.sb("onesfC", [1, 128], F32)
    st["onescf"] = prog.sb("onescf", [128, 1], F32)
    st["ones32"] = prog.sb("ones32", [128, 32], F32)
    st["maskc"] = prog.sb("maskc", [8, 1], F32)
    st["posb"] = prog.sb("posb", [1, 1], F32)
    st["T8sb"] = prog.sb("T8sb", [8, 8], F32)
    st["e0i"] = prog.sb("e0i", [128, 32], F32)
    st["e1i"] = prog.sb("e1i", [128, 32], F32)
    st["slotrow"] = prog.sb("slotrow", [1, 8], F32)
    st["m8b"] = prog.sb("m8b", [128, 8], F32)

    ln = LNHelper(ph, "lnC", 2)
    xt = [ph.sb(f"xt{i}", [128, D], F32) for i in range(2)]
    y2 = [ph.sb(f"y2_{i}", [128, D], F32) for i in range(2)]
    y2b = [ph.sb(f"y2b{i}", [128, D], BF16) for i in range(2)]
    logits = ph.sb("logits", [128, 256], F32)
    X8p = ph.sb("X8p", [8, D], F32)
    wrow = [ph.sb(f"wrow{i}", [128, D], BF16) for i in range(2)]
    wgrow = ph.sb("wgrow", [8, D], BF16)
    ebrow = ph.sb("ebrow", [1, 8], F32)
    ebb8 = ph.sb("ebb8", [128, 8], F32)
    ptr = [ph.ps(f"ptr{i}", [128, 128], BF16) for i in range(2)]
    plog = [ph.ps(f"plog{i}", [128, 8], F32) for i in range(2)]
    ptc = ph.ps("ptc", [128, 128], BF16)
    ptg = ph.ps("ptg", [128, 8], BF16)
    pmeb = ph.ps("pmeb", [128, 8], F32)

    ph.dma("sync", lambda: nc.sync.dma_start(out=st["idbC"][:], in_=identb), writes=["idb"])
    ph.dma("sync", lambda: nc.sync.dma_start(out=st["maskc"][:], in_=maskin), writes=["maskc"])
    ph.dma("sync", lambda: nc.sync.dma_start(out=st["slotrow"][:], in_=slotin), writes=["slotrow"])
    ph.dma("sync", lambda: nc.sync.dma_start(out=st["posb"][:], in_=posin), writes=["posb"])
    ph.dma("sync", lambda: nc.sync.dma_start(out=st["T8sb"][:], in_=t8in), writes=["T8sb"])
    ph.step("gpsimd", lambda: nc.gpsimd.memset(st["onesf"][:], 1.0), writes=["onesf"])
    ph.step("gpsimd", lambda: nc.gpsimd.memset(st["onescf"][:], 1.0), writes=["onescf"])
    ph.step("gpsimd", lambda: nc.gpsimd.memset(st["ones32"][:], 1.0), writes=["ones32"])
    ph.dma("gpsimd", lambda: nc.gpsimd.dma_start(out=wgrow[:], in_=wg), writes=["wgrow"])
    for dc in range(8):
        ph.step("tensor", lambda dc=dc: nc.tensor.transpose(
            ptg[:], wgrow[:, 128 * dc:128 * (dc + 1)], st["idbC"][0:8, 0:8]),
            reads=["wgrow", "idb"], writes=["ptg"])
        ph.step("vector", lambda dc=dc: nc.vector.tensor_copy(st["wgT"][:, 8 * dc:8 * (dc + 1)], ptg[:]),
                reads=["ptg"], writes=["wgT"])
    ph.step("tensor", lambda: nc.tensor.matmul(pmeb[:], st["onesf"][:], st["slotrow"][:],
                                               start=True, stop=True),
            reads=["slotrow", "onesf"], writes=["pmeb"])
    ph.step("vector", lambda: nc.vector.tensor_copy(st["m8b"][:], pmeb[:]), reads=["pmeb"], writes=["m8b"])
    ph.dma("sync", lambda: nc.sync.dma_start(out=ebrow[:], in_=eb), writes=["ebrow"])
    ph.step("tensor", lambda: nc.tensor.matmul(pmeb[:], st["onesf"][:], ebrow[:], start=True, stop=True),
            reads=["ebrow", "onesf", "m8b"], writes=["pmeb"])
    ph.step("vector", lambda: nc.vector.tensor_copy(ebb8[:], pmeb[:]), reads=["pmeb"], writes=["ebb8"])
    for e in range(8):
        ph.step("vector", lambda e=e: nc.vector.tensor_scalar(
            st["ebbem"][:, 32 * e:32 * (e + 1)], st["ones32"][:], ebb8[:, e:e + 1], None, ALU.mult),
            reads=["ones32", "ebb8"], writes=["ebbem"])

    sw_srcs = [(sw1, "sw1T"), (sw2, "sw2T"), (sw3, "sw3T")]

    def sw_chunk(k):
        w, dst = sw_srcs[k // 8]
        sc = k % 8
        b = sc % 2
        ph.dma("gpsimd", lambda w=w, sc=sc, b=b: nc.gpsimd.dma_start(
            out=wrow[b][:], in_=w[128 * sc:128 * (sc + 1), :]), writes=[("wrow", b)])
        for dc in range(8):
            ph.step("tensor", lambda sc=sc, dc=dc, b=b: nc.tensor.transpose(
                ptc[:], wrow[b][:, 128 * dc:128 * (dc + 1)], st["idbC"][:]),
                reads=[("wrow", b), "idb"], writes=["ptc"])
            ph.step("vector", lambda dst=dst, sc=sc, dc=dc: nc.vector.tensor_copy(
                st[dst][:, D * dc + 128 * sc:D * dc + 128 * (sc + 1)], ptc[:]),
                reads=["ptc"], writes=[dst])

    for t in range(NT):
        b = t % 2
        XT, Y2, Y2B = ("xt", b), ("y2", b), ("y2b", b)
        ph.dma("sync", lambda t=t, b=b: nc.sync.dma_start(
            out=xt[b][:], in_=x1[128 * t:128 * (t + 1), :]), writes=[XT])
        ln.emit(b, xt[b], XT, y2[b], Y2)
        if t == 0:
            ph.step("vector", lambda b=b: nc.vector.tensor_scalar(
                X8p[:], y2[b][0:8, :], st["maskc"][:], None, ALU.mult),
                reads=[Y2, "maskc"], writes=["X8p"])
        ph.step("vector", lambda b=b: nc.vector.tensor_copy(y2b[b][:], y2[b][:]),
                reads=[Y2], writes=[Y2B])
        for dc in range(8):
            pb = dc % 2
            ph.step("tensor", lambda b=b, dc=dc, pb=pb: nc.tensor.transpose(
                ptr[pb][:], y2b[b][:, 128 * dc:128 * (dc + 1)], st["idbC"][:]),
                reads=[Y2B, "idb"], writes=[("ptr", pb)])
            ph.step("scalar", lambda t=t, dc=dc, pb=pb: nc.scalar.copy(
                st["y2T"][:, T * dc + 128 * t:T * dc + 128 * (t + 1)], ptr[pb][:]),
                reads=[("ptr", pb)], writes=[("y2T", t, dc)])
        pl = plog[b]
        PL = ("plog", b)
        for dc in range(8):
            ph.step("tensor", lambda t=t, dc=dc, pl=pl: nc.tensor.matmul(
                pl[:], st["y2T"][:, T * dc + 128 * t:T * dc + 128 * (t + 1)],
                st["wgT"][:, 8 * dc:8 * (dc + 1)], start=(dc == 0), stop=(dc == 7)),
                reads=[("y2T", t, dc), "wgT"], writes=[PL])
        ph.step("vector", lambda t=t, pl=pl: nc.vector.tensor_copy(logits[:, 8 * t:8 * (t + 1)], pl[:]),
                reads=[PL], writes=["logits"])
        if t < 24:
            sw_chunk(t)

    nm = ph.sb("nm", [128, 32], F32)
    les = ph.sb("les", [128, 256], F32)
    scm = ph.sb("scm", [128, 256], F32)
    sbm = ph.sb("sbm", [128, 256], F32)
    gs = ph.sb("gs", [128, 128], F32)
    rk4 = ph.sb("rk4", [128, 128], F32)
    keep = ph.sb("keep", [128, 128], F32)
    big1 = ph.sb("big1", [128, 128], F32)
    me = ph.sb("me", [128, 256], F32)
    rk8 = ph.sb("rk8", [128, 256], F32)
    ssum = ph.sb("ssum", [128, 32], F32)
    tmpc = ph.sb("tmpc", [128, 32], F32)
    cntpp = ph.sb("cntpp", [128, 8], F32)
    crow = ph.sb("crow", [1, 1024], F32)
    pcnt = ph.ps("pcnt", [1, 8], F32)

    lg3 = logits.rearrange("p (t e) -> p t e", e=8)
    ph.step("vector", lambda: nc.vector.tensor_reduce(nm[:], lg3, AX.X, ALU.max, negate=True),
            reads=["logits"], writes=["nm"])
    for e in range(8):
        ph.step("vector", lambda e=e: nc.vector.tensor_tensor(
            les[:, 32 * e:32 * (e + 1)], lg3[:, :, e], nm[:], ALU.add),
            reads=["logits", "nm"], writes=["les"])
    ph.step("scalar", lambda: nc.scalar.activation(les[:], les[:], AF.Exp),
            reads=["les"], writes=["les"])
    ph.step("vector", lambda: nc.vector.tensor_tensor(ssum[:], les[:, 0:32], les[:, 32:64], ALU.add),
            reads=["les"], writes=["ssum"])
    for e in range(2, 8):
        ph.step("vector", lambda e=e: nc.vector.tensor_tensor(
            ssum[:], ssum[:], les[:, 32 * e:32 * (e + 1)], ALU.add),
            reads=["les", "ssum"], writes=["ssum"])
    ph.step("vector", lambda: nc.vector.reciprocal(tmpc[:], ssum[:]), reads=["ssum"], writes=["tmpc"])
    for e in range(8):
        ph.step("vector", lambda e=e: nc.vector.tensor_tensor(
            scm[:, 32 * e:32 * (e + 1)], les[:, 32 * e:32 * (e + 1)], tmpc[:], ALU.mult),
            reads=["les", "tmpc"], writes=["scm"])
    ph.step("vector", lambda: nc.vector.tensor_tensor(sbm[:], scm[:], st["ebbem"][:], ALU.add),
            reads=["scm", "ebbem"], writes=["sbm"])
    sb3 = sbm.rearrange("p (j u t) -> p j u t", j=4, u=2)
    for j in range(4):
        ph.step("vector", lambda j=j: nc.vector.tensor_tensor(
            gs[:, 32 * j:32 * (j + 1)], sb3[:, j, 0, :], sb3[:, j, 1, :], ALU.add),
            reads=["sbm"], writes=["gs"])
    for j in range(4):
        ph.step("gpsimd", lambda j=j: nc.gpsimd.memset(rk4[:, 32 * j:32 * (j + 1)], float(3 - j)),
                writes=["rk4"])
    for j1 in range(4):
        for j2 in range(j1 + 1, 4):
            ph.step("vector", lambda j1=j1, j2=j2: nc.vector.tensor_tensor(
                tmpc[:], gs[:, 32 * j1:32 * (j1 + 1)], gs[:, 32 * j2:32 * (j2 + 1)], ALU.is_ge),
                reads=["gs"], writes=["tmpc"])
            ph.step("vector", lambda j1=j1: nc.vector.tensor_tensor(
                rk4[:, 32 * j1:32 * (j1 + 1)], rk4[:, 32 * j1:32 * (j1 + 1)], tmpc[:], ALU.subtract),
                reads=["rk4", "tmpc"], writes=["rk4"])
            ph.step("vector", lambda j2=j2: nc.vector.tensor_tensor(
                rk4[:, 32 * j2:32 * (j2 + 1)], rk4[:, 32 * j2:32 * (j2 + 1)], tmpc[:], ALU.add),
                reads=["rk4", "tmpc"], writes=["rk4"])
    ph.step("vector", lambda: nc.vector.tensor_scalar(keep[:], rk4[:], 1.5, None, ALU.is_le),
            reads=["rk4"], writes=["keep"])
    ph.step("vector", lambda: nc.vector.tensor_scalar(big1[:], keep[:], 1e30, -1e30, ALU.mult, ALU.add),
            reads=["keep"], writes=["big1"])
    for e in range(8):
        j = e // 2
        ph.step("vector", lambda e=e, j=j: nc.vector.tensor_tensor(
            me[:, 32 * e:32 * (e + 1)], sbm[:, 32 * e:32 * (e + 1)], keep[:, 32 * j:32 * (j + 1)], ALU.mult),
            reads=["sbm", "keep"], writes=["me"])
        ph.step("vector", lambda e=e, j=j: nc.vector.tensor_tensor(
            me[:, 32 * e:32 * (e + 1)], me[:, 32 * e:32 * (e + 1)], big1[:, 32 * j:32 * (j + 1)], ALU.add),
            reads=["me", "big1"], writes=["me"])
    for e in range(8):
        ph.step("gpsimd", lambda e=e: nc.gpsimd.memset(rk8[:, 32 * e:32 * (e + 1)], float(7 - e)),
                writes=["rk8"])
    for a in range(8):
        for b2 in range(a + 1, 8):
            ph.step("vector", lambda a=a, b2=b2: nc.vector.tensor_tensor(
                tmpc[:], me[:, 32 * a:32 * (a + 1)], me[:, 32 * b2:32 * (b2 + 1)], ALU.is_ge),
                reads=["me"], writes=["tmpc"])
            ph.step("vector", lambda a=a: nc.vector.tensor_tensor(
                rk8[:, 32 * a:32 * (a + 1)], rk8[:, 32 * a:32 * (a + 1)], tmpc[:], ALU.subtract),
                reads=["rk8", "tmpc"], writes=["rk8"])
            ph.step("vector", lambda b2=b2: nc.vector.tensor_tensor(
                rk8[:, 32 * b2:32 * (b2 + 1)], rk8[:, 32 * b2:32 * (b2 + 1)], tmpc[:], ALU.add),
                reads=["rk8", "tmpc"], writes=["rk8"])
    ph.step("vector", lambda: nc.vector.tensor_scalar(st["oh0"][:], rk8[:], 0.0, None, ALU.is_equal),
            reads=["rk8"], writes=["oh0"])
    ph.step("vector", lambda: nc.vector.tensor_scalar(st["oh1"][:], rk8[:], 1.0, None, ALU.is_equal),
            reads=["rk8"], writes=["oh1"])
    for k, (ohn, wn) in enumerate([("oh0", "w0"), ("oh1", "w1s")]):
        ph.step("vector", lambda ohn=ohn: nc.vector.tensor_tensor(les[:], st[ohn][:], scm[:], ALU.mult),
                reads=[ohn, "scm"], writes=["les"])
        ph.step("vector", lambda wn=wn: nc.vector.tensor_tensor(
            st[wn][:], les[:, 0:32], les[:, 32:64], ALU.add),
            reads=["les"], writes=[wn])
        for e in range(2, 8):
            ph.step("vector", lambda wn=wn, e=e: nc.vector.tensor_tensor(
                st[wn][:], st[wn][:], les[:, 32 * e:32 * (e + 1)], ALU.add),
                reads=["les", wn], writes=[wn])
    e0i = st["e0i"]
    e1i = st["e1i"]
    for k, (ohn, dst, DST) in enumerate([("oh0", e0i, "e0i"), ("oh1", e1i, "e1i")]):
        ph.step("gpsimd", lambda dst=dst: nc.gpsimd.memset(dst[:], 0.0), writes=[DST])
        for e in range(1, 8):
            ph.step("vector", lambda ohn=ohn, e=e: nc.vector.tensor_scalar(
                tmpc[:], st[ohn][:, 32 * e:32 * (e + 1)], float(e), None, ALU.mult),
                reads=[ohn], writes=["tmpc"])
            ph.step("vector", lambda dst=dst, DST=DST: nc.vector.tensor_tensor(
                dst[:], dst[:], tmpc[:], ALU.add),
                reads=[DST, "tmpc"], writes=[DST])
    ph.step("vector", lambda: nc.vector.tensor_tensor(les[:], st["oh0"][:], st["oh1"][:], ALU.add),
            reads=["oh0", "oh1"], writes=["les"])
    ph.step("vector", lambda: nc.vector.tensor_reduce(
        cntpp[:], les.rearrange("p (e t) -> p e t", e=8), AX.X, ALU.add),
        reads=["les"], writes=["cntpp"])
    ph.step("tensor", lambda: nc.tensor.matmul(pcnt[:], st["onescf"][:], cntpp[:], start=True, stop=True),
            reads=["onescf", "cntpp"], writes=["pcnt"])
    ph.step("gpsimd", lambda: nc.gpsimd.memset(crow[:], 0.0), writes=["crow"])
    ph.step("vector", lambda: nc.vector.tensor_copy(crow[0:1, 0:8], pcnt[:]),
            reads=["pcnt"], writes=["crow"])
    ph.dma("gpsimd", lambda: nc.gpsimd.dma_start(out=cc1_in[0:8, :], in_=X8p[:]),
           reads=["X8p"], writes=["cc1h"])
    ph.dma("gpsimd", lambda: nc.gpsimd.dma_start(out=cc1_in[8:9, :], in_=crow[:]),
           reads=["crow"], writes=["cc1h"])
    st["cc1_sem"] = ph.collective("cc1", lambda: nc.gpsimd.collective_compute(
        "AllReduce", ALU.add, replica_groups=[list(range(N_CORES))],
        ins=[cc1_in.opt()], outs=[cc1_out.opt()]),
        reads=["cc1h"], writes=["cc1o"])



def emit_C1b(ph, reg):
    nc, prog = ph.nc, ph.prog
    st = prog.st
    x1 = reg.get("x1", *STREAMS["x1"])
    x2 = reg.get("x2", *STREAMS["x2"])
    cc1_out = reg.get("cc1_out", (9, 1024), F32)
    w1d = reg.get("L0_ffn_w1", (8 * D, 512), F32)
    w3d = reg.get("L0_ffn_w3", (8 * D, 512), F32)

    st["Uflat"] = prog.sb("Uflat", [64, 512], BF16)
    st["T2"] = prog.sb("T2", [8, 4096], BF16)
    st["X8T"] = prog.sb("X8T", [128, 64], BF16)

    hh = ph.sb("hh", [128, 8 * 512], BF16)
    hs = ph.sb("hs", [128, 512], F32)
    x1t = [ph.sb(f"x1t{i}", [128, D], F32) for i in range(2)]
    x2t = [ph.sb(f"x2t{i}", [128, D], F32) for i in range(2)]
    ph1 = [ph.ps(f"ph1_{i}", [128, 512], F32) for i in range(2)]
    ph3 = [ph.ps(f"ph3_{i}", [128, 512], F32) for i in range(2)]
    pso = ph.ps("pso", [128, D], F32)
    X8sb = ph.sb("X8sb", [8, D], BF16)
    wab = [ph.sb(f"wab{i}", [128, 2 * 8 * 512], BF16) for i in range(2)]
    uttmp = ph.sb("uttmp", [8, 512], BF16)
    put = ph.ps("put", [8, 512], F32)
    ptx = ph.ps("ptx", [128, 8], BF16)

    def emit_block(bi):
        for sc in range(8):
            p1, p3 = ph1[sc % 2], ph3[sc % 2]
            P1, P3 = ("ph1", sc % 2), ("ph3", sc % 2)
            for dc in range(8):
                ph.step("tensor", lambda bi=bi, sc=sc, dc=dc, p1=p1: nc.tensor.matmul(
                    p1[:], st["sw1T"][:, D * dc + 128 * sc:D * dc + 128 * (sc + 1)],
                    st["y2T"][:, T * dc + 512 * bi:T * dc + 512 * (bi + 1)],
                    start=(dc == 0), stop=(dc == 7)),
                    reads=["sw1T"], writes=[P1])
            for dc in range(8):
                ph.step("tensor", lambda bi=bi, sc=sc, dc=dc, p3=p3: nc.tensor.matmul(
                    p3[:], st["sw3T"][:, D * dc + 128 * sc:D * dc + 128 * (sc + 1)],
                    st["y2T"][:, T * dc + 512 * bi:T * dc + 512 * (bi + 1)],
                    start=(dc == 0), stop=(dc == 7)),
                    reads=["sw3T"], writes=[P3])
            ph.step("scalar", lambda p1=p1: nc.scalar.activation(hs[:], p1[:], AF.Silu),
                    reads=[P1], writes=["hs"])
            ph.step("vector", lambda sc=sc, p3=p3: nc.vector.tensor_tensor(
                hh[:, 512 * sc:512 * (sc + 1)], hs[:], p3[:], ALU.mult),
                reads=["hs", P3], writes=[("hh", sc)])
        for tc in range(4):
            b = tc % 2
            tglob = 4 * bi + tc
            ph.dma("sync", lambda tglob=tglob, b=b: nc.sync.dma_start(
                out=x1t[b][:], in_=x1[128 * tglob:128 * (tglob + 1), :]), writes=[("x1t", b)])
            for sc in range(8):
                for oh in range(2):
                    ph.step("tensor", lambda sc=sc, tc=tc, oh=oh: nc.tensor.matmul(
                        pso[:, 512 * oh:512 * (oh + 1)],
                        hh[:, 512 * sc + 128 * tc:512 * sc + 128 * (tc + 1)],
                        st["sw2T"][:, D * sc + 512 * oh:D * sc + 512 * (oh + 1)],
                        start=(sc == 0), stop=(sc == 7)),
                        reads=[("hh", sc), "sw2T"], writes=[("pso", oh)])
            ph.step("vector", lambda b=b: nc.vector.tensor_tensor(
                x2t[b][:], pso[:], x1t[b][:], ALU.add),
                reads=[("pso", 0), ("pso", 1), ("x1t", b)], writes=[("x2t", b)])
            ph.dma("gpsimd", lambda tglob=tglob, b=b: nc.gpsimd.dma_start(
                out=x2[128 * tglob:128 * (tglob + 1), :], in_=x2t[b][:]),
                reads=[("x2t", b)], writes=[("x2h", tglob)])

    for bi in range(4):
        emit_block(bi)

    ccw = [(st["cc1_sem"], 1)]
    ph.dma("gpsimd", lambda: nc.gpsimd.dma_start(out=X8sb[:], in_=cc1_out[0:8, :]),
           writes=["X8sb"], extra_waits=ccw)
    for dc in range(8):
        ph.step("tensor", lambda dc=dc: nc.tensor.transpose(
            ptx[:], X8sb[:, 128 * dc:128 * (dc + 1)], st["idbC"][0:8, 0:8]),
            reads=["X8sb"], writes=["ptx"])
        ph.step("vector", lambda dc=dc: nc.vector.tensor_copy(st["X8T"][:, 8 * dc:8 * (dc + 1)], ptx[:]),
                reads=["ptx"], writes=["X8T"])
    for g in range(8):
        b = g % 2
        ph.dma("gpsimd", lambda g=g, b=b: nc.gpsimd.dma_start(
            out=wab[b][:, 0:4096].rearrange("p (c h) -> p c h", c=8),
            in_=w1d[1024 * g:1024 * (g + 1), :].rearrange(
                "(c p) h -> p c h", p=128)), writes=[("wab", b)])
        ph.dma("gpsimd", lambda g=g, b=b: nc.gpsimd.dma_start(
            out=wab[b][:, 4096:8192].rearrange("p (c h) -> p c h", c=8),
            in_=w3d[1024 * g:1024 * (g + 1), :].rearrange(
                "(c p) h -> p c h", p=128)), writes=[("wab", b)])
        for half, dst in ((0, "Uflat"), (1, "T2")):
            for dc in range(8):
                ph.step("tensor", lambda g=g, b=b, half=half, dc=dc: nc.tensor.matmul(
                    put[:], st["X8T"][:, 8 * dc:8 * (dc + 1)],
                    wab[b][:, 4096 * half + 512 * dc:4096 * half + 512 * (dc + 1)],
                    start=(dc == 0), stop=(dc == 7)),
                    reads=[("wab", b), "X8T"], writes=["put"])
            ph.step("vector", lambda: nc.vector.tensor_copy(uttmp[:], put[:]),
                    reads=["put"], writes=["uttmp"])
            if dst == "Uflat":
                ph.dma("gpsimd", lambda g=g: nc.gpsimd.dma_start(
                    out=st["Uflat"][8 * g:8 * (g + 1), :], in_=uttmp[:]),
                    reads=["uttmp"], writes=["Uflat"])
            else:
                ph.step("vector", lambda g=g: nc.vector.tensor_copy(
                    st["T2"][:, 512 * g:512 * (g + 1)], uttmp[:]),
                    reads=["uttmp"], writes=["T2"])

    for bi in range(4, 8):
        emit_block(bi)



def emit_C3(ph, reg):
    nc, prog = ph.nc, ph.prog
    st = prog.st
    x2 = reg.get("x2", *STREAMS["x2"])
    cc1_out = reg.get("cc1_out", (9, 1024), F32)
    cc2_in = reg.get("cc2_in", (64, 512), F32)
    cc2_out = reg.get("cc2_out", (64, 512), F32)
    w2d = reg.get("L0_ffn_w2", (8 * 512, D), F32)

    cnt8 = ph.sb("cnt8", [8, 1], F32)
    cum = ph.sb("cum", [1, 8], F32)
    thrb = ph.sb("thrb", [128, 8], F32)
    iota64 = ph.sb("iota64", [128, 64], F32)
    pos = ph.sb("pos", [128, 32], F32)
    geb = ph.sb("geb", [128, 32], F32)
    gidx = ph.sb("gidx", [128, 32], F32)
    ccol = [ph.sb(f"ccol{k}", [128, 32], F32) for k in range(2)]
    M1 = [ph.sb(f"M1_{i}", [128, 64], BF16) for i in range(3)]
    M1T = [ph.sb(f"M1T_{i}", [64, 128], BF16) for i in range(3)]
    ss = [ph.sb(f"ss{i}", [128, 512], F32) for i in range(3)]
    Ssb = [ph.sb(f"Ssb{i}", [128, 512], BF16) for i in range(3)]
    Psb = ph.sb("Psb", [64, 512], F32)
    P2 = ph.sb("P2", [8, 4096], F32)
    A2 = ph.sb("A2", [8, 4096], BF16)
    A2T = ph.sb("A2T", [128, 256], BF16)
    w2g = [ph.sb(f"w2g{i}", [128, 4 * D], BF16) for i in range(2)]
    corr = ph.sb("corr", [8, D], F32)
    px8 = ph.sb("px8", [8, D], F32)
    pmisc = ph.ps("pmisc", [128, 8], F32)
    pG = [ph.ps(f"pG{i}", [128, 512], F32) for i in range(2)]
    pM = [ph.ps(f"pM{i}", [64, 128], BF16) for i in range(2)]
    pP = ph.ps("pP", [64, 512], F32)
    pcor = ph.ps("pcor", [8, D], F32)

    ccw = [(st["cc1_sem"], 1)]
    ph.dma("sync", lambda: nc.sync.dma_start(
        out=cnt8[:], in_=cc1_out[8:9, 0:8].rearrange("a b -> b a")),
        writes=["cnt8"], extra_waits=ccw)
    ph.step("tensor", lambda: nc.tensor.matmul(pmisc[0:1, 0:8], cnt8[:], st["T8sb"][:],
                                               start=True, stop=True),
            reads=["cnt8", "T8sb"], writes=["pmisc"])
    ph.step("vector", lambda: nc.vector.tensor_scalar(
        cum[:], pmisc[0:1, 0:8], st["posb"][0:1, 0:1], None, ALU.subtract),
        reads=["pmisc", "posb"], writes=["cum"])
    ph.step("tensor", lambda: nc.tensor.matmul(pmisc[:], st["onesf"][:], cum[:], start=True, stop=True),
            reads=["cum", "onesf"], writes=["pmisc"])
    ph.step("vector", lambda: nc.vector.tensor_copy(thrb[:], pmisc[:]), reads=["pmisc"], writes=["thrb"])
    ph.step("gpsimd", lambda: nc.gpsimd.iota(iota64[:], [[1, 64]], base=0, channel_multiplier=0,
                                             allow_small_or_imprecise_dtypes=True),
            writes=["iota64"])
    for k in range(2):
        ph.step("gpsimd", lambda k=k: nc.gpsimd.iota(
            pos[:], [[256, 32]], base=k, channel_multiplier=2,
            allow_small_or_imprecise_dtypes=True), writes=["pos"])
        ph.step("vector", lambda: nc.vector.tensor_scalar(
            gidx[:], pos[:], thrb[:, 0:1], None, ALU.is_ge),
            reads=["pos", "thrb"], writes=["gidx"])
        for e in range(1, 8):
            ph.step("vector", lambda e=e: nc.vector.tensor_scalar(
                geb[:], pos[:], thrb[:, e:e + 1], None, ALU.is_ge),
                reads=["pos", "thrb"], writes=["geb"])
            ph.step("vector", lambda: nc.vector.tensor_tensor(gidx[:], gidx[:], geb[:], ALU.add),
                    reads=["gidx", "geb"], writes=["gidx"])
        ph.step("vector", lambda k=k: nc.vector.tensor_scalar(
            ccol[k][:], gidx[:], 8.0, None, ALU.mult),
            reads=["gidx"], writes=[("ccol", k)])
        ei = st["e0i"] if k == 0 else st["e1i"]
        EI = "e0i" if k == 0 else "e1i"
        ph.step("vector", lambda k=k, ei=ei: nc.vector.tensor_tensor(
            ccol[k][:], ccol[k][:], ei[:], ALU.add),
            reads=[("ccol", k), EI], writes=[("ccol", k)])

    for g in range(2):
        ph.dma("gpsimd", lambda g=g: nc.gpsimd.dma_start(
            out=w2g[g][:, :].rearrange("p (c o) -> p c o", c=4),
            in_=w2d[512 * g:512 * (g + 1), :].rearrange("(c p) o -> p c o", p=128)),
            writes=[("w2g", g)])

    first = True
    it = 0
    for tb in range(32):
        for k in range(2):
            sl = it % 3
            pl = it % 2
            it += 1
            wsrc = st["w0"] if k == 0 else st["w1s"]
            WK = "w0" if k == 0 else "w1s"
            wcol_ap = wsrc[:, tb:tb + 1]
            ph.step("vector", lambda k=k, tb=tb, sl=sl: nc.vector.tensor_scalar(
                M1[sl][:], iota64[:], ccol[k][:, tb:tb + 1], None, ALU.is_equal),
                reads=["iota64", ("ccol", k)], writes=[("M1", sl)])
            ph.step("tensor", lambda sl=sl, pl=pl: nc.tensor.transpose(pM[pl][:], M1[sl][:], st["idbC"][:]),
                    reads=[("M1", sl), "idb"], writes=[("pM", pl)])
            ph.step("vector", lambda sl=sl, pl=pl: nc.vector.tensor_copy(M1T[sl][:], pM[pl][:]),
                    reads=[("pM", pl)], writes=[("M1T", sl)])
            ph.step("tensor", lambda sl=sl, pl=pl: nc.tensor.matmul(
                pG[pl][:], M1T[sl][:], st["Uflat"][:], start=True, stop=True),
                reads=[("M1T", sl), "Uflat"], writes=[("pG", pl)])
            ph.step("scalar", lambda sl=sl, pl=pl, wcol_ap=wcol_ap, WK=WK: nc.scalar.activation(
                ss[sl][:], pG[pl][:], AF.Silu, scale=wcol_ap),
                reads=[("pG", pl), WK], writes=[("ss", sl)])
            ph.step("vector", lambda sl=sl, wcol_ap=wcol_ap, WK=WK: nc.vector.tensor_scalar(
                Ssb[sl][:], ss[sl][:], wcol_ap, None, ALU.mult),
                reads=[("ss", sl), WK], writes=[("Ssb", sl)])
            ph.step("tensor", lambda sl=sl, first=first, it=it: nc.tensor.matmul(
                pP[:], M1[sl][:], Ssb[sl][:], start=first, stop=(it == 64),
                skip_group_check=True),
                reads=[("M1", sl), ("Ssb", sl)], writes=["pP"])
            first = False
    ph.step("vector", lambda: nc.vector.tensor_copy(Psb[:], pP[:]), reads=["pP"], writes=["Psb"])
    ph.dma("gpsimd", lambda: nc.gpsimd.dma_start(out=cc2_in, in_=Psb[:]),
           reads=["Psb"], writes=["cc2h"])
    cc2_sem = ph.collective("cc2", lambda: nc.gpsimd.collective_compute(
        "AllReduce", ALU.add, replica_groups=[list(range(N_CORES))],
        ins=[cc2_in.opt()], outs=[cc2_out.opt()]),
        reads=["cc2h"], writes=["cc2o"])

    ph.dma("gpsimd", lambda: nc.gpsimd.dma_start(
        out=P2.rearrange("e (g h) -> e g h", g=8),
        in_=cc2_out.rearrange("(g e) h -> e g h", e=8)),
        writes=["P2"], extra_waits=[(cc2_sem, 1)])
    ph.step("vector", lambda: nc.vector.tensor_tensor(A2[:], P2[:], st["T2"][:], ALU.mult),
            reads=["P2", "T2"], writes=["A2"])
    patv = pmisc[:].bitcast(BF16)[:, 0:8]
    for c in range(32):
        ph.step("tensor", lambda c=c: nc.tensor.transpose(
            patv, A2[:, 128 * c:128 * (c + 1)], st["idbC"][0:8, 0:8]),
            reads=["A2"], writes=["pmisc"])
        ph.step("vector", lambda c=c: nc.vector.tensor_copy(A2T[:, 8 * c:8 * (c + 1)], patv),
                reads=["pmisc"], writes=["A2T"])
    for g in range(8):
        for hc in range(4):
            c = 4 * g + hc
            for oh2 in range(2):
                ph.step("tensor", lambda g=g, c=c, hc=hc, oh2=oh2: nc.tensor.matmul(
                    pcor[:, 512 * oh2:512 * (oh2 + 1)], A2T[:, 8 * c:8 * (c + 1)],
                    w2g[g % 2][:, D * hc + 512 * oh2:D * hc + 512 * (oh2 + 1)],
                    start=(g == 0 and hc == 0), stop=(g == 7 and hc == 3),
                    skip_group_check=True),
                    reads=["A2T", ("w2g", g % 2)], writes=[("pcor", oh2)])
        if g % 2 == 1 and g < 7:
            for gg in (g + 1, g + 2):
                if gg < 8:
                    ph.dma("gpsimd", lambda gg=gg: nc.gpsimd.dma_start(
                        out=w2g[gg % 2][:, :].rearrange("p (c o) -> p c o", c=4),
                        in_=w2d[512 * gg:512 * (gg + 1), :].rearrange(
                            "(c p) o -> p c o", p=128)),
                        writes=[("w2g", gg % 2)])
    ph.step("vector", lambda: nc.vector.tensor_scalar(corr[:], pcor[0:8, :], st["maskc"][:],
                                                      None, ALU.mult),
            reads=[("pcor", 0), ("pcor", 1), "maskc"], writes=["corr"])
    ph.dma("sync", lambda: nc.sync.dma_start(out=px8[:], in_=x2[0:8, :]), writes=["px8"])
    ph.step("vector", lambda: nc.vector.tensor_tensor(px8[:], px8[:], corr[:], ALU.add),
            reads=["px8", "corr"], writes=["px8"])
    ph.dma("sync", lambda: nc.sync.dma_start(out=x2[0:8, :], in_=px8[:]),
           reads=["px8"], writes=["x2patch"])



HP = 8
QB = 2048


def emit_D0(ph, reg):
    nc, prog = ph.nc, ph.prog
    st = prog.st
    identb = reg.get("ident_bf", (128, 128), BF16)
    grmsrep = reg.get("grms_rep", (1, D), F32)

    st["idbD"] = prog.sb("idbD", [128, 128], BF16)
    st["grb"] = prog.sb("grb", [128, D], F32)
    st["Mb"] = prog.sb("Mb", [128, 1], F32)
    st["onesfD"] = prog.sb("onesfD", [1, 128], F32)
    st["v_all"] = prog.sb("v_all", [128, NT * 1040], BF16)

    ph.dma("sync", lambda: nc.sync.dma_start(out=st["idbD"][:], in_=identb), writes=["idb"])
    ph.step("gpsimd", lambda: nc.gpsimd.memset(st["onesfD"][:], 1.0), writes=["onesf"])
    ph.step("vector", lambda: nc.vector.memset(
        st["v_all"].rearrange("p (t h e) -> p t h e", t=NT, h=16)[:, :, :, 64:65], 1.0),
        writes=["v_ones"])
    grow = ph.sb("grow", [1, D], F32)
    pb = ph.ps("pb", [128, 512], F32)
    ph.dma("sync", lambda: nc.sync.dma_start(out=grow[:], in_=grmsrep), writes=["grow"])
    for h in range(2):
        ph.step("tensor", lambda h=h: nc.tensor.matmul(
            pb[:], st["onesfD"][:], grow[0:1, 512 * h:512 * (h + 1)], start=True, stop=True),
            reads=["grow", "onesf"], writes=["pb"])
        ph.step("vector", lambda h=h: nc.vector.tensor_copy(st["grb"][:, 512 * h:512 * (h + 1)], pb[:]),
                reads=["pb"], writes=["grb"])
    g2 = ph.sb("g2", [1, 64], F32)
    mg = ph.sb("mg", [1, 1], F32)
    pmb = ph.ps("pmb", [128, 1], F32)
    ph.step("scalar", lambda: nc.scalar.activation(g2[:], grow[0:1, 0:64], AF.Square),
            reads=["grow"], writes=["g2"])
    ph.step("vector", lambda: nc.vector.reduce_max(mg[:], g2[:], AX.X),
            reads=["g2"], writes=["mg"])
    ph.step("vector", lambda: nc.vector.tensor_scalar(mg[:], mg[:], -8.16, None, ALU.mult),
            reads=["mg"], writes=["mg"])
    ph.step("tensor", lambda: nc.tensor.matmul(pmb[:], st["onesfD"][:], mg[:], start=True, stop=True),
            reads=["mg", "onesf"], writes=["pmb"])
    ph.step("vector", lambda: nc.vector.tensor_copy(st["Mb"][:], pmb[:]), reads=["pmb"], writes=["Mb"])


def emit_D1(ph, reg):
    nc, prog = ph.nc, ph.prog
    st = prog.st
    x2 = reg.get("x2", *STREAMS["x2"])
    cosr_h = reg.get("cosrep", (T, 512), F32)
    sinr_h = reg.get("sinrep", (T, 512), F32)
    qfmT = reg.get("q_tm_hbm", (T, D), BF16)
    kfmT = reg.get("k_tm_hbm", (T, D), BF16)
    wq_h = reg.get("L1_attn_wq", (D, D), F32)
    wk_h = reg.get("L1_attn_wk", (D, D), F32)
    wv_h = reg.get("L1_attn_wv", (D, D), F32)
    wts = [("wqT", wq_h), ("wkT", wk_h), ("wvT", wv_h)]

    wT = {}
    wrow = [ph.sb(f"wrow{i}", [128, D], BF16) for i in range(2)]
    ptd = ph.ps("ptd", [128, 128], BF16)
    for nm, src_h in wts:
        wT[nm] = ph.sb(nm, [128, 8 * D], BF16)
        for cc in range(8):
            b = cc % 2
            ph.dma("gpsimd", lambda src_h=src_h, cc=cc, b=b: nc.gpsimd.dma_start(
                out=wrow[b][:], in_=src_h[128 * cc:128 * (cc + 1), :]), writes=[("wrow", b)])
            for dc in range(8):
                ph.step("tensor", lambda b=b, dc=dc: nc.tensor.transpose(
                    ptd[:], wrow[b][:, 128 * dc:128 * (dc + 1)], st["idbD"][:]),
                    reads=[("wrow", b), "idb"], writes=["ptd"])
                ph.step("vector", lambda nm=nm, cc=cc, dc=dc: nc.vector.tensor_copy(
                    wT[nm][:, D * dc + 128 * cc:D * dc + 128 * (cc + 1)], ptd[:]),
                    reads=["ptd"], writes=[nm])

    ln = LNHelper(ph, "lnD", 2)
    xt = [ph.sb(f"xt{i}", [128, D], F32) for i in range(2)]
    y3b = [ph.sb(f"y3b{i}", [128, D], BF16) for i in range(2)]
    y3T_b = [ph.sb(f"y3T{i}", [128, 8 * 128], BF16) for i in range(2)]
    qn_b = [ph.sb(f"qn{i}", [128, D], F32) for i in range(2)]
    rt_b = [[ph.sb(f"rt{i}_{j}", [128, 512], F32) for i in range(2)] for j in range(2)]
    qr_b = [ph.sb(f"qr{i}", [128, D], BF16) for i in range(2)]
    cosb_b = [ph.sb(f"cosb{i}", [128, 512], F32) for i in range(2)]
    sinb_b = [ph.sb(f"sinb{i}", [128, 512], F32) for i in range(2)]
    ssq_b = [ph.sb(f"ssq{i}", [128, 32], F32) for i in range(2)]
    rstd_b = [ph.sb(f"rstd{i}", [128, 32], F32) for i in range(2)]

    pmm_m = {nm: [ph.ps(f"pmm_{nm}_{i}", [128, 512], F32) for i in range(2)]
             for nm in ("wqT", "wkT", "wvT")}
    ptq = ph.ps("ptq", [128, 128], BF16)

    qraw_b = {nm: [ph.sb(f"qraw_{nm}{i}", [128, D], F32) for i in range(2)]
              for nm in ("wqT", "wkT")}

    def stageA1(t):
        b = t % 2
        y3T = y3T_b[b]
        cosb = cosb_b[b]
        sinb = sinb_b[b]
        Y3T = lambda dc, b=b: ("y3T", b, dc)
        COSB, SINB = ("cosb", b), ("sinb", b)
        XT, Y3, Y3B = ("xt", b), ("y3", b), ("y3b", b)
        ph.dma("sync", lambda t=t, b=b: nc.sync.dma_start(
            out=xt[b][:], in_=x2[128 * t:128 * (t + 1), :]), writes=[XT])
        ln.emit(b, xt[b], XT, y3b[b], Y3B)
        for dc in range(8):
            pt = ptq if dc % 2 == 0 else ptd
            PTK = "ptq" if dc % 2 == 0 else "ptd"
            ph.step("tensor", lambda b=b, dc=dc, pt=pt: nc.tensor.transpose(
                pt[:], y3b[b][:, 128 * dc:128 * (dc + 1)], st["idbD"][:]),
                reads=[Y3B, "idb"], writes=[PTK])
            ph.step("scalar", lambda dc=dc, y3T=y3T, pt=pt: nc.scalar.copy(
                y3T[:, 128 * dc:128 * (dc + 1)], pt[:]),
                reads=[PTK], writes=[Y3T(dc)])
        ph.dma("sync", lambda t=t, cosb=cosb: nc.sync.dma_start(
            out=cosb[:], in_=cosr_h[128 * t:128 * (t + 1), :]), writes=[COSB])
        ph.dma("sync", lambda t=t, sinb=sinb: nc.sync.dma_start(
            out=sinb[:], in_=sinr_h[128 * t:128 * (t + 1), :]), writes=[SINB])
        for nm in ("wqT", "wkT"):
            pmm = pmm_m[nm]
            PMM = lambda half, nm=nm: ("pmm", nm, half)
            for half in range(2):
                for dc in range(8):
                    ph.step("tensor", lambda nm=nm, half=half, dc=dc, pmm=pmm, y3T=y3T: nc.tensor.matmul(
                        pmm[half][:], y3T[:, 128 * dc:128 * (dc + 1)],
                        wT[nm][:, D * dc + 512 * half:D * dc + 512 * (half + 1)],
                        start=(dc == 0), stop=(dc == 7)),
                        reads=[Y3T(dc), nm], writes=[PMM(half)])

    def stageA2(t):
        b = t % 2
        y3T = y3T_b[b]
        Y3T = lambda dc, b=b: ("y3T", b, dc)
        pmm = pmm_m["wvT"]
        PMMV = lambda half: ("pmm", "wvT", half)
        for half in range(2):
            for dc in range(8):
                ph.step("tensor", lambda half=half, dc=dc, pmm=pmm, y3T=y3T: nc.tensor.matmul(
                    pmm[half][:], y3T[:, 128 * dc:128 * (dc + 1)],
                    wT["wvT"][:, D * dc + 512 * half:D * dc + 512 * (half + 1)],
                    start=(dc == 0), stop=(dc == 7)),
                    reads=[Y3T(dc), "wvT"], writes=[PMMV(half)])
        for nm in ("wqT", "wkT"):
            qraw = qraw_b[nm][b]
            pmq = pmm_m[nm]
            for half in range(2):
                ph.step("scalar", lambda half=half, pmq=pmq, qraw=qraw: nc.scalar.copy(
                    qraw[:, 512 * half:512 * (half + 1)], pmq[half][:]),
                    reads=[("pmm", nm, half)], writes=[("qraw", nm, b)])

    def stageBhead(t):
        b = t % 2
        rt = rt_b[b]
        ssq = ssq_b[b]
        rstd = rstd_b[b]
        RT = lambda i, b=b: ("rt", b, i)
        SSQ, RSTD = ("ssq", b), ("rstd", b)
        pmm = pmm_m["wvT"]
        for half in range(2):
            ph.step("scalar", lambda t=t, half=half, pmm=pmm: nc.scalar.copy(
                st["v_all"].rearrange("p (tt h e) -> p tt h e", tt=NT, h=16)
                [:, t, 8 * half:8 * (half + 1), 0:64],
                pmm[half].rearrange("p (h e) -> p h e", h=8)),
                reads=[("pmm", "wvT", half)], writes=[("v", t)])
        for nm in ("wqT", "wkT"):
            qraw = qraw_b[nm][b]
            QRAW = ("qraw", nm, b)
            sb2 = 0 if nm == "wqT" else 16
            for half in range(2):
                ph.step("scalar", lambda half=half, rt=rt, qraw=qraw: nc.scalar.activation(
                    rt[half][:], qraw[:, 512 * half:512 * (half + 1)], AF.Square),
                    reads=[QRAW], writes=[RT(half)])
                ph.step("vector", lambda half=half, ssq=ssq, rt=rt, sb2=sb2: nc.vector.tensor_reduce(
                    ssq[:, sb2 + 8 * half:sb2 + 8 * (half + 1)],
                    rt[half].rearrange("p (h e) -> p h e", h=8), AX.X, ALU.add),
                    reads=[RT(half)], writes=[SSQ])
        ph.step("vector", lambda rstd=rstd, ssq=ssq: nc.vector.tensor_scalar(
            rstd[:], ssq[:], 1.0 / 64.0, 1e-5, ALU.mult, ALU.add),
            reads=[SSQ], writes=[RSTD])
        ph.step("scalar", lambda rstd=rstd: nc.scalar.activation(rstd[:], rstd[:], AF.Sqrt),
                reads=[RSTD], writes=[RSTD])
        ph.step("vector", lambda rstd=rstd: nc.vector.reciprocal(rstd[:], rstd[:]),
                reads=[RSTD], writes=[RSTD])

    def stageBrest(t):
        b = t % 2
        qn = qn_b[b]
        rt = rt_b[b]
        qr = qr_b[b]
        cosb = cosb_b[b]
        sinb = sinb_b[b]
        rstd = rstd_b[b]
        QN, QR = ("qn", b), ("qr", b)
        RT = lambda i, b=b: ("rt", b, i)
        COSB, SINB, RSTD = ("cosb", b), ("sinb", b), ("rstd", b)
        for nm, dstT in (("wqT", qfmT), ("wkT", kfmT)):
            qraw = qraw_b[nm][b]
            QRAW = ("qraw", nm, b)
            rb = 0 if nm == "wqT" else 16
            for h in range(16):
                ph.step("vector", lambda h=h, qn=qn, qraw=qraw, rstd=rstd, rb=rb: nc.vector.tensor_scalar(
                    qn[:, 64 * h:64 * (h + 1)], qraw[:, 64 * h:64 * (h + 1)],
                    rstd[:, rb + h:rb + h + 1], None, ALU.mult),
                    reads=[QRAW, RSTD], writes=[QN])
            ph.step("vector", lambda qn=qn: nc.vector.tensor_tensor(qn[:], qn[:], st["grb"][:], ALU.mult),
                    reads=[QN, "grb"], writes=[QN])
            qn4 = qn.rearrange("p (h u f) -> p h u f", h=16, u=2)
            qr4 = qr.rearrange("p (h u f) -> p h u f", h=16, u=2)
            c3 = cosb.rearrange("p (h f) -> p h f", h=16)
            s3 = sinb.rearrange("p (h f) -> p h f", h=16)
            ph.step("vector", lambda qn4=qn4, c3=c3, rt=rt: nc.vector.tensor_tensor(
                rt[0].rearrange("p (h f) -> p h f", h=16), qn4[:, :, 0, :], c3, ALU.mult),
                reads=[QN, COSB], writes=[RT(0)])
            ph.step("vector", lambda qn4=qn4, s3=s3, rt=rt: nc.vector.tensor_tensor(
                rt[1].rearrange("p (h f) -> p h f", h=16), qn4[:, :, 1, :], s3, ALU.mult),
                reads=[QN, SINB], writes=[RT(1)])
            ph.step("vector", lambda qr4=qr4, rt=rt: nc.vector.tensor_tensor(
                qr4[:, :, 0, :], rt[0].rearrange("p (h f) -> p h f", h=16),
                rt[1].rearrange("p (h f) -> p h f", h=16), ALU.add),
                reads=[RT(0), RT(1)], writes=[QR])
            ph.step("vector", lambda qn4=qn4, c3=c3, rt=rt: nc.vector.tensor_tensor(
                rt[0].rearrange("p (h f) -> p h f", h=16), qn4[:, :, 1, :], c3, ALU.mult),
                reads=[QN, COSB], writes=[RT(0)])
            ph.step("vector", lambda qn4=qn4, s3=s3, rt=rt: nc.vector.tensor_tensor(
                rt[1].rearrange("p (h f) -> p h f", h=16), qn4[:, :, 0, :], s3, ALU.mult),
                reads=[QN, SINB], writes=[RT(1)])
            ph.step("vector", lambda qr4=qr4, rt=rt: nc.vector.tensor_tensor(
                qr4[:, :, 1, :], rt[0].rearrange("p (h f) -> p h f", h=16),
                rt[1].rearrange("p (h f) -> p h f", h=16), ALU.subtract),
                reads=[RT(0), RT(1)], writes=[QR])
            ph.dma("gpsimd", lambda t=t, qr=qr, dstT=dstT: nc.gpsimd.dma_start(
                out=dstT[128 * t:128 * (t + 1), :], in_=qr[:]),
                reads=[QR], writes=[("tm", nm, t)])

    stageA1(0)
    stageA2(0)
    for t in range(NT):
        if t + 1 < NT:
            stageA1(t + 1)
        stageBhead(t)
        if t + 1 < NT:
            stageA2(t + 1)
        stageBrest(t)



def emit_D2(ph, reg):
    nc, prog = ph.nc, ph.prog
    st = prog.st
    qtm = reg.get("q_tm_hbm", (T, D), BF16)
    ktm = reg.get("k_tm_hbm", (T, D), BF16)
    afm = reg.get("afm_hbm", (NT * D, 128), BF16)

    qhead = [ph.sb(f"qhead{i}", [128, T], BF16) for i in range(2)]
    qpairT = [ph.sb(f"qpairT{i}", [128, T], BF16) for i in range(2)]
    kpair = [ph.sb(f"kpair{i}", [128, T], BF16) for i in range(2)]
    for b in range(2):
        zb = 64 * (1 - b)
        ph.step("vector", lambda b=b, zb=zb: nc.vector.memset(
            qhead[b][zb:zb + 64, :], 0.0), writes=[("qz", b)])
    PT = [ph.sb(f"PT{i}", [128, 1024], BF16) for i in range(4)]
    rec = ph.sb("rec", [1, 512], F32)
    recb = ph.sb("recb", [64, 512], F32)
    stage = ph.sb("stage", [128, T], BF16)
    pS = [ph.ps(f"pS{i}", [128, 1024], F32) for i in range(3)]
    pPV = [ph.ps(f"pPV{i}", [65, 512], F32) for i in range(2)]

    NQB = T // 1024
    for hp in range(HP):
        bufp = hp % 2
        for t in range(NT):
            ph.dma("sync", lambda hp=hp, bufp=bufp, t=t: nc.sync.dma_start(
                out=kpair[bufp][:, 128 * t:128 * (t + 1)],
                in_=ktm[128 * t:128 * (t + 1), 128 * hp:128 * (hp + 1)],
                transpose=True),
                writes=[("kpair", bufp, t)])
        for t in range(NT):
            ph.dma("sync", lambda hp=hp, bufp=bufp, t=t: nc.sync.dma_start(
                out=qpairT[bufp][:, 128 * t:128 * (t + 1)],
                in_=qtm[128 * t:128 * (t + 1), 128 * hp:128 * (hp + 1)],
                transpose=True),
                writes=[("qpT", bufp, t)])
        for sub in range(2):
            rb = 64 * sub
            for t in range(NT):
                ph.step("vector", lambda bufp=bufp, sub=sub, rb=rb, t=t: nc.vector.tensor_copy(
                    qhead[sub][rb:rb + 64, 128 * t:128 * (t + 1)],
                    qpairT[bufp][rb:rb + 64, 128 * t:128 * (t + 1)]),
                    reads=[("qpT", bufp, t), ("qz", sub)], writes=[("qhead", sub, t)])
        for qb in range(NQB):
            for sub in range(2):
                h = 2 * hp + sub
                bufq = sub
                for kt0 in range(0, NT + 2, 2):
                    for kt in (kt0, kt0 + 1):
                        if kt >= NT:
                            continue
                        par = kt % 3
                        for qc in range(2):
                            qt0 = (1024 * qb + 512 * qc) // 128
                            ph.step("tensor", lambda qb=qb, kt=kt, qc=qc, par=par,
                                    bufp=bufp, bufq=bufq: nc.tensor.matmul(
                                pS[par][:, 512 * qc:512 * (qc + 1)],
                                kpair[bufp][:, 128 * kt:128 * (kt + 1)],
                                qhead[bufq][:,
                                            1024 * qb + 512 * qc:1024 * qb + 512 * (qc + 1)],
                                start=True, stop=True),
                                reads=[("kpair", bufp, kt)] + [("qhead", bufq, qt0 + i)
                                                               for i in range(4)],
                                writes=[("pS", par, qc)])
                    for kt in (kt0, kt0 + 1):
                        if kt >= NT:
                            continue
                        par = kt % 3
                        par4 = kt % 4
                        ph.step("scalar", lambda kt=kt, par=par, par4=par4: nc.scalar.activation(
                            PT[par4][:], pS[par][:], AF.Exp, bias=st["Mb"][:], scale=0.125),
                            reads=[("pS", par, 0), ("pS", par, 1), "Mb"],
                            writes=[("PT", par4)])
                    for kp in (kt0 - 2, kt0 - 1):
                        if kp < 0 or kp >= NT:
                            continue
                        pp = kp % 4
                        for qc in range(2):
                            ph.step("tensor", lambda h=h, kp=kp, qc=qc, pp=pp: nc.tensor.matmul(
                                pPV[qc][:], st["v_all"][:, 1040 * kp + 65 * h:1040 * kp + 65 * (h + 1)],
                                PT[pp][:, 512 * qc:512 * (qc + 1)],
                                start=(kp == 0), stop=(kp == NT - 1)),
                                reads=["v_ones", ("v", kp), ("PT", pp)],
                                writes=[("pPV", qc)])
                for qc in range(2):
                    ph.step("scalar", lambda qc=qc: nc.scalar.activation(
                        rec[:], pPV[qc][64:65, :], AF.Ln),
                        reads=[("pPV", qc)], writes=["rec"])
                    ph.step("scalar", lambda: nc.scalar.activation(rec[:], rec[:], AF.Exp, scale=-1.0),
                            reads=["rec"], writes=["rec"])
                    ph.step("tensor", lambda qc=qc: nc.tensor.matmul(
                        pS[0][0:64, 0:512], st["onesfD"][0:1, 0:64], rec[:],
                        start=True, stop=True),
                        reads=["rec", "onesf"], writes=[("pS", 0, 0)])
                    ph.step("vector", lambda: nc.vector.tensor_copy(recb[:], pS[0][0:64, 0:512]),
                            reads=[("pS", 0, 0)], writes=["recb"])
                    ph.step("vector", lambda sub=sub, qb=qb, qc=qc: nc.vector.tensor_tensor(
                        stage[64 * sub:64 * (sub + 1),
                              1024 * qb + 512 * qc:1024 * qb + 512 * (qc + 1)],
                        pPV[qc][0:64, :], recb[:], ALU.mult),
                        reads=[("pPV", qc), "recb"], writes=[("stage", sub, qb)])
            for t in range(8 * qb, 8 * qb + 8):
                ph.dma("gpsimd", lambda hp=hp, t=t: nc.gpsimd.dma_start(
                    out=afm[D * t + 128 * hp:D * t + 128 * (hp + 1), :],
                    in_=stage[:, 128 * t:128 * (t + 1)]),
                    reads=[("stage", 0, qb), ("stage", 1, qb)], writes=[("afmh", hp, t)])



def emit_D3(ph, reg):
    nc, prog = ph.nc, ph.prog
    st = prog.st
    x2 = reg.get("x2", *STREAMS["x2"])
    x3 = reg.get("x3", *STREAMS["x3"])
    afm = reg.get("afm_hbm", (NT * D, 128), BF16)
    wproj = reg.get("L1_attn_wproj", (D, D), F32)

    wprojT = ph.sb("wprojT", [128, 8 * D], BF16)
    wrow = [ph.sb(f"wrow{i}", [128, D], BF16) for i in range(2)]
    ptd = ph.ps("ptd", [128, 128], BF16)
    for cc in range(8):
        b = cc % 2
        ph.dma("gpsimd", lambda cc=cc, b=b: nc.gpsimd.dma_start(
            out=wrow[b][:], in_=wproj[128 * cc:128 * (cc + 1), :]), writes=[("wrow", b)])
        for dc in range(8):
            ph.step("tensor", lambda b=b, dc=dc: nc.tensor.transpose(
                ptd[:], wrow[b][:, 128 * dc:128 * (dc + 1)], st["idbD"][:]),
                reads=[("wrow", b), "idb"], writes=["ptd"])
            ph.step("vector", lambda cc=cc, dc=dc: nc.vector.tensor_copy(
                wprojT[:, D * dc + 128 * cc:D * dc + 128 * (cc + 1)], ptd[:]),
                reads=["ptd"], writes=["wprojT"])

    at = [ph.sb(f"at{i}", [128, 8 * 128], BF16) for i in range(2)]
    xt = [ph.sb(f"xt{i}", [128, D], F32) for i in range(2)]
    x3t = [ph.sb(f"x3t{i}", [128, D], F32) for i in range(2)]
    pso = ph.ps("pso", [128, D], F32)

    for t in range(NT):
        b = t % 2
        ph.dma("sync", lambda t=t, b=b: nc.sync.dma_start(
            out=at[b][:].rearrange("p (cc tt) -> p cc tt", cc=8),
            in_=afm[D * t:D * (t + 1), :].rearrange("(cc p) tt -> p cc tt", p=128)),
            writes=[("at", b)])
        ph.dma("sync", lambda t=t, b=b: nc.sync.dma_start(
            out=xt[b][:], in_=x2[128 * t:128 * (t + 1), :]), writes=[("xt", b)])
        for cc in range(8):
            for half in range(2):
                ph.step("tensor", lambda b=b, cc=cc, half=half: nc.tensor.matmul(
                    pso[:, 512 * half:512 * (half + 1)],
                    at[b][:, 128 * cc:128 * (cc + 1)],
                    wprojT[:, D * cc + 512 * half:D * cc + 512 * (half + 1)],
                    start=(cc == 0), stop=(cc == 7)),
                    reads=[("at", b), "wprojT"], writes=[("pso", half)])
        ph.step("vector", lambda b=b: nc.vector.tensor_tensor(
            x3t[b][:], pso[:], xt[b][:], ALU.add),
            reads=[("pso", 0), ("pso", 1), ("xt", b)], writes=[("x3t", b)])
        ph.dma("gpsimd", lambda t=t, b=b: nc.gpsimd.dma_start(
            out=x3[128 * t:128 * (t + 1), :], in_=x3t[b][:]),
            reads=[("x3t", b)], writes=[("x3h", t)])



def emit_E(ph, reg):
    nc = ph.nc
    x3 = reg.get("x3", *STREAMS["x3"])
    out = reg.get("out", (T, D), F32)
    Wf = reg.get("L1_ffn_weight", (D, D), F32)
    ident = reg.get("ident_bf", (128, 128), BF16)

    idb = ph.sb("idb", [128, 128], BF16)
    ph.dma("sync", lambda: nc.sync.dma_start(out=idb[:], in_=ident), writes=["idb"])

    wrow = ph.sb("wrow", [128, D], F32)
    wsgn = ph.sb("wsgn", [128, D], BF16)
    sgnT = ph.sb("sgnT", [128, 8 * D], BF16)
    ptr = [ph.ps(f"ptr{i}", [128, 128], BF16) for i in range(2)]
    for oc in range(8):
        ph.dma("sync", lambda oc=oc: nc.sync.dma_start(out=wrow[:], in_=Wf[128 * oc:128 * (oc + 1), :]),
               writes=[("wrow",)])
        ph.step("scalar", lambda: nc.scalar.activation(wsgn[:], wrow[:], AF.Sign),
                reads=[("wrow",)], writes=[("wsgn",)])
        for dc in range(8):
            pb = dc % 2
            ph.step("tensor", lambda oc=oc, dc=dc, pb=pb: nc.tensor.transpose(
                ptr[pb][:], wsgn[:, 128 * dc:128 * (dc + 1)], idb[:]),
                reads=[("wsgn",), "idb"], writes=[("ptr", pb)])
            ph.step("vector", lambda oc=oc, dc=dc, pb=pb: nc.vector.tensor_copy(
                sgnT[:, dc * D + 128 * oc: dc * D + 128 * (oc + 1)], ptr[pb][:]),
                reads=[("ptr", pb)], writes=[("sgnT",)])

    ph.dbg = dict(sgnT=sgnT, wsgn=wsgn)

    nb = 2
    xt = [ph.sb(f"xt{i}", [128, D], F32) for i in range(nb)]
    yb = [ph.sb(f"yb{i}", [128, D], BF16) for i in range(nb)]
    yT = [ph.sb(f"yT{i}", [128, D], BF16) for i in range(nb)]
    stats2 = [ph.sb(f"stats{i}", [128, 12], F32) for i in range(nb)]
    mv2 = [ph.sb(f"mv{i}", [128, 2], F32) for i in range(nb)]
    rstd2 = [ph.sb(f"rstd{i}", [128, 1], F32) for i in range(nb)]
    res = [ph.sb(f"res{i}", [128, D], F32) for i in range(nb)]
    pmm = [ph.ps(f"pmm{i}", [128, D], F32) for i in range(nb)]
    ptr2 = [ph.ps(f"ptr2_{i}", [128, 128], BF16) for i in range(2)]

    def stageEA(t):
        b = t % nb
        stats, mv, rstd = stats2[b], mv2[b], rstd2[b]
        XT, YB, YT, PMM = ("xt", b), ("yb", b), ("yT", b), ("pmm", b)
        ST_, MV_, RS_ = ("st", b), ("mv", b), ("rstd", b)
        ph.dma("sync", lambda t=t, b=b: nc.sync.dma_start(out=xt[b][:], in_=x3[128 * t:128 * (t + 1), :]),
               writes=[XT])
        ph.step("vector", lambda b=b, stats=stats: nc.vector.bn_stats(stats[:, 0:6], xt[b][:, 0:512]),
                reads=[XT], writes=[ST_])
        ph.step("vector", lambda b=b, stats=stats: nc.vector.bn_stats(stats[:, 6:12], xt[b][:, 512:1024]),
                reads=[XT], writes=[ST_])
        ph.step("vector", lambda mv=mv, stats=stats: nc.vector.bn_aggr(mv[:], stats[:]), reads=[ST_], writes=[MV_])
        ph.step("vector", lambda rstd=rstd, mv=mv: nc.vector.tensor_scalar(rstd[:], mv[:, 1:2], 1e-5, None, ALU.add),
                reads=[MV_], writes=[RS_])
        ph.step("scalar", lambda rstd=rstd: nc.scalar.activation(rstd[:], rstd[:], AF.Sqrt),
                reads=[RS_], writes=[RS_])
        ph.step("vector", lambda rstd=rstd: nc.vector.reciprocal(rstd[:], rstd[:]),
                reads=[RS_], writes=[RS_])
        ph.step("vector", lambda b=b, mv=mv, rstd=rstd: nc.vector.tensor_scalar(yb[b][:], xt[b][:], mv[:, 0:1], rstd[:],
                                                              ALU.subtract, ALU.mult),
                reads=[XT, MV_, RS_], writes=[YB])
        for dc in range(8):
            pb = dc % 2
            ph.step("tensor", lambda b=b, dc=dc, pb=pb: nc.tensor.transpose(
                ptr2[pb][:], yb[b][:, 128 * dc:128 * (dc + 1)], idb[:]),
                reads=[YB, "idb"], writes=[("ptr2", pb)])
            ph.step("scalar", lambda b=b, dc=dc, pb=pb: nc.scalar.copy(
                yT[b][:, 128 * dc:128 * (dc + 1)], ptr2[pb][:]),
                reads=[("ptr2", pb)], writes=[YT])
        for half in range(2):
            for dc in range(8):
                ph.step("tensor", lambda b=b, dc=dc, half=half: nc.tensor.matmul(
                    pmm[b][:, 512 * half:512 * (half + 1)],
                    yT[b][:, 128 * dc:128 * (dc + 1)],
                    sgnT[:, dc * D + 512 * half: dc * D + 512 * (half + 1)],
                    start=(dc == 0), stop=(dc == 7)),
                    reads=[YT, "sgnT"], writes=[PMM])

    def stageEB(t):
        b = t % nb
        XT, RES, PMM = ("xt", b), ("res", b), ("pmm", b)
        ph.step("vector", lambda b=b: nc.vector.tensor_tensor(res[b][:], pmm[b][:], xt[b][:], ALU.add),
                reads=[PMM, XT], writes=[RES])
        ph.dma("gpsimd", lambda t=t, b=b: nc.gpsimd.dma_start(out=out[128 * t:128 * (t + 1), :], in_=res[b][:]),
               reads=[RES], writes=[("out_hbm", t)])

    stageEA(0)
    for t in range(1, NT):
        stageEA(t)
        stageEB(t - 1)
    stageEB(NT - 1)
    ph.dbg.update(yb=yb, xt=xt)


PHASE_EMITTERS = {"A": emit_A, "B": emit_B,
                  "C1": emit_C1, "C1b": emit_C1b, "C3": emit_C3,
                  "D0": emit_D0, "D1": emit_D1,
                  "D2": emit_D2, "D3": emit_D3, "E": emit_E}



GROUP_IO = {
    ("A", "B"): (["x0"], ["x1"]),
    ("C",): (["x1"], ["x2"]),
    ("D1", "D2", "D3"): (["x2"], ["x3"]),
    ("E",): (["x3"], ["out"]),
}

WEIGHT_NAMES = [
    "L0_attn_norm_g", "L0_attn_norm_b", "L0_attn_W_h", "L0_attn_b_h",
    "L0_attn_W_qk", "L0_attn_b_qk", "L0_attn_qk_gamma", "L0_attn_qk_beta",
    "L0_attn_rel_emb", "L0_attn_W_o", "L0_attn_b_o",
    "L0_ffn_w_gate", "L0_ffn_w1", "L0_ffn_w2", "L0_ffn_w3",
    "L0_ffn_sw1", "L0_ffn_sw2", "L0_ffn_sw3", "L0_ffn_expert_bias",
    "L1_attn_wq", "L1_attn_wk", "L1_attn_wv", "L1_attn_wproj", "L1_attn_g_rms",
    "L1_ffn_weight",
]


def kernel(**inputs):
    consts = host_constants()
    x_full = np.ascontiguousarray(inputs["L0_attn_x"])

    host_common = {k: np.ascontiguousarray(inputs[k]) for k in WEIGHT_NAMES}
    host_common["L0_attn_norm_g"] = host_common["L0_attn_norm_g"].reshape(1, D)
    host_common["L0_attn_norm_b"] = host_common["L0_attn_norm_b"].reshape(1, D)
    host_common["L0_attn_b_h"] = host_common["L0_attn_b_h"].reshape(1, 4096)
    host_common["L0_attn_b_qk"] = host_common["L0_attn_b_qk"].reshape(128, 1)
    host_common["L0_attn_b_o"] = host_common["L0_attn_b_o"].reshape(1, D)
    host_common["L1_attn_g_rms"] = host_common["L1_attn_g_rms"].reshape(1, 64)
    host_common["L0_ffn_expert_bias"] = host_common["L0_ffn_expert_bias"].reshape(1, 8)
    host_common["L0_ffn_w1"] = host_common["L0_ffn_w1"].reshape(8 * D, 512)
    host_common["L0_ffn_w2"] = host_common["L0_ffn_w2"].reshape(8 * 512, D)
    host_common["L0_ffn_w3"] = host_common["L0_ffn_w3"].reshape(8 * D, 512)
    host_common["grms_rep"] = np.ascontiguousarray(
        np.tile(inputs["L1_attn_g_rms"].reshape(1, 64), (1, 16)).astype(np.float32))
    host_common.update(consts)
    per_core = [dict() for _ in range(N_CORES)]
    for c in range(N_CORES):
        per_core[c]["x0"] = np.ascontiguousarray(x_full[c])
        per_core[c]["mask_core"] = np.full((8, 1), 1.0 if c == 0 else 0.0, np.float32)
        per_core[c]["posbase"] = np.full((1, 1), float(c * 2 * T), np.float32)
        oh = np.zeros((1, 8), np.float32); oh[0, c] = 1.0
        per_core[c]["slot_onehot"] = oh

    for group in LAUNCH_PLAN:
        gkey = tuple(group)
        ins, outs = GROUP_IO_FUSED(gkey)
        nc, reg, prog = build_program(group, ins, outs)
        in_maps = []
        for c in range(N_CORES):
            m = {}
            for name in reg.map:
                if name in reg.ext_in:
                    if name in per_core[c]:
                        m[name] = per_core[c][name]
                    else:
                        m[name] = host_common[name]
            in_maps.append(m)
        res = run_bass_kernel_spmd(nc, in_maps, core_ids=list(range(N_CORES)))
        for c in range(N_CORES):
            for name in reg.ext_out:
                if name in reg.map:
                    per_core[c][name] = res.results[c][name]

    return np.stack([per_core[c]["out"] for c in range(N_CORES)], axis=0)


def GROUP_IO_FUSED(gkey):
    produced = set()
    ins, outs = set(), set()
    for p in gkey:
        need, prod = PH_IO[p]
        for n in need:
            if n not in produced:
                ins.add(n)
        for o in prod:
            produced.add(o)
    all_later_needs = set()
    for grp, (need, prod) in PH_IO.items():
        if grp not in gkey:
            all_later_needs |= set(need)
    for o in produced:
        if o == "out" or o in all_later_needs:
            outs.add(o)
    ins |= set(WEIGHT_NAMES) | {"ident_bf", "ident_f32", "OH511", "cosrep",
                                "sinrep", "T8", "mask_core", "posbase", "slot_onehot",
                                "grms_rep"}
    ins -= outs
    return ins, outs


# version 43 (final)
# speedup vs baseline: 1.2123x; 1.2123x over previous
import math
import numpy as np
import ml_dtypes
import concourse.bass as bass
import concourse.mybir as mybir
from concourse.bass_utils import run_bass_kernel_spmd

F32 = mybir.dt.float32
BF16 = mybir.dt.bfloat16
AF = mybir.ActivationFunctionType
ALU = mybir.AluOpType
AX = mybir.AxisListType

N_CORES = 8
T = 4096
D = 1024
NT = T // 128
BF = ml_dtypes.bfloat16

ENGINES = ("sync", "tensor", "vector", "scalar", "gpsimd")
DMA_QUEUES = ("sync", "gpsimd", "scalar")

LAUNCH_PLAN = [["A", "B", "C1", "C1b", "C3",
                "D0", "D1", "D2", "D3", "E"]]

PH_IO = {
    "A": (("x0",), ()),
    "B": (("x0",), ("x1",)),
    "C1": (("x1",), ()),
    "C1b": (("x1",), ("x2",)),
    "C3": (("x2",), ("x2",)),
    "D0": ((), ()),
    "D1": (("x2",), ()),
    "D2": ((), ()),
    "D3": (("x2",), ("x3",)),
    "E": (("x3",), ("out",)),
}


class Phase:
    def __init__(self, prog, name):
        self.prog = prog
        self.nc = prog.nc
        self.name = name
        self.steps = []
        self._ctxs = []

    def sb(self, name, shape, dtype):
        cm = self.nc.sbuf_tensor(f"{self.name}_{name}", shape, dtype)
        t = cm.__enter__()
        self._ctxs.append(cm)
        return t

    def ps(self, name, shape, dtype=F32):
        cm = self.nc.psum_tensor(f"{self.name}_{name}", shape, dtype)
        t = cm.__enter__()
        self._ctxs.append(cm)
        return t

    def step(self, eng, fn, reads=(), writes=(), extra_waits=()):
        assert eng in ENGINES
        self.steps.append((eng, "c", fn, tuple(reads), tuple(writes), None,
                           tuple(extra_waits)))

    def dma(self, eng, fn, reads=(), writes=(), extra_waits=()):
        assert eng in DMA_QUEUES
        self.steps.append((eng, "d", fn, tuple(reads), tuple(writes), None,
                           tuple(extra_waits)))

    def collective(self, name, fn, reads=(), writes=()):
        cm = self.nc.semaphore(f"cc_{name}")
        sem = cm.__enter__()
        self.prog._sem_ctxs.append(cm)
        self.steps.append(("gpsimd", "cc", fn, tuple(reads), tuple(writes), sem, ()))
        return sem


DMA_POOL = 8
PRUNE_SAME_ENGINE = False


class Prog:
    def __init__(self, nc):
        self.nc = nc
        self.phase_idx = 0
        self._sem_ctxs = []
        self._prog_ctxs = []
        self.st = {}
        self.barrier = self._mksem("barrier")
        self.psem = {e: self._mksem(f"p_{e}") for e in ENGINES}
        self.dpool = {q: [self._mksem(f"d_{q}_{i}") for i in range(DMA_POOL)]
                      for q in DMA_QUEUES}
        self.cnt = {e: 0 for e in ENGINES}
        self.duse = {q: [0] * DMA_POOL for q in DMA_QUEUES}

    def _mksem(self, name):
        cm = self.nc.semaphore(name)
        s = cm.__enter__()
        self._sem_ctxs.append(cm)
        return s

    def sb(self, name, shape, dtype):
        cm = self.nc.sbuf_tensor(f"g_{name}", shape, dtype)
        t = cm.__enter__()
        self._prog_ctxs.append(cm)
        return t

    def mark(self):
        return len(self._prog_ctxs)

    def release(self, idx):
        while len(self._prog_ctxs) > idx:
            self._prog_ctxs.pop().__exit__(None, None, None)

    def close(self):
        self.release(0)

    def flush_phase(self, ph):
        nc = self.nc
        pidx = self.phase_idx
        psem, dpool = self.psem, self.dpool
        cnt, duse = self.cnt, self.duse

        last_write = {}
        readers = {}

        emit = {e: [] for e in ENGINES}

        for eng, kind, fn, reads, writes, cc_sem, extra_waits in ph.steps:
            deps = []
            for tok in reads:
                d = last_write.get(tok)
                if d:
                    deps.append(d)
            for tok in writes:
                d = last_write.get(tok)
                if d:
                    deps.append(d)
                deps.extend(readers.get(tok, ()))
            waits = {}
            for d in deps:
                dk, dsrc, dval = d
                if dk == "c" and dsrc == eng and kind == "c" and (
                        PRUNE_SAME_ENGINE or eng == "tensor"):
                    continue
                if dk == "c":
                    sem = psem[dsrc]
                elif dk == "d":
                    sem = dpool[dsrc[0]][dsrc[1]]
                else:
                    sem = dsrc
                key = id(sem)
                if key not in waits or waits[key][1] < dval:
                    waits[key] = (sem, dval)
            if kind == "c":
                cnt[eng] += 1
                mydep = ("c", eng, cnt[eng])
                inc_sem, inc_val = psem[eng], 1
            elif kind == "d":
                n = sum(duse[eng])
                slot = n % DMA_POOL
                prev = duse[eng][slot]
                if prev:
                    sem = dpool[eng][slot]
                    k = id(sem)
                    if k not in waits or waits[k][1] < 16 * prev:
                        waits[k] = (sem, 16 * prev)
                duse[eng][slot] += 1
                mydep = ("d", (eng, slot), 16 * duse[eng][slot])
                inc_sem, inc_val = dpool[eng][slot], 16
            else:
                mydep = ("cc", cc_sem, 1)
                inc_sem, inc_val = cc_sem, 1
            for tok in writes:
                last_write[tok] = mydep
                readers[tok] = []
            for tok in reads:
                readers.setdefault(tok, []).append(mydep)
            for sem, val in extra_waits:
                k = id(sem)
                if k not in waits or waits[k][1] < val:
                    waits[k] = (sem, val)
            emit[eng].append((list(waits.values()), fn, inc_sem, inc_val))

        bar_target = 5 * pidx

        with nc.Block(f"ph{pidx}_{ph.name}") as block:
            def make(e):
                cnt_e = cnt[e]
                duse_e = {q: list(duse[q]) for q in DMA_QUEUES}

                def body(engobj):
                    if pidx > 0:
                        engobj.wait_ge(self.barrier, bar_target)
                    hw = {}
                    for waits, fn, inc_sem, inc_val in emit[e]:
                        for sem, val in waits:
                            k = id(sem)
                            if hw.get(k, -1) >= val:
                                continue
                            engobj.wait_ge(sem, val)
                            hw[k] = val
                        inst = fn()
                        inst.then_inc(inc_sem, inc_val)
                    if cnt_e:
                        engobj.wait_ge(psem[e], cnt_e)
                    if e in DMA_QUEUES:
                        for i in range(DMA_POOL):
                            if duse_e[e][i]:
                                engobj.wait_ge(dpool[e][i], 16 * duse_e[e][i])
                    engobj.sem_inc(self.barrier, 1)
                body.__name__ = f"b{pidx}_{e}"
                return body
            for e in ENGINES:
                getattr(block, e)(make(e))

        for cm in reversed(ph._ctxs):
            cm.__exit__(None, None, None)
        self.phase_idx += 1



def host_constants():
    C = {}
    C["ident_bf"] = np.eye(128, dtype=BF)
    C["ident_f32"] = np.eye(128, dtype=np.float32)
    NB, MAXD = 32, 128
    deltas = np.arange(-255, 256)
    n = deltas
    ret = (n < 0).astype(np.int64) * (NB // 2)
    na = np.abs(n)
    max_exact = NB // 4
    is_small = na < max_exact
    with np.errstate(divide="ignore"):
        val_large = max_exact + (
            np.log(np.maximum(na, 1).astype(np.float32) / max_exact)
            / math.log(MAXD / max_exact) * (NB // 2 - max_exact)
        ).astype(np.int64)
    val_large = np.minimum(val_large, NB // 2 - 1)
    bucket = ret + np.where(is_small, na, val_large)
    oh = np.zeros((NB, 511), dtype=np.float32)
    oh[bucket, np.arange(511)] = 1.0
    C["OH511"] = oh
    hd = 64
    inv_freq = 1.0 / (10000.0 ** (np.arange(0, hd, 2, dtype=np.float32) / hd))
    freqs = np.outer(np.arange(T, dtype=np.float32), inv_freq)
    cos = np.cos(freqs).astype(BF).astype(np.float32)
    sin = np.sin(freqs).astype(BF).astype(np.float32)
    C["cosrep"] = np.tile(cos, (1, 16))
    C["sinrep"] = np.tile(sin, (1, 16))
    C["T8"] = np.triu(np.ones((8, 8), dtype=np.float32))
    return C



class HbmReg:
    def __init__(self, nc, ext_in, ext_out):
        self.nc = nc
        self.ext_in = set(ext_in)
        self.ext_out = set(ext_out)
        self.map = {}
        self.shapes = {}

    def get(self, name, shape, dtype):
        if name in self.map:
            return self.map[name]
        if name in self.ext_in:
            kind = "ExternalInput"
        elif name in self.ext_out:
            kind = "ExternalOutput"
        else:
            kind = "Internal"
        ap = self.nc.dram_tensor(name, list(shape), dtype, kind=kind).ap()
        self.map[name] = ap
        self.shapes[name] = (tuple(shape), dtype)
        return ap


STREAMS = {
    "x1": ((T, D), F32),
    "x2": ((T, D), F32),
    "x3": ((T, D), F32),
}


PH_GROUP = {"A": "AB", "B": "AB",
            "C1": "C", "C1b": "C", "C3": "C",
            "D0": "D", "D1": "D", "D2": "D", "D3": "D",
            "E": "E"}


def build_program(phases, ext_in_names, ext_out_names):
    nc = bass.Bass("TRN2", target_bir_lowering=False, debug=False)
    reg = HbmReg(nc, ext_in_names, ext_out_names)
    prog = Prog(nc)
    cur_group, group_mark = None, 0
    for name in phases:
        g = PH_GROUP[name]
        if g != cur_group:
            if cur_group is not None:
                prog.release(group_mark)
            cur_group, group_mark = g, prog.mark()
        ph = Phase(prog, name)
        PHASE_EMITTERS[name](ph, reg)
        prog.flush_phase(ph)
    prog.close()
    return nc, reg, prog



def ln_tiles(ph, nc, src_ap_fn, ntiles, out_fn, name, eng_stats="vector"):
    raise NotImplementedError



class LNHelper:

    def __init__(self, ph, tag, nslots=2):
        self.ph = ph
        self.nc = ph.nc
        self.stats = [ph.sb(f"{tag}_stats{i}", [128, 12], F32) for i in range(nslots)]
        self.mv = [ph.sb(f"{tag}_mv{i}", [128, 2], F32) for i in range(nslots)]
        self.rstd = [ph.sb(f"{tag}_rstd{i}", [128, 1], F32) for i in range(nslots)]
        self.tag = tag

    def emit(self, slot, x_ap, x_tok, y_ap, y_tok):
        ph, nc = self.ph, self.nc
        st, mv, rs = self.stats[slot], self.mv[slot], self.rstd[slot]
        ST, MV, RS = (self.tag, "st", slot), (self.tag, "mv", slot), (self.tag, "rs", slot)
        ph.step("vector", lambda: nc.vector.bn_stats(st[:, 0:6], x_ap[:, 0:512]),
                reads=[x_tok], writes=[ST])
        ph.step("vector", lambda: nc.vector.bn_stats(st[:, 6:12], x_ap[:, 512:1024]),
                reads=[x_tok], writes=[ST])
        ph.step("vector", lambda: nc.vector.bn_aggr(mv[:], st[:]), reads=[ST], writes=[MV])
        ph.step("vector", lambda: nc.vector.tensor_scalar(rs[:], mv[:, 1:2], 1e-5, None, ALU.add),
                reads=[MV], writes=[RS])
        ph.step("scalar", lambda: nc.scalar.activation(rs[:], rs[:], AF.Sqrt),
                reads=[RS], writes=[RS])
        ph.step("vector", lambda: nc.vector.reciprocal(rs[:], rs[:]), reads=[RS], writes=[RS])
        ph.step("vector", lambda: nc.vector.tensor_scalar(y_ap[:], x_ap[:], mv[:, 0:1], rs[:],
                                                          ALU.subtract, ALU.mult),
                reads=[x_tok, MV, RS], writes=[y_tok])



def emit_A(ph, reg):
    nc, prog = ph.nc, ph.prog
    st = prog.st
    x0 = reg.get("x0", (T, D), F32)
    v0 = reg.get("v0_hbm", (T, 2048), BF16)
    Wh = reg.get("L0_attn_W_h", (D, 4096), F32)
    Wqk = reg.get("L0_attn_W_qk", (D, 128), F32)
    Wo = reg.get("L0_attn_W_o", (2048, D), F32)
    bh = reg.get("L0_attn_b_h", (1, 4096), F32)
    bqk = reg.get("L0_attn_b_qk", (128, 1), F32)
    bo = reg.get("L0_attn_b_o", (1, D), F32)
    ng = reg.get("L0_attn_norm_g", (1, D), F32)
    nb = reg.get("L0_attn_norm_b", (1, D), F32)
    gam = reg.get("L0_attn_qk_gamma", (4, 128), F32)
    bet = reg.get("L0_attn_qk_beta", (4, 128), F32)
    rel = reg.get("L0_attn_rel_emb", (32, 1), F32)
    identb = reg.get("ident_bf", (128, 128), BF16)
    identf = reg.get("ident_f32", (128, 128), F32)
    ohrev = reg.get("OH511", (32, 511), F32)
    browrep = reg.get("browrep_hbm", (128, 511), F32)

    st["idb"] = prog.sb("idb", [128, 128], BF16)
    st["idf"] = prog.sb("idf", [128, 128], F32)
    st["Wqk_fm"] = prog.sb("Wqk_fm", [128, 8 * 128], BF16)
    st["Whg"] = prog.sb("Whg", [128, 8 * 2048], BF16)
    st["Wo_sb"] = prog.sb("Wo_sb", [128, 16 * D], BF16)
    st["gbc"] = prog.sb("gbc", [128, D], F32)
    st["bbc"] = prog.sb("bbc", [128, D], F32)
    st["bh_row"] = prog.sb("bh_row", [1, 4096], BF16)
    st["bo_row"] = prog.sb("bo_row", [1, D], BF16)
    st["bqkT"] = prog.sb("bqkT", [128, 1], F32)
    st["bhgT"] = prog.sb("bhgT", [128, 16], F32)
    st["gbT"] = prog.sb("gbT", [128, 8], F32)
    st["gq"] = prog.sb("gq", [128, 2], F32)
    st["ones_b"] = prog.sb("ones_b", [1, 128], BF16)
    st["ones_f"] = prog.sb("ones_f", [1, 128], F32)
    st["biasT"] = prog.sb("biasT", [128, 2 * 256], F32)
    st["linkv"] = prog.sb("linkv", [128, 2048], BF16)

    ln1 = LNHelper(ph, "ln1", 2)
    ln2 = LNHelper(ph, "ln2", 2)
    xg = [ph.sb(f"xg{j}", [128, D], F32) for j in range(2)]
    yb = [ph.sb(f"yb{j}", [128, D], F32) for j in range(2)]
    nrm = [ph.sb(f"nrm{j}", [128, D], F32) for j in range(2)]
    nrmb = [ph.sb(f"nrmb{j}", [128, D], BF16) for j in range(2)]
    nT2 = [ph.sb(f"nT{i}", [128, 8 * 256], BF16) for i in range(2)]
    qk2 = [ph.sb(f"qk{i}", [128, 256], F32) for i in range(2)]
    lkf2 = [ph.sb(f"lkf{i}", [128, 256], BF16) for i in range(2)]
    lktm2 = [[ph.sb(f"lktm{i}_{j}", [128, 128], BF16) for j in range(2)] for i in range(2)]
    vbf2 = [[ph.sb(f"vbf{i}_{j}", [128, 2048], BF16) for j in range(2)] for i in range(2)]
    st["Whv"] = ph.sb("Whv", [128, 8 * 2048], BF16)
    pqk = ph.ps("pqk", [128, 256], F32)
    ptA = ph.ps("ptA", [128, 128], BF16)
    pv = [ph.ps(f"pv{j}", [128, 512], F32) for j in range(2)]
    pkv = ph.ps("pkv", [128, 2048], F32)


    ph.step("gpsimd", lambda: nc.gpsimd.memset(st["ones_b"][:], 1.0), writes=["ones_b"])
    ph.step("gpsimd", lambda: nc.gpsimd.memset(st["ones_f"][:], 1.0), writes=["ones_f"])
    ph.dma("sync", lambda: nc.sync.dma_start(out=st["idb"][:], in_=identb), writes=["idb"])
    ph.dma("sync", lambda: nc.sync.dma_start(out=st["idf"][:], in_=identf), writes=["idf"])

    for dc in range(8):
        ph.dma("gpsimd", lambda dc=dc: nc.gpsimd.dma_start(
            out=st["Wqk_fm"][:, 128 * dc:128 * (dc + 1)], in_=Wqk[128 * dc:128 * (dc + 1), :]),
            writes=["Wqk_fm"])
    ph.dma("gpsimd", lambda: nc.gpsimd.dma_start(out=st["bh_row"][:], in_=bh), writes=["bh_row"])
    ph.dma("gpsimd", lambda: nc.gpsimd.dma_start(out=st["bo_row"][:], in_=bo), writes=["bo_row"])
    ph.dma("sync", lambda: nc.sync.dma_start(out=st["bqkT"][:], in_=bqk), writes=["bqkT"])
    ph.dma("gpsimd", lambda: nc.gpsimd.dma_start(
        out=st["bhgT"][:], in_=bh[0:1, 2048:4096].rearrange("o (c p) -> (o p) c", p=128),
        allow_slow_non_contiguous=True), writes=["bhgT"])

    rowbuf = ph.sb("rowbuf", [1, D], F32)
    pbc = pv[0]
    for nm, src, dst in (("g", ng, "gbc"), ("b", nb, "bbc")):
        ph.dma("sync", lambda src=src: nc.sync.dma_start(out=rowbuf[:], in_=src),
               writes=["rowbuf"])
        for h in range(2):
            ph.step("tensor", lambda h=h: nc.tensor.matmul(
                pbc[:], st["ones_f"][:], rowbuf[0:1, 512 * h:512 * (h + 1)], start=True, stop=True),
                reads=["rowbuf", "ones_f"], writes=[("pv", 0)])
            ph.step("vector", lambda h=h, dst=dst: nc.vector.tensor_copy(
                st[dst][:, 512 * h:512 * (h + 1)], pbc[:]),
                reads=[("pv", 0)], writes=[dst])

    gbrows = ph.sb("gbrows", [8, 128], F32)
    pgb = pv[1][:, 0:8]
    ph.dma("sync", lambda: nc.sync.dma_start(out=gbrows[0:4, :], in_=gam), writes=["gbrows"])
    ph.dma("sync", lambda: nc.sync.dma_start(out=gbrows[4:8, :], in_=bet), writes=["gbrows"])
    ph.step("tensor", lambda: nc.tensor.transpose(pgb, gbrows[:], st["idf"][0:8, 0:8]),
            reads=["gbrows", "idf"], writes=[("pv", 1)])
    ph.step("vector", lambda: nc.vector.tensor_copy(st["gbT"][:], pgb), reads=[("pv", 1)], writes=["gbT"])
    ph.step("vector", lambda: nc.vector.tensor_scalar(st["gq"][:, 0:1], st["gbT"][:, 0:1],
                                                      1.0 / 256.0, None, ALU.mult),
            reads=["gbT"], writes=["gq"])
    ph.step("vector", lambda: nc.vector.tensor_scalar(st["gq"][:, 1:2], st["gbT"][:, 4:5],
                                                      1.0 / 256.0, None, ALU.mult),
            reads=["gbT"], writes=["gq"])

    relsb = ph.sb("relsb", [32, 1], F32)
    ohsb = ph.sb("ohsb", [32, 511], F32)
    brow = ph.sb("brow", [1, 511], F32)
    browb = ph.sb("browb", [128, 511], F32)
    pb1 = pkv[0:1, 0:511]
    pb2 = pkv[:, 512:1023]
    ph.dma("sync", lambda: nc.sync.dma_start(out=relsb[:], in_=rel), writes=["relsb"])
    ph.dma("sync", lambda: nc.sync.dma_start(out=ohsb[:], in_=ohrev), writes=["ohsb"])
    ph.step("vector", lambda: nc.vector.tensor_scalar(relsb[:], relsb[:], float(math.sqrt(128.0)),
                                                      None, ALU.mult),
            reads=["relsb"], writes=["relsb"])
    ph.step("tensor", lambda: nc.tensor.matmul(pb1, relsb[:], ohsb[:], start=True, stop=True),
            reads=["relsb", "ohsb"], writes=[("pkv", 0)])
    ph.step("vector", lambda: nc.vector.tensor_copy(brow[:], pb1), reads=[("pkv", 0)], writes=["brow"])
    ph.step("tensor", lambda: nc.tensor.matmul(pb2, st["ones_f"][:], brow[:], start=True, stop=True),
            reads=["brow", "ones_f"], writes=[("pkv", 1)])
    ph.step("vector", lambda: nc.vector.tensor_copy(browb[:], pb2), reads=[("pkv", 1)], writes=["browb"])
    ph.dma("sync", lambda: nc.sync.dma_start(out=browrep, in_=browb[:]), reads=["browb"], writes=["browrep"])
    for jt in range(2):
        def mk(jt=jt):
            src = browrep.copy()
            pair_t = type(src.ap)
            src.ap = pair_t([(510, 128), (1, 256)])
            src.offset = 255 - 128 * jt
            return nc.sync.dma_start(out=st["biasT"][:, 256 * jt:256 * (jt + 1)], in_=src)
        ph.dma("sync", mk, reads=["browrep"], writes=["biasT"])

    for dc in range(8):
        ph.dma("gpsimd", lambda dc=dc: nc.gpsimd.dma_start(
            out=st["Whv"][:, 2048 * dc:2048 * (dc + 1)],
            in_=Wh[128 * dc:128 * (dc + 1), 0:2048]),
            writes=["Whv"])

    for g in range(16):
        G = g % 2
        nT, qk, lkf, lktm, vbf = nT2[G], qk2[G], lkf2[G], lktm2[G], vbf2[G]
        ptA2 = pqk.bitcast(BF16)[:, 0:128]
        if g < 8:
            ph.dma("gpsimd", lambda g=g: nc.gpsimd.dma_start(
                out=st["Whg"][:, 2048 * g:2048 * (g + 1)], in_=Wh[128 * g:128 * (g + 1), 2048:4096]),
                writes=["Whg"])
        for ec in (2 * g, 2 * g + 1):
            if ec < 16:
                ph.dma("gpsimd", lambda ec=ec: nc.gpsimd.dma_start(
                    out=st["Wo_sb"][:, D * ec:D * (ec + 1)], in_=Wo[128 * ec:128 * (ec + 1), :]),
                    writes=["Wo_sb"])
        for j in range(2):
            t = 2 * g + j
            XG, YB, NRM, NRMB = ("xg", j), ("yb", j), ("nrm", j), ("nrmb", j)
            ph.dma("sync", lambda t=t, j=j: nc.sync.dma_start(
                out=xg[j][:], in_=x0[128 * t:128 * (t + 1), :]), writes=[XG])
            ln1.emit(j, xg[j], XG, yb[j], YB)
            ln2.emit(j, yb[j], YB, nrm[j], NRM)
            ph.step("vector", lambda j=j: nc.vector.tensor_tensor(nrm[j][:], nrm[j][:], st["gbc"][:], ALU.mult),
                    reads=[NRM, "gbc"], writes=[NRM])
            ph.step("vector", lambda j=j: nc.vector.tensor_tensor(nrmb[j][:], nrm[j][:], st["bbc"][:], ALU.add),
                    reads=[NRM, "bbc"], writes=[NRMB])
            for dc in range(8):
                pt = ptA if dc % 2 == 0 else ptA2
                PTK = "ptA" if dc % 2 == 0 else "pqk"
                ph.step("tensor", lambda j=j, dc=dc, pt=pt: nc.tensor.transpose(
                    pt[:], nrmb[j][:, 128 * dc:128 * (dc + 1)], st["idb"][:]),
                    reads=[NRMB, "idb"], writes=[PTK])
                if dc % 2 == 0:
                    ph.step("scalar", lambda j=j, dc=dc, nT=nT, pt=pt: nc.scalar.copy(
                        nT[:, 256 * dc + 128 * j:256 * dc + 128 * (j + 1)], pt[:]),
                        reads=[PTK], writes=[("nT", G, j, dc)])
                else:
                    ph.step("vector", lambda j=j, dc=dc, nT=nT, pt=pt: nc.vector.tensor_copy(
                        nT[:, 256 * dc + 128 * j:256 * dc + 128 * (j + 1)], pt[:]),
                        reads=[PTK], writes=[("nT", G, j, dc)])
        for dc in range(8):
            ph.step("tensor", lambda dc=dc, nT=nT: nc.tensor.matmul(
                pqk[:], st["Wqk_fm"][:, 128 * dc:128 * (dc + 1)], nT[:, 256 * dc:256 * (dc + 1)],
                start=(dc == 0), stop=(dc == 7)),
                reads=[("nT", G, 0, dc), ("nT", G, 1, dc), "Wqk_fm"], writes=["pqk"])
        ph.step("scalar", lambda qk=qk: nc.scalar.activation(qk[:], pqk[:], AF.Silu, bias=st["bqkT"][:]),
                reads=["pqk", "bqkT"], writes=[("qk", G)])
        ph.step("vector", lambda qk=qk, lkf=lkf: nc.vector.tensor_scalar(lkf[:], qk[:], st["gbT"][:, 3:4],
                                                          st["gbT"][:, 7:8], ALU.mult, ALU.add),
                reads=[("qk", G), "gbT"], writes=[("lkf", G)])
        for j in range(2):
            ph.step("tensor", lambda j=j, lkf=lkf: nc.tensor.transpose(
                ptA[:], lkf[:, 128 * j:128 * (j + 1)], st["idb"][:]),
                reads=[("lkf", G), "idb"], writes=["ptA"])
            ph.step("vector", lambda j=j, lktm=lktm: nc.vector.tensor_copy(lktm[j][:], ptA[:]),
                    reads=["ptA"], writes=[("lktm", G, j)])
        for j in range(2):
            for hc in range(4):
                pvb = pv[hc % 2]
                PV = ("pv", hc % 2)
                for dc in range(8):
                    ph.step("tensor", lambda j=j, hc=hc, dc=dc, pvb=pvb, nT=nT: nc.tensor.matmul(
                        pvb[:], nT[:, 256 * dc + 128 * j:256 * dc + 128 * (j + 1)],
                        st["Whv"][:, 2048 * dc + 512 * hc:2048 * dc + 512 * (hc + 1)],
                        start=(dc == 0), stop=False),
                        reads=[("nT", G, j, dc), "Whv"], writes=[PV])
                ph.step("tensor", lambda hc=hc, pvb=pvb: nc.tensor.matmul(
                    pvb[:], st["ones_b"][:], st["bh_row"][0:1, 512 * hc:512 * (hc + 1)],
                    start=False, stop=True),
                    reads=["ones_b", "bh_row"], writes=[PV])
                ph.step("scalar", lambda j=j, hc=hc, pvb=pvb, vbf=vbf: nc.scalar.activation(
                    vbf[j][:, 512 * hc:512 * (hc + 1)], pvb[:], AF.Silu),
                    reads=[PV], writes=[("vbf", G, j, hc)])
            ph.dma("gpsimd", lambda g=g, j=j, vbf=vbf: nc.gpsimd.dma_start(
                out=v0[256 * g + 128 * j:256 * g + 128 * (j + 1), :], in_=vbf[j][:]),
                reads=[("vbf", G, j, 0), ("vbf", G, j, 1), ("vbf", G, j, 2), ("vbf", G, j, 3)],
                writes=[("v0h", g, j)])
            for hc in range(4):
                ph.step("tensor", lambda g=g, j=j, hc=hc, lktm=lktm, vbf=vbf: nc.tensor.matmul(
                    pkv[:, 512 * hc:512 * (hc + 1)], lktm[j][:],
                    vbf[j][:, 512 * hc:512 * (hc + 1)],
                    start=(g == 0 and j == 0), stop=(g == 15 and j == 1),
                    skip_group_check=True),
                    reads=[("lktm", G, j), ("vbf", G, j, hc)], writes=[("pkv", hc)])
    for hc in range(4):
        ph.step("vector", lambda hc=hc: nc.vector.tensor_scalar(
            st["linkv"][:, 512 * hc:512 * (hc + 1)], pkv[:, 512 * hc:512 * (hc + 1)],
            1.0 / T, None, ALU.mult),
            reads=[("pkv", hc)], writes=["linkv"])



def emit_B(ph, reg):
    nc, prog = ph.nc, ph.prog
    st = prog.st
    x0 = reg.get("x0", (T, D), F32)
    v0 = reg.get("v0_hbm", (T, 2048), BF16)
    x1 = reg.get("x1", *STREAMS["x1"])

    ln1 = LNHelper(ph, "ln1", 2)
    ln2 = LNHelper(ph, "ln2", 2)
    xg = [ph.sb(f"xg{j}", [128, D], F32) for j in range(2)]
    yb = [ph.sb(f"yb{j}", [128, D], F32) for j in range(2)]
    rsum = [ph.sb(f"rsum{j}", [128, D], F32) for j in range(8)]
    nrm = [ph.sb(f"nrm{j}", [128, D], F32) for j in range(1)]
    nrmb = [ph.sb(f"nrmb{j}", [128, D], BF16) for j in range(2)]
    nT2 = [ph.sb(f"nT{i}", [128, 8 * 512], BF16) for i in range(2)]
    qk = ph.sb("qk", [128, 512], F32)
    qqb = ph.sb("qqb", [128, 512], BF16)
    qkb = ph.sb("qkb", [128, 512], BF16)
    lqb = ph.sb("lqb", [128, 512], BF16)
    gate = ph.sb("gate", [128, 16 * 512], BF16)
    s1 = ph.sb("s1", [128, 256], F32)
    attnT = [ph.sb(f"attnT{jh}", [128, 256], BF16) for jh in range(2)]
    vin = [ph.sb(f"vin{j}", [128, 2048], BF16) for j in range(4)]
    x1t = [ph.sb(f"x1t{j}", [128, D], F32) for j in range(1)]
    pqk = ph.ps("pqk", [128, 512], F32)
    ptB = ph.ps("ptB", [128, 128], BF16)
    pg = [ph.ps(f"pg{i}", [128, 512], F32) for i in range(2)]
    ps_s = pqk[:, 0:256]
    pe = [ph.ps(f"pe{i}", [128, 256], F32) for i in range(2)]
    po = ph.ps("po", [128, D], F32)

    ptB2 = pg[1].bitcast(BF16)[:, 0:128]

    def prep(p):
        P = p % 2
        nT = nT2[P]
        for jj in range(4):
            t = 4 * p + jj
            b = jj % 2
            s8 = t % 8
            XG, YB, RS = ("xg", b), ("yb", b), ("rs", s8)
            NRM, NRMB = ("nrm", 0), ("nrmb", b)
            ph.dma("sync", lambda t=t, b=b: nc.sync.dma_start(
                out=xg[b][:], in_=x0[128 * t:128 * (t + 1), :]), writes=[XG])
            ln1.emit(b, xg[b], XG, yb[b], YB)
            ln2.emit(b, yb[b], YB, nrm[0], NRM)
            ph.step("vector", lambda b=b, s8=s8: nc.vector.tensor_tensor(
                rsum[s8][:], xg[b][:], yb[b][:], ALU.add),
                reads=[XG, YB], writes=[RS])
            ph.step("vector", lambda: nc.vector.tensor_tensor(
                nrm[0][:], nrm[0][:], st["gbc"][:], ALU.mult),
                reads=[NRM, "gbc"], writes=[NRM])
            ph.step("vector", lambda b=b: nc.vector.tensor_tensor(
                nrmb[b][:], nrm[0][:], st["bbc"][:], ALU.add),
                reads=[NRM, "bbc"], writes=[NRMB])
            for dc in range(8):
                pt = ptB if dc % 2 == 0 else ptB2
                PTK = "ptB" if dc % 2 == 0 else ("pg", 1)
                ph.step("tensor", lambda b=b, dc=dc, jj=jj, pt=pt: nc.tensor.transpose(
                    pt[:], nrmb[b][:, 128 * dc:128 * (dc + 1)], st["idb"][:]),
                    reads=[NRMB, "idb"], writes=[PTK])
                if dc % 2 == 0:
                    ph.step("scalar", lambda dc=dc, jj=jj, nT=nT, pt=pt: nc.scalar.copy(
                        nT[:, 512 * dc + 128 * jj:512 * dc + 128 * (jj + 1)], pt[:]),
                        reads=[PTK], writes=[("nT", P, jj, dc)])
                else:
                    ph.step("vector", lambda dc=dc, jj=jj, nT=nT, pt=pt: nc.vector.tensor_copy(
                        nT[:, 512 * dc + 128 * jj:512 * dc + 128 * (jj + 1)], pt[:]),
                        reads=[PTK], writes=[("nT", P, jj, dc)])

    prep(0)
    for p in range(8):
        P = p % 2
        nT = nT2[P]
        nt_toks = lambda dc, P=P: [("nT", P, jj, dc) for jj in range(4)]
        for jj in range(4):
            t = 4 * p + jj
            vs = t % 4
            ph.dma("gpsimd", lambda t=t, vs=vs: nc.gpsimd.dma_start(
                out=vin[vs][:], in_=v0[128 * t:128 * (t + 1), :]),
                writes=[("vin", vs)])
        for dc in range(8):
            ph.step("tensor", lambda dc=dc, nT=nT: nc.tensor.matmul(
                pqk[:], st["Wqk_fm"][:, 128 * dc:128 * (dc + 1)],
                nT[:, 512 * dc:512 * (dc + 1)], start=(dc == 0), stop=(dc == 7)),
                reads=nt_toks(dc) + ["Wqk_fm"], writes=["pqk"])
        ph.step("scalar", lambda: nc.scalar.activation(qk[:], pqk[:], AF.Silu, bias=st["bqkT"][:]),
                reads=["pqk", "bqkT"], writes=["qk"])
        ph.step("vector", lambda: nc.vector.tensor_scalar(qqb[:], qk[:], st["gq"][:, 0:1],
                                                          st["gq"][:, 1:2], ALU.mult, ALU.add),
                reads=["qk", "gq"], writes=["qqb"])
        ph.step("vector", lambda: nc.vector.tensor_scalar(qkb[:], qk[:], st["gbT"][:, 2:3],
                                                          st["gbT"][:, 6:7], ALU.mult, ALU.add),
                reads=["qk", "gbT"], writes=["qkb"])
        ph.step("vector", lambda: nc.vector.tensor_scalar(lqb[:], qk[:], st["gbT"][:, 1:2],
                                                          st["gbT"][:, 5:6], ALU.mult, ALU.add),
                reads=["qk", "gbT"], writes=["lqb"])
        for hc in range(16):
            pgb = pg[hc % 2]
            PG = ("pg", hc % 2)
            for dc in range(8):
                ph.step("tensor", lambda hc=hc, dc=dc, pgb=pgb, nT=nT: nc.tensor.matmul(
                    pgb[:], st["Whg"][:, 2048 * dc + 128 * hc:2048 * dc + 128 * (hc + 1)],
                    nT[:, 512 * dc:512 * (dc + 1)], start=(dc == 0), stop=(dc == 7)),
                    reads=nt_toks(dc) + ["Whg"], writes=[PG])
            ph.step("scalar", lambda hc=hc, pgb=pgb: nc.scalar.activation(
                gate[:, 512 * hc:512 * (hc + 1)], pgb[:], AF.Silu, bias=st["bhgT"][:, hc:hc + 1]),
                reads=[PG, "bhgT"], writes=[("gate", hc)])
        if p + 1 < 8:
            prep(p + 1)
        for gg in range(2):
            g = 2 * p + gg
            qoff = 256 * gg
            for jh in range(2):
                ph.step("tensor", lambda jh=jh, qoff=qoff: nc.tensor.matmul(
                    ps_s, qkb[:, qoff + 128 * jh:qoff + 128 * (jh + 1)],
                    qqb[:, qoff:qoff + 256], start=True, stop=True),
                    reads=["qkb", "qqb"], writes=["pqk"])
                ph.step("vector", lambda jh=jh: nc.vector.tensor_tensor(
                    s1[:], ps_s, st["biasT"][:, 256 * jh:256 * (jh + 1)], ALU.add),
                    reads=["pqk", "biasT"], writes=["s1"])
                ph.step("vector", lambda: nc.vector.tensor_scalar(s1[:], s1[:], 0.0, None, ALU.max),
                        reads=["s1"], writes=["s1"])
                ph.step("vector", lambda jh=jh: nc.vector.tensor_tensor(
                    attnT[jh][:], s1[:], s1[:], ALU.mult),
                    reads=["s1"], writes=[("attnT", jh)])
            for ec in range(16):
                peb = pe[ec % 2]
                PE_ = ("pe", ec % 2)
                for j in range(2):
                    vs = (4 * p + 2 * gg + j) % 4
                    ph.step("tensor", lambda ec=ec, j=j, vs=vs, peb=peb: nc.tensor.matmul(
                        peb[:], vin[vs][:, 128 * ec:128 * (ec + 1)], attnT[j][:],
                        start=(j == 0), stop=False),
                        reads=[("vin", vs), ("attnT", j)], writes=[PE_])
                ph.step("tensor", lambda ec=ec, qoff=qoff, peb=peb: nc.tensor.matmul(
                    peb[:], st["linkv"][:, 128 * ec:128 * (ec + 1)], lqb[:, qoff:qoff + 256],
                    start=False, stop=True),
                    reads=["linkv", "lqb"], writes=[PE_])
                ph.step("vector", lambda ec=ec, qoff=qoff, peb=peb: nc.vector.tensor_tensor(
                    gate[:, 512 * ec + qoff:512 * ec + qoff + 256], peb[:],
                    gate[:, 512 * ec + qoff:512 * ec + qoff + 256], ALU.mult),
                    reads=[PE_, ("gate", ec)], writes=[("gated", ec, gg)])
            for j in range(2):
                jj = 2 * gg + j
                t = 4 * p + jj
                for ec in range(16):
                    for h in range(2):
                        ph.step("tensor", lambda ec=ec, h=h, qoff=qoff, j=j: nc.tensor.matmul(
                            po[:, 512 * h:512 * (h + 1)],
                            gate[:, 512 * ec + qoff + 128 * j:512 * ec + qoff + 128 * (j + 1)],
                            st["Wo_sb"][:, D * ec + 512 * h:D * ec + 512 * (h + 1)],
                            start=(ec == 0), stop=False),
                            reads=[("gated", ec, gg), "Wo_sb"], writes=[("po", h)])
                for h in range(2):
                    ph.step("tensor", lambda h=h: nc.tensor.matmul(
                        po[:, 512 * h:512 * (h + 1)], st["ones_b"][:],
                        st["bo_row"][0:1, 512 * h:512 * (h + 1)], start=False, stop=True),
                        reads=["ones_b", "bo_row"], writes=[("po", h)])
                xb = 0
                s8 = t % 8
                ph.step("vector", lambda s8=s8, xb=xb: nc.vector.tensor_tensor(
                    x1t[xb][:], po[:], rsum[s8][:], ALU.add),
                    reads=[("po", 0), ("po", 1), ("rs", s8)], writes=[("x1t", xb)])
                ph.dma("sync", lambda t=t, xb=xb: nc.sync.dma_start(
                    out=x1[128 * t:128 * (t + 1), :], in_=x1t[xb][:]),
                    reads=[("x1t", xb)], writes=[("x1h", t)])



def emit_C1(ph, reg):
    nc, prog = ph.nc, ph.prog
    st = prog.st
    x1 = reg.get("x1", *STREAMS["x1"])
    cc1_in = reg.get("cc1_in", (9, 1024), F32)
    cc1_out = reg.get("cc1_out", (9, 1024), F32)
    sw1 = reg.get("L0_ffn_sw1", (D, D), F32)
    sw2 = reg.get("L0_ffn_sw2", (D, D), F32)
    sw3 = reg.get("L0_ffn_sw3", (D, D), F32)
    wg = reg.get("L0_ffn_w_gate", (8, D), F32)
    eb = reg.get("L0_ffn_expert_bias", (1, 8), F32)
    identb = reg.get("ident_bf", (128, 128), BF16)
    maskin = reg.get("mask_core", (8, 1), F32)
    slotin = reg.get("slot_onehot", (1, 8), F32)
    posin = reg.get("posbase", (1, 1), F32)
    t8in = reg.get("T8", (8, 8), F32)

    st["idbC"] = prog.sb("idbC", [128, 128], BF16)
    st["sw1T"] = prog.sb("sw1T", [128, 8 * D], BF16)
    st["sw2T"] = prog.sb("sw2T", [128, 8 * D], BF16)
    st["sw3T"] = prog.sb("sw3T", [128, 8 * D], BF16)
    st["wgT"] = prog.sb("wgT", [128, 64], BF16)
    st["y2T"] = prog.sb("y2T", [128, 8 * T], BF16)
    st["ebbem"] = prog.sb("ebbem", [128, 256], F32)
    st["oh0"] = prog.sb("oh0", [128, 256], F32)
    st["oh1"] = prog.sb("oh1", [128, 256], F32)
    st["w0"] = prog.sb("w0", [128, 32], F32)
    st["w1s"] = prog.sb("w1s", [128, 32], F32)
    st["onesf"] = prog.sb("onesfC", [1, 128], F32)
    st["onescf"] = prog.sb("onescf", [128, 1], F32)
    st["ones32"] = prog.sb("ones32", [128, 32], F32)
    st["maskc"] = prog.sb("maskc", [8, 1], F32)
    st["posb"] = prog.sb("posb", [1, 1], F32)
    st["T8sb"] = prog.sb("T8sb", [8, 8], F32)
    st["e0i"] = prog.sb("e0i", [128, 32], F32)
    st["e1i"] = prog.sb("e1i", [128, 32], F32)
    st["slotrow"] = prog.sb("slotrow", [1, 8], F32)
    st["m8b"] = prog.sb("m8b", [128, 8], F32)

    ln = LNHelper(ph, "lnC", 2)
    xt = [ph.sb(f"xt{i}", [128, D], F32) for i in range(2)]
    y2 = [ph.sb(f"y2_{i}", [128, D], F32) for i in range(2)]
    y2b = [ph.sb(f"y2b{i}", [128, D], BF16) for i in range(2)]
    logits = ph.sb("logits", [128, 256], F32)
    X8p = ph.sb("X8p", [8, D], F32)
    wrow = [ph.sb(f"wrow{i}", [128, D], BF16) for i in range(2)]
    wgrow = ph.sb("wgrow", [8, D], BF16)
    ebrow = ph.sb("ebrow", [1, 8], F32)
    ebb8 = ph.sb("ebb8", [128, 8], F32)
    ptr = [ph.ps(f"ptr{i}", [128, 128], BF16) for i in range(2)]
    plog = [ph.ps(f"plog{i}", [128, 8], F32) for i in range(2)]
    ptc = ph.ps("ptc", [128, 128], BF16)
    ptg = ph.ps("ptg", [128, 8], BF16)
    pmeb = ph.ps("pmeb", [128, 8], F32)

    ph.dma("sync", lambda: nc.sync.dma_start(out=st["idbC"][:], in_=identb), writes=["idb"])
    ph.dma("sync", lambda: nc.sync.dma_start(out=st["maskc"][:], in_=maskin), writes=["maskc"])
    ph.dma("sync", lambda: nc.sync.dma_start(out=st["slotrow"][:], in_=slotin), writes=["slotrow"])
    ph.dma("sync", lambda: nc.sync.dma_start(out=st["posb"][:], in_=posin), writes=["posb"])
    ph.dma("sync", lambda: nc.sync.dma_start(out=st["T8sb"][:], in_=t8in), writes=["T8sb"])
    ph.step("gpsimd", lambda: nc.gpsimd.memset(st["onesf"][:], 1.0), writes=["onesf"])
    ph.step("gpsimd", lambda: nc.gpsimd.memset(st["onescf"][:], 1.0), writes=["onescf"])
    ph.step("gpsimd", lambda: nc.gpsimd.memset(st["ones32"][:], 1.0), writes=["ones32"])
    ph.dma("gpsimd", lambda: nc.gpsimd.dma_start(out=wgrow[:], in_=wg), writes=["wgrow"])
    for dc in range(8):
        ph.step("tensor", lambda dc=dc: nc.tensor.transpose(
            ptg[:], wgrow[:, 128 * dc:128 * (dc + 1)], st["idbC"][0:8, 0:8]),
            reads=["wgrow", "idb"], writes=["ptg"])
        ph.step("vector", lambda dc=dc: nc.vector.tensor_copy(st["wgT"][:, 8 * dc:8 * (dc + 1)], ptg[:]),
                reads=["ptg"], writes=["wgT"])
    ph.step("tensor", lambda: nc.tensor.matmul(pmeb[:], st["onesf"][:], st["slotrow"][:],
                                               start=True, stop=True),
            reads=["slotrow", "onesf"], writes=["pmeb"])
    ph.step("vector", lambda: nc.vector.tensor_copy(st["m8b"][:], pmeb[:]), reads=["pmeb"], writes=["m8b"])
    ph.dma("sync", lambda: nc.sync.dma_start(out=ebrow[:], in_=eb), writes=["ebrow"])
    ph.step("tensor", lambda: nc.tensor.matmul(pmeb[:], st["onesf"][:], ebrow[:], start=True, stop=True),
            reads=["ebrow", "onesf", "m8b"], writes=["pmeb"])
    ph.step("vector", lambda: nc.vector.tensor_copy(ebb8[:], pmeb[:]), reads=["pmeb"], writes=["ebb8"])
    for e in range(8):
        ph.step("vector", lambda e=e: nc.vector.tensor_scalar(
            st["ebbem"][:, 32 * e:32 * (e + 1)], st["ones32"][:], ebb8[:, e:e + 1], None, ALU.mult),
            reads=["ones32", "ebb8"], writes=["ebbem"])

    sw_srcs = [(sw1, "sw1T"), (sw2, "sw2T"), (sw3, "sw3T")]

    def sw_chunk(k):
        w, dst = sw_srcs[k // 8]
        sc = k % 8
        b = sc % 2
        ph.dma("gpsimd", lambda w=w, sc=sc, b=b: nc.gpsimd.dma_start(
            out=wrow[b][:], in_=w[128 * sc:128 * (sc + 1), :]), writes=[("wrow", b)])
        for dc in range(8):
            ph.step("tensor", lambda sc=sc, dc=dc, b=b: nc.tensor.transpose(
                ptc[:], wrow[b][:, 128 * dc:128 * (dc + 1)], st["idbC"][:]),
                reads=[("wrow", b), "idb"], writes=["ptc"])
            ph.step("vector", lambda dst=dst, sc=sc, dc=dc: nc.vector.tensor_copy(
                st[dst][:, D * dc + 128 * sc:D * dc + 128 * (sc + 1)], ptc[:]),
                reads=["ptc"], writes=[dst])

    for t in range(NT):
        b = t % 2
        XT, Y2, Y2B = ("xt", b), ("y2", b), ("y2b", b)
        ph.dma("sync", lambda t=t, b=b: nc.sync.dma_start(
            out=xt[b][:], in_=x1[128 * t:128 * (t + 1), :]), writes=[XT])
        ln.emit(b, xt[b], XT, y2[b], Y2)
        if t == 0:
            ph.step("vector", lambda b=b: nc.vector.tensor_scalar(
                X8p[:], y2[b][0:8, :], st["maskc"][:], None, ALU.mult),
                reads=[Y2, "maskc"], writes=["X8p"])
        ph.step("vector", lambda b=b: nc.vector.tensor_copy(y2b[b][:], y2[b][:]),
                reads=[Y2], writes=[Y2B])
        for dc in range(8):
            pb = dc % 2
            ph.step("tensor", lambda b=b, dc=dc, pb=pb: nc.tensor.transpose(
                ptr[pb][:], y2b[b][:, 128 * dc:128 * (dc + 1)], st["idbC"][:]),
                reads=[Y2B, "idb"], writes=[("ptr", pb)])
            ph.step("scalar", lambda t=t, dc=dc, pb=pb: nc.scalar.copy(
                st["y2T"][:, T * dc + 128 * t:T * dc + 128 * (t + 1)], ptr[pb][:]),
                reads=[("ptr", pb)], writes=[("y2T", t, dc)])
        pl = plog[b]
        PL = ("plog", b)
        for dc in range(8):
            ph.step("tensor", lambda t=t, dc=dc, pl=pl: nc.tensor.matmul(
                pl[:], st["y2T"][:, T * dc + 128 * t:T * dc + 128 * (t + 1)],
                st["wgT"][:, 8 * dc:8 * (dc + 1)], start=(dc == 0), stop=(dc == 7)),
                reads=[("y2T", t, dc), "wgT"], writes=[PL])
        ph.step("vector", lambda t=t, pl=pl: nc.vector.tensor_copy(logits[:, 8 * t:8 * (t + 1)], pl[:]),
                reads=[PL], writes=["logits"])
        if t < 24:
            sw_chunk(t)

    nm = ph.sb("nm", [128, 32], F32)
    les = ph.sb("les", [128, 256], F32)
    scm = ph.sb("scm", [128, 256], F32)
    sbm = ph.sb("sbm", [128, 256], F32)
    gs = ph.sb("gs", [128, 128], F32)
    rk4 = ph.sb("rk4", [128, 128], F32)
    keep = ph.sb("keep", [128, 128], F32)
    big1 = ph.sb("big1", [128, 128], F32)
    me = ph.sb("me", [128, 256], F32)
    rk8 = ph.sb("rk8", [128, 256], F32)
    ssum = ph.sb("ssum", [128, 32], F32)
    tmpc = ph.sb("tmpc", [128, 32], F32)
    cntpp = ph.sb("cntpp", [128, 8], F32)
    crow = ph.sb("crow", [1, 1024], F32)
    pcnt = ph.ps("pcnt", [1, 8], F32)

    lg3 = logits.rearrange("p (t e) -> p t e", e=8)
    ph.step("vector", lambda: nc.vector.tensor_reduce(nm[:], lg3, AX.X, ALU.max, negate=True),
            reads=["logits"], writes=["nm"])
    for e in range(8):
        ph.step("vector", lambda e=e: nc.vector.tensor_tensor(
            les[:, 32 * e:32 * (e + 1)], lg3[:, :, e], nm[:], ALU.add),
            reads=["logits", "nm"], writes=["les"])
    ph.step("scalar", lambda: nc.scalar.activation(les[:], les[:], AF.Exp),
            reads=["les"], writes=["les"])
    ph.step("vector", lambda: nc.vector.tensor_tensor(ssum[:], les[:, 0:32], les[:, 32:64], ALU.add),
            reads=["les"], writes=["ssum"])
    for e in range(2, 8):
        ph.step("vector", lambda e=e: nc.vector.tensor_tensor(
            ssum[:], ssum[:], les[:, 32 * e:32 * (e + 1)], ALU.add),
            reads=["les", "ssum"], writes=["ssum"])
    ph.step("vector", lambda: nc.vector.reciprocal(tmpc[:], ssum[:]), reads=["ssum"], writes=["tmpc"])
    for e in range(8):
        ph.step("vector", lambda e=e: nc.vector.tensor_tensor(
            scm[:, 32 * e:32 * (e + 1)], les[:, 32 * e:32 * (e + 1)], tmpc[:], ALU.mult),
            reads=["les", "tmpc"], writes=["scm"])
    ph.step("vector", lambda: nc.vector.tensor_tensor(sbm[:], scm[:], st["ebbem"][:], ALU.add),
            reads=["scm", "ebbem"], writes=["sbm"])
    sb3 = sbm.rearrange("p (j u t) -> p j u t", j=4, u=2)
    for j in range(4):
        ph.step("vector", lambda j=j: nc.vector.tensor_tensor(
            gs[:, 32 * j:32 * (j + 1)], sb3[:, j, 0, :], sb3[:, j, 1, :], ALU.add),
            reads=["sbm"], writes=["gs"])
    for j in range(4):
        ph.step("gpsimd", lambda j=j: nc.gpsimd.memset(rk4[:, 32 * j:32 * (j + 1)], float(3 - j)),
                writes=["rk4"])
    for j1 in range(4):
        for j2 in range(j1 + 1, 4):
            ph.step("vector", lambda j1=j1, j2=j2: nc.vector.tensor_tensor(
                tmpc[:], gs[:, 32 * j1:32 * (j1 + 1)], gs[:, 32 * j2:32 * (j2 + 1)], ALU.is_ge),
                reads=["gs"], writes=["tmpc"])
            ph.step("vector", lambda j1=j1: nc.vector.tensor_tensor(
                rk4[:, 32 * j1:32 * (j1 + 1)], rk4[:, 32 * j1:32 * (j1 + 1)], tmpc[:], ALU.subtract),
                reads=["rk4", "tmpc"], writes=["rk4"])
            ph.step("vector", lambda j2=j2: nc.vector.tensor_tensor(
                rk4[:, 32 * j2:32 * (j2 + 1)], rk4[:, 32 * j2:32 * (j2 + 1)], tmpc[:], ALU.add),
                reads=["rk4", "tmpc"], writes=["rk4"])
    ph.step("vector", lambda: nc.vector.tensor_scalar(keep[:], rk4[:], 1.5, None, ALU.is_le),
            reads=["rk4"], writes=["keep"])
    ph.step("vector", lambda: nc.vector.tensor_scalar(big1[:], keep[:], 1e30, -1e30, ALU.mult, ALU.add),
            reads=["keep"], writes=["big1"])
    for e in range(8):
        j = e // 2
        ph.step("vector", lambda e=e, j=j: nc.vector.tensor_tensor(
            me[:, 32 * e:32 * (e + 1)], sbm[:, 32 * e:32 * (e + 1)], keep[:, 32 * j:32 * (j + 1)], ALU.mult),
            reads=["sbm", "keep"], writes=["me"])
        ph.step("vector", lambda e=e, j=j: nc.vector.tensor_tensor(
            me[:, 32 * e:32 * (e + 1)], me[:, 32 * e:32 * (e + 1)], big1[:, 32 * j:32 * (j + 1)], ALU.add),
            reads=["me", "big1"], writes=["me"])
    for e in range(8):
        ph.step("gpsimd", lambda e=e: nc.gpsimd.memset(rk8[:, 32 * e:32 * (e + 1)], float(7 - e)),
                writes=["rk8"])
    for a in range(8):
        for b2 in range(a + 1, 8):
            ph.step("vector", lambda a=a, b2=b2: nc.vector.tensor_tensor(
                tmpc[:], me[:, 32 * a:32 * (a + 1)], me[:, 32 * b2:32 * (b2 + 1)], ALU.is_ge),
                reads=["me"], writes=["tmpc"])
            ph.step("vector", lambda a=a: nc.vector.tensor_tensor(
                rk8[:, 32 * a:32 * (a + 1)], rk8[:, 32 * a:32 * (a + 1)], tmpc[:], ALU.subtract),
                reads=["rk8", "tmpc"], writes=["rk8"])
            ph.step("vector", lambda b2=b2: nc.vector.tensor_tensor(
                rk8[:, 32 * b2:32 * (b2 + 1)], rk8[:, 32 * b2:32 * (b2 + 1)], tmpc[:], ALU.add),
                reads=["rk8", "tmpc"], writes=["rk8"])
    ph.step("vector", lambda: nc.vector.tensor_scalar(st["oh0"][:], rk8[:], 0.0, None, ALU.is_equal),
            reads=["rk8"], writes=["oh0"])
    ph.step("vector", lambda: nc.vector.tensor_scalar(st["oh1"][:], rk8[:], 1.0, None, ALU.is_equal),
            reads=["rk8"], writes=["oh1"])
    for k, (ohn, wn) in enumerate([("oh0", "w0"), ("oh1", "w1s")]):
        ph.step("vector", lambda ohn=ohn: nc.vector.tensor_tensor(les[:], st[ohn][:], scm[:], ALU.mult),
                reads=[ohn, "scm"], writes=["les"])
        ph.step("vector", lambda wn=wn: nc.vector.tensor_tensor(
            st[wn][:], les[:, 0:32], les[:, 32:64], ALU.add),
            reads=["les"], writes=[wn])
        for e in range(2, 8):
            ph.step("vector", lambda wn=wn, e=e: nc.vector.tensor_tensor(
                st[wn][:], st[wn][:], les[:, 32 * e:32 * (e + 1)], ALU.add),
                reads=["les", wn], writes=[wn])
    e0i = st["e0i"]
    e1i = st["e1i"]
    for k, (ohn, dst, DST) in enumerate([("oh0", e0i, "e0i"), ("oh1", e1i, "e1i")]):
        ph.step("gpsimd", lambda dst=dst: nc.gpsimd.memset(dst[:], 0.0), writes=[DST])
        for e in range(1, 8):
            ph.step("vector", lambda ohn=ohn, e=e: nc.vector.tensor_scalar(
                tmpc[:], st[ohn][:, 32 * e:32 * (e + 1)], float(e), None, ALU.mult),
                reads=[ohn], writes=["tmpc"])
            ph.step("vector", lambda dst=dst, DST=DST: nc.vector.tensor_tensor(
                dst[:], dst[:], tmpc[:], ALU.add),
                reads=[DST, "tmpc"], writes=[DST])
    ph.step("vector", lambda: nc.vector.tensor_tensor(les[:], st["oh0"][:], st["oh1"][:], ALU.add),
            reads=["oh0", "oh1"], writes=["les"])
    ph.step("vector", lambda: nc.vector.tensor_reduce(
        cntpp[:], les.rearrange("p (e t) -> p e t", e=8), AX.X, ALU.add),
        reads=["les"], writes=["cntpp"])
    ph.step("tensor", lambda: nc.tensor.matmul(pcnt[:], st["onescf"][:], cntpp[:], start=True, stop=True),
            reads=["onescf", "cntpp"], writes=["pcnt"])
    ph.step("gpsimd", lambda: nc.gpsimd.memset(crow[:], 0.0), writes=["crow"])
    ph.step("vector", lambda: nc.vector.tensor_copy(crow[0:1, 0:8], pcnt[:]),
            reads=["pcnt"], writes=["crow"])
    ph.dma("gpsimd", lambda: nc.gpsimd.dma_start(out=cc1_in[0:8, :], in_=X8p[:]),
           reads=["X8p"], writes=["cc1h"])
    ph.dma("gpsimd", lambda: nc.gpsimd.dma_start(out=cc1_in[8:9, :], in_=crow[:]),
           reads=["crow"], writes=["cc1h"])
    st["cc1_sem"] = ph.collective("cc1", lambda: nc.gpsimd.collective_compute(
        "AllReduce", ALU.add, replica_groups=[list(range(N_CORES))],
        ins=[cc1_in.opt()], outs=[cc1_out.opt()]),
        reads=["cc1h"], writes=["cc1o"])



def emit_C1b(ph, reg):
    nc, prog = ph.nc, ph.prog
    st = prog.st
    x1 = reg.get("x1", *STREAMS["x1"])
    x2 = reg.get("x2", *STREAMS["x2"])
    cc1_out = reg.get("cc1_out", (9, 1024), F32)
    w1d = reg.get("L0_ffn_w1", (8 * D, 512), F32)
    w3d = reg.get("L0_ffn_w3", (8 * D, 512), F32)

    st["Uflat"] = prog.sb("Uflat", [64, 512], BF16)
    st["T2"] = prog.sb("T2", [8, 4096], BF16)
    st["X8T"] = prog.sb("X8T", [128, 64], BF16)

    hh = ph.sb("hh", [128, 8 * 512], BF16)
    hs = ph.sb("hs", [128, 512], F32)
    x1t = [ph.sb(f"x1t{i}", [128, D], F32) for i in range(2)]
    x2t = [ph.sb(f"x2t{i}", [128, D], F32) for i in range(2)]
    ph1 = [ph.ps(f"ph1_{i}", [128, 512], F32) for i in range(2)]
    ph3 = [ph.ps(f"ph3_{i}", [128, 512], F32) for i in range(2)]
    pso = ph.ps("pso", [128, D], F32)
    X8sb = ph.sb("X8sb", [8, D], BF16)
    wab = [ph.sb(f"wab{i}", [128, 2 * 8 * 512], BF16) for i in range(2)]
    uttmp = ph.sb("uttmp", [8, 512], BF16)
    put = ph.ps("put", [8, 512], F32)
    ptx = ph.ps("ptx", [128, 8], BF16)

    def emit_block(bi):
        for sc in range(8):
            p1, p3 = ph1[sc % 2], ph3[sc % 2]
            P1, P3 = ("ph1", sc % 2), ("ph3", sc % 2)
            for dc in range(8):
                ph.step("tensor", lambda bi=bi, sc=sc, dc=dc, p1=p1: nc.tensor.matmul(
                    p1[:], st["sw1T"][:, D * dc + 128 * sc:D * dc + 128 * (sc + 1)],
                    st["y2T"][:, T * dc + 512 * bi:T * dc + 512 * (bi + 1)],
                    start=(dc == 0), stop=(dc == 7)),
                    reads=["sw1T"], writes=[P1])
            for dc in range(8):
                ph.step("tensor", lambda bi=bi, sc=sc, dc=dc, p3=p3: nc.tensor.matmul(
                    p3[:], st["sw3T"][:, D * dc + 128 * sc:D * dc + 128 * (sc + 1)],
                    st["y2T"][:, T * dc + 512 * bi:T * dc + 512 * (bi + 1)],
                    start=(dc == 0), stop=(dc == 7)),
                    reads=["sw3T"], writes=[P3])
            ph.step("scalar", lambda p1=p1: nc.scalar.activation(hs[:], p1[:], AF.Silu),
                    reads=[P1], writes=["hs"])
            ph.step("vector", lambda sc=sc, p3=p3: nc.vector.tensor_tensor(
                hh[:, 512 * sc:512 * (sc + 1)], hs[:], p3[:], ALU.mult),
                reads=["hs", P3], writes=[("hh", sc)])
        for tc in range(4):
            b = tc % 2
            tglob = 4 * bi + tc
            ph.dma("sync", lambda tglob=tglob, b=b: nc.sync.dma_start(
                out=x1t[b][:], in_=x1[128 * tglob:128 * (tglob + 1), :]), writes=[("x1t", b)])
            for sc in range(8):
                for oh in range(2):
                    ph.step("tensor", lambda sc=sc, tc=tc, oh=oh: nc.tensor.matmul(
                        pso[:, 512 * oh:512 * (oh + 1)],
                        hh[:, 512 * sc + 128 * tc:512 * sc + 128 * (tc + 1)],
                        st["sw2T"][:, D * sc + 512 * oh:D * sc + 512 * (oh + 1)],
                        start=(sc == 0), stop=(sc == 7)),
                        reads=[("hh", sc), "sw2T"], writes=[("pso", oh)])
            ph.step("vector", lambda b=b: nc.vector.tensor_tensor(
                x2t[b][:], pso[:], x1t[b][:], ALU.add),
                reads=[("pso", 0), ("pso", 1), ("x1t", b)], writes=[("x2t", b)])
            ph.dma("gpsimd", lambda tglob=tglob, b=b: nc.gpsimd.dma_start(
                out=x2[128 * tglob:128 * (tglob + 1), :], in_=x2t[b][:]),
                reads=[("x2t", b)], writes=[("x2h", tglob)])

    for bi in range(4):
        emit_block(bi)

    ccw = [(st["cc1_sem"], 1)]
    ph.dma("gpsimd", lambda: nc.gpsimd.dma_start(out=X8sb[:], in_=cc1_out[0:8, :]),
           writes=["X8sb"], extra_waits=ccw)
    for dc in range(8):
        ph.step("tensor", lambda dc=dc: nc.tensor.transpose(
            ptx[:], X8sb[:, 128 * dc:128 * (dc + 1)], st["idbC"][0:8, 0:8]),
            reads=["X8sb"], writes=["ptx"])
        ph.step("vector", lambda dc=dc: nc.vector.tensor_copy(st["X8T"][:, 8 * dc:8 * (dc + 1)], ptx[:]),
                reads=["ptx"], writes=["X8T"])
    for g in range(8):
        b = g % 2
        ph.dma("gpsimd", lambda g=g, b=b: nc.gpsimd.dma_start(
            out=wab[b][:, 0:4096].rearrange("p (c h) -> p c h", c=8),
            in_=w1d[1024 * g:1024 * (g + 1), :].rearrange(
                "(c p) h -> p c h", p=128)), writes=[("wab", b)])
        ph.dma("gpsimd", lambda g=g, b=b: nc.gpsimd.dma_start(
            out=wab[b][:, 4096:8192].rearrange("p (c h) -> p c h", c=8),
            in_=w3d[1024 * g:1024 * (g + 1), :].rearrange(
                "(c p) h -> p c h", p=128)), writes=[("wab", b)])
        for half, dst in ((0, "Uflat"), (1, "T2")):
            for dc in range(8):
                ph.step("tensor", lambda g=g, b=b, half=half, dc=dc: nc.tensor.matmul(
                    put[:], st["X8T"][:, 8 * dc:8 * (dc + 1)],
                    wab[b][:, 4096 * half + 512 * dc:4096 * half + 512 * (dc + 1)],
                    start=(dc == 0), stop=(dc == 7)),
                    reads=[("wab", b), "X8T"], writes=["put"])
            ph.step("vector", lambda: nc.vector.tensor_copy(uttmp[:], put[:]),
                    reads=["put"], writes=["uttmp"])
            if dst == "Uflat":
                ph.dma("gpsimd", lambda g=g: nc.gpsimd.dma_start(
                    out=st["Uflat"][8 * g:8 * (g + 1), :], in_=uttmp[:]),
                    reads=["uttmp"], writes=["Uflat"])
            else:
                ph.step("vector", lambda g=g: nc.vector.tensor_copy(
                    st["T2"][:, 512 * g:512 * (g + 1)], uttmp[:]),
                    reads=["uttmp"], writes=["T2"])

    for bi in range(4, 8):
        emit_block(bi)



def emit_C3(ph, reg):
    nc, prog = ph.nc, ph.prog
    st = prog.st
    x2 = reg.get("x2", *STREAMS["x2"])
    cc1_out = reg.get("cc1_out", (9, 1024), F32)
    cc2_in = reg.get("cc2_in", (64, 512), F32)
    cc2_out = reg.get("cc2_out", (64, 512), F32)
    w2d = reg.get("L0_ffn_w2", (8 * 512, D), F32)

    cnt8 = ph.sb("cnt8", [8, 1], F32)
    cum = ph.sb("cum", [1, 8], F32)
    thrb = ph.sb("thrb", [128, 8], F32)
    iota64 = ph.sb("iota64", [128, 64], F32)
    pos = ph.sb("pos", [128, 32], F32)
    geb = ph.sb("geb", [128, 32], F32)
    gidx = ph.sb("gidx", [128, 32], F32)
    ccol = [ph.sb(f"ccol{k}", [128, 32], F32) for k in range(2)]
    M1 = [ph.sb(f"M1_{i}", [128, 64], BF16) for i in range(3)]
    M1T = [ph.sb(f"M1T_{i}", [64, 128], BF16) for i in range(3)]
    ss = [ph.sb(f"ss{i}", [128, 512], F32) for i in range(3)]
    Ssb = [ph.sb(f"Ssb{i}", [128, 512], BF16) for i in range(3)]
    Psb = ph.sb("Psb", [64, 512], F32)
    P2 = ph.sb("P2", [8, 4096], F32)
    A2 = ph.sb("A2", [8, 4096], BF16)
    A2T = ph.sb("A2T", [128, 256], BF16)
    w2g = [ph.sb(f"w2g{i}", [128, 4 * D], BF16) for i in range(2)]
    corr = ph.sb("corr", [8, D], F32)
    px8 = ph.sb("px8", [8, D], F32)
    pmisc = ph.ps("pmisc", [128, 8], F32)
    pG = [ph.ps(f"pG{i}", [128, 512], F32) for i in range(2)]
    pM = [ph.ps(f"pM{i}", [64, 128], BF16) for i in range(2)]
    pP = ph.ps("pP", [64, 512], F32)
    pcor = ph.ps("pcor", [8, D], F32)

    ccw = [(st["cc1_sem"], 1)]
    ph.dma("sync", lambda: nc.sync.dma_start(
        out=cnt8[:], in_=cc1_out[8:9, 0:8].rearrange("a b -> b a")),
        writes=["cnt8"], extra_waits=ccw)
    ph.step("tensor", lambda: nc.tensor.matmul(pmisc[0:1, 0:8], cnt8[:], st["T8sb"][:],
                                               start=True, stop=True),
            reads=["cnt8", "T8sb"], writes=["pmisc"])
    ph.step("vector", lambda: nc.vector.tensor_scalar(
        cum[:], pmisc[0:1, 0:8], st["posb"][0:1, 0:1], None, ALU.subtract),
        reads=["pmisc", "posb"], writes=["cum"])
    ph.step("tensor", lambda: nc.tensor.matmul(pmisc[:], st["onesf"][:], cum[:], start=True, stop=True),
            reads=["cum", "onesf"], writes=["pmisc"])
    ph.step("vector", lambda: nc.vector.tensor_copy(thrb[:], pmisc[:]), reads=["pmisc"], writes=["thrb"])
    ph.step("gpsimd", lambda: nc.gpsimd.iota(iota64[:], [[1, 64]], base=0, channel_multiplier=0,
                                             allow_small_or_imprecise_dtypes=True),
            writes=["iota64"])
    for k in range(2):
        ph.step("gpsimd", lambda k=k: nc.gpsimd.iota(
            pos[:], [[256, 32]], base=k, channel_multiplier=2,
            allow_small_or_imprecise_dtypes=True), writes=["pos"])
        ph.step("vector", lambda: nc.vector.tensor_scalar(
            gidx[:], pos[:], thrb[:, 0:1], None, ALU.is_ge),
            reads=["pos", "thrb"], writes=["gidx"])
        for e in range(1, 8):
            ph.step("vector", lambda e=e: nc.vector.tensor_scalar(
                geb[:], pos[:], thrb[:, e:e + 1], None, ALU.is_ge),
                reads=["pos", "thrb"], writes=["geb"])
            ph.step("vector", lambda: nc.vector.tensor_tensor(gidx[:], gidx[:], geb[:], ALU.add),
                    reads=["gidx", "geb"], writes=["gidx"])
        ph.step("vector", lambda k=k: nc.vector.tensor_scalar(
            ccol[k][:], gidx[:], 8.0, None, ALU.mult),
            reads=["gidx"], writes=[("ccol", k)])
        ei = st["e0i"] if k == 0 else st["e1i"]
        EI = "e0i" if k == 0 else "e1i"
        ph.step("vector", lambda k=k, ei=ei: nc.vector.tensor_tensor(
            ccol[k][:], ccol[k][:], ei[:], ALU.add),
            reads=[("ccol", k), EI], writes=[("ccol", k)])

    for g in range(2):
        ph.dma("gpsimd", lambda g=g: nc.gpsimd.dma_start(
            out=w2g[g][:, :].rearrange("p (c o) -> p c o", c=4),
            in_=w2d[512 * g:512 * (g + 1), :].rearrange("(c p) o -> p c o", p=128)),
            writes=[("w2g", g)])

    first = True
    it = 0
    for tb in range(32):
        for k in range(2):
            sl = it % 3
            pl = it % 2
            it += 1
            wsrc = st["w0"] if k == 0 else st["w1s"]
            WK = "w0" if k == 0 else "w1s"
            wcol_ap = wsrc[:, tb:tb + 1]
            ph.step("vector", lambda k=k, tb=tb, sl=sl: nc.vector.tensor_scalar(
                M1[sl][:], iota64[:], ccol[k][:, tb:tb + 1], None, ALU.is_equal),
                reads=["iota64", ("ccol", k)], writes=[("M1", sl)])
            ph.step("tensor", lambda sl=sl, pl=pl: nc.tensor.transpose(pM[pl][:], M1[sl][:], st["idbC"][:]),
                    reads=[("M1", sl), "idb"], writes=[("pM", pl)])
            ph.step("vector", lambda sl=sl, pl=pl: nc.vector.tensor_copy(M1T[sl][:], pM[pl][:]),
                    reads=[("pM", pl)], writes=[("M1T", sl)])
            ph.step("tensor", lambda sl=sl, pl=pl: nc.tensor.matmul(
                pG[pl][:], M1T[sl][:], st["Uflat"][:], start=True, stop=True),
                reads=[("M1T", sl), "Uflat"], writes=[("pG", pl)])
            ph.step("scalar", lambda sl=sl, pl=pl, wcol_ap=wcol_ap, WK=WK: nc.scalar.activation(
                ss[sl][:], pG[pl][:], AF.Silu, scale=wcol_ap),
                reads=[("pG", pl), WK], writes=[("ss", sl)])
            ph.step("vector", lambda sl=sl, wcol_ap=wcol_ap, WK=WK: nc.vector.tensor_scalar(
                Ssb[sl][:], ss[sl][:], wcol_ap, None, ALU.mult),
                reads=[("ss", sl), WK], writes=[("Ssb", sl)])
            ph.step("tensor", lambda sl=sl, first=first, it=it: nc.tensor.matmul(
                pP[:], M1[sl][:], Ssb[sl][:], start=first, stop=(it == 64),
                skip_group_check=True),
                reads=[("M1", sl), ("Ssb", sl)], writes=["pP"])
            first = False
    ph.step("vector", lambda: nc.vector.tensor_copy(Psb[:], pP[:]), reads=["pP"], writes=["Psb"])
    ph.dma("gpsimd", lambda: nc.gpsimd.dma_start(out=cc2_in, in_=Psb[:]),
           reads=["Psb"], writes=["cc2h"])
    cc2_sem = ph.collective("cc2", lambda: nc.gpsimd.collective_compute(
        "AllReduce", ALU.add, replica_groups=[list(range(N_CORES))],
        ins=[cc2_in.opt()], outs=[cc2_out.opt()]),
        reads=["cc2h"], writes=["cc2o"])

    ph.dma("gpsimd", lambda: nc.gpsimd.dma_start(
        out=P2.rearrange("e (g h) -> e g h", g=8),
        in_=cc2_out.rearrange("(g e) h -> e g h", e=8)),
        writes=["P2"], extra_waits=[(cc2_sem, 1)])
    ph.step("vector", lambda: nc.vector.tensor_tensor(A2[:], P2[:], st["T2"][:], ALU.mult),
            reads=["P2", "T2"], writes=["A2"])
    patv = pmisc[:].bitcast(BF16)[:, 0:8]
    for c in range(32):
        ph.step("tensor", lambda c=c: nc.tensor.transpose(
            patv, A2[:, 128 * c:128 * (c + 1)], st["idbC"][0:8, 0:8]),
            reads=["A2"], writes=["pmisc"])
        ph.step("vector", lambda c=c: nc.vector.tensor_copy(A2T[:, 8 * c:8 * (c + 1)], patv),
                reads=["pmisc"], writes=["A2T"])
    for g in range(8):
        for hc in range(4):
            c = 4 * g + hc
            for oh2 in range(2):
                ph.step("tensor", lambda g=g, c=c, hc=hc, oh2=oh2: nc.tensor.matmul(
                    pcor[:, 512 * oh2:512 * (oh2 + 1)], A2T[:, 8 * c:8 * (c + 1)],
                    w2g[g % 2][:, D * hc + 512 * oh2:D * hc + 512 * (oh2 + 1)],
                    start=(g == 0 and hc == 0), stop=(g == 7 and hc == 3),
                    skip_group_check=True),
                    reads=["A2T", ("w2g", g % 2)], writes=[("pcor", oh2)])
        if g % 2 == 1 and g < 7:
            for gg in (g + 1, g + 2):
                if gg < 8:
                    ph.dma("gpsimd", lambda gg=gg: nc.gpsimd.dma_start(
                        out=w2g[gg % 2][:, :].rearrange("p (c o) -> p c o", c=4),
                        in_=w2d[512 * gg:512 * (gg + 1), :].rearrange(
                            "(c p) o -> p c o", p=128)),
                        writes=[("w2g", gg % 2)])
    ph.step("vector", lambda: nc.vector.tensor_scalar(corr[:], pcor[0:8, :], st["maskc"][:],
                                                      None, ALU.mult),
            reads=[("pcor", 0), ("pcor", 1), "maskc"], writes=["corr"])
    ph.dma("sync", lambda: nc.sync.dma_start(out=px8[:], in_=x2[0:8, :]), writes=["px8"])
    ph.step("vector", lambda: nc.vector.tensor_tensor(px8[:], px8[:], corr[:], ALU.add),
            reads=["px8", "corr"], writes=["px8"])
    ph.dma("sync", lambda: nc.sync.dma_start(out=x2[0:8, :], in_=px8[:]),
           reads=["px8"], writes=["x2patch"])



HP = 8
QB = 2048


def emit_D0(ph, reg):
    nc, prog = ph.nc, ph.prog
    st = prog.st
    identb = reg.get("ident_bf", (128, 128), BF16)
    grmsrep = reg.get("grms_rep", (1, D), F32)

    st["idbD"] = prog.sb("idbD", [128, 128], BF16)
    st["grb"] = prog.sb("grb", [128, D], F32)
    st["Mb"] = prog.sb("Mb", [128, 1], F32)
    st["onesfD"] = prog.sb("onesfD", [1, 128], F32)
    st["v_all"] = prog.sb("v_all", [128, NT * 1040], BF16)

    ph.dma("sync", lambda: nc.sync.dma_start(out=st["idbD"][:], in_=identb), writes=["idb"])
    ph.step("gpsimd", lambda: nc.gpsimd.memset(st["onesfD"][:], 1.0), writes=["onesf"])
    ph.step("vector", lambda: nc.vector.memset(
        st["v_all"].rearrange("p (t h e) -> p t h e", t=NT, h=16)[:, :, :, 64:65], 1.0),
        writes=["v_ones"])
    grow = ph.sb("grow", [1, D], F32)
    pb = ph.ps("pb", [128, 512], F32)
    ph.dma("sync", lambda: nc.sync.dma_start(out=grow[:], in_=grmsrep), writes=["grow"])
    for h in range(2):
        ph.step("tensor", lambda h=h: nc.tensor.matmul(
            pb[:], st["onesfD"][:], grow[0:1, 512 * h:512 * (h + 1)], start=True, stop=True),
            reads=["grow", "onesf"], writes=["pb"])
        ph.step("vector", lambda h=h: nc.vector.tensor_copy(st["grb"][:, 512 * h:512 * (h + 1)], pb[:]),
                reads=["pb"], writes=["grb"])
    g2 = ph.sb("g2", [1, 64], F32)
    mg = ph.sb("mg", [1, 1], F32)
    pmb = ph.ps("pmb", [128, 1], F32)
    ph.step("scalar", lambda: nc.scalar.activation(g2[:], grow[0:1, 0:64], AF.Square),
            reads=["grow"], writes=["g2"])
    ph.step("vector", lambda: nc.vector.reduce_max(mg[:], g2[:], AX.X),
            reads=["g2"], writes=["mg"])
    ph.step("vector", lambda: nc.vector.tensor_scalar(mg[:], mg[:], -8.16, None, ALU.mult),
            reads=["mg"], writes=["mg"])
    ph.step("tensor", lambda: nc.tensor.matmul(pmb[:], st["onesfD"][:], mg[:], start=True, stop=True),
            reads=["mg", "onesf"], writes=["pmb"])
    ph.step("vector", lambda: nc.vector.tensor_copy(st["Mb"][:], pmb[:]), reads=["pmb"], writes=["Mb"])


def emit_D1(ph, reg):
    nc, prog = ph.nc, ph.prog
    st = prog.st
    x2 = reg.get("x2", *STREAMS["x2"])
    cosr_h = reg.get("cosrep", (T, 512), F32)
    sinr_h = reg.get("sinrep", (T, 512), F32)
    qfmT = reg.get("q_tm_hbm", (T, D), BF16)
    kfmT = reg.get("k_tm_hbm", (T, D), BF16)
    wq_h = reg.get("L1_attn_wq", (D, D), F32)
    wk_h = reg.get("L1_attn_wk", (D, D), F32)
    wv_h = reg.get("L1_attn_wv", (D, D), F32)
    wts = [("wqT", wq_h), ("wkT", wk_h), ("wvT", wv_h)]

    wT = {}
    wrow = [ph.sb(f"wrow{i}", [128, D], BF16) for i in range(2)]
    ptd = ph.ps("ptd", [128, 128], BF16)
    for nm, src_h in wts:
        wT[nm] = ph.sb(nm, [128, 8 * D], BF16)
        for cc in range(8):
            b = cc % 2
            ph.dma("gpsimd", lambda src_h=src_h, cc=cc, b=b: nc.gpsimd.dma_start(
                out=wrow[b][:], in_=src_h[128 * cc:128 * (cc + 1), :]), writes=[("wrow", b)])
            for dc in range(8):
                ph.step("tensor", lambda b=b, dc=dc: nc.tensor.transpose(
                    ptd[:], wrow[b][:, 128 * dc:128 * (dc + 1)], st["idbD"][:]),
                    reads=[("wrow", b), "idb"], writes=["ptd"])
                ph.step("vector", lambda nm=nm, cc=cc, dc=dc: nc.vector.tensor_copy(
                    wT[nm][:, D * dc + 128 * cc:D * dc + 128 * (cc + 1)], ptd[:]),
                    reads=["ptd"], writes=[nm])

    ln = LNHelper(ph, "lnD", 2)
    xt = [ph.sb(f"xt{i}", [128, D], F32) for i in range(2)]
    y3b = [ph.sb(f"y3b{i}", [128, D], BF16) for i in range(2)]
    y3T_b = [ph.sb(f"y3T{i}", [128, 8 * 128], BF16) for i in range(2)]
    qn_b = [ph.sb(f"qn{i}", [128, D], F32) for i in range(2)]
    rt_b = [[ph.sb(f"rt{i}_{j}", [128, 512], F32) for i in range(2)] for j in range(2)]
    qr_b = [ph.sb(f"qr{i}", [128, D], BF16) for i in range(2)]
    cosb_b = [ph.sb(f"cosb{i}", [128, 512], F32) for i in range(2)]
    sinb_b = [ph.sb(f"sinb{i}", [128, 512], F32) for i in range(2)]
    ssq_b = [ph.sb(f"ssq{i}", [128, 32], F32) for i in range(2)]
    rstd_b = [ph.sb(f"rstd{i}", [128, 32], F32) for i in range(2)]

    pmm_m = {nm: [ph.ps(f"pmm_{nm}_{i}", [128, 512], F32) for i in range(2)]
             for nm in ("wqT", "wkT", "wvT")}
    ptq = ph.ps("ptq", [128, 128], BF16)

    qraw_b = {nm: [ph.sb(f"qraw_{nm}{i}", [128, D], F32) for i in range(2)]
              for nm in ("wqT", "wkT")}

    def stageA1(t):
        b = t % 2
        y3T = y3T_b[b]
        cosb = cosb_b[b]
        sinb = sinb_b[b]
        Y3T = lambda dc, b=b: ("y3T", b, dc)
        COSB, SINB = ("cosb", b), ("sinb", b)
        XT, Y3, Y3B = ("xt", b), ("y3", b), ("y3b", b)
        ph.dma("sync", lambda t=t, b=b: nc.sync.dma_start(
            out=xt[b][:], in_=x2[128 * t:128 * (t + 1), :]), writes=[XT])
        ln.emit(b, xt[b], XT, y3b[b], Y3B)
        for dc in range(8):
            pt = ptq if dc % 2 == 0 else ptd
            PTK = "ptq" if dc % 2 == 0 else "ptd"
            ph.step("tensor", lambda b=b, dc=dc, pt=pt: nc.tensor.transpose(
                pt[:], y3b[b][:, 128 * dc:128 * (dc + 1)], st["idbD"][:]),
                reads=[Y3B, "idb"], writes=[PTK])
            ph.step("scalar", lambda dc=dc, y3T=y3T, pt=pt: nc.scalar.copy(
                y3T[:, 128 * dc:128 * (dc + 1)], pt[:]),
                reads=[PTK], writes=[Y3T(dc)])
        ph.dma("sync", lambda t=t, cosb=cosb: nc.sync.dma_start(
            out=cosb[:], in_=cosr_h[128 * t:128 * (t + 1), :]), writes=[COSB])
        ph.dma("sync", lambda t=t, sinb=sinb: nc.sync.dma_start(
            out=sinb[:], in_=sinr_h[128 * t:128 * (t + 1), :]), writes=[SINB])
        for nm in ("wqT", "wkT"):
            pmm = pmm_m[nm]
            PMM = lambda half, nm=nm: ("pmm", nm, half)
            for half in range(2):
                for dc in range(8):
                    ph.step("tensor", lambda nm=nm, half=half, dc=dc, pmm=pmm, y3T=y3T: nc.tensor.matmul(
                        pmm[half][:], y3T[:, 128 * dc:128 * (dc + 1)],
                        wT[nm][:, D * dc + 512 * half:D * dc + 512 * (half + 1)],
                        start=(dc == 0), stop=(dc == 7)),
                        reads=[Y3T(dc), nm], writes=[PMM(half)])

    def stageA2(t):
        b = t % 2
        y3T = y3T_b[b]
        Y3T = lambda dc, b=b: ("y3T", b, dc)
        pmm = pmm_m["wvT"]
        PMMV = lambda half: ("pmm", "wvT", half)
        for half in range(2):
            for dc in range(8):
                ph.step("tensor", lambda half=half, dc=dc, pmm=pmm, y3T=y3T: nc.tensor.matmul(
                    pmm[half][:], y3T[:, 128 * dc:128 * (dc + 1)],
                    wT["wvT"][:, D * dc + 512 * half:D * dc + 512 * (half + 1)],
                    start=(dc == 0), stop=(dc == 7)),
                    reads=[Y3T(dc), "wvT"], writes=[PMMV(half)])
        for nm in ("wqT", "wkT"):
            qraw = qraw_b[nm][b]
            pmq = pmm_m[nm]
            for half in range(2):
                ph.step("scalar", lambda half=half, pmq=pmq, qraw=qraw: nc.scalar.copy(
                    qraw[:, 512 * half:512 * (half + 1)], pmq[half][:]),
                    reads=[("pmm", nm, half)], writes=[("qraw", nm, b)])

    def stageBhead(t):
        b = t % 2
        rt = rt_b[b]
        ssq = ssq_b[b]
        rstd = rstd_b[b]
        RT = lambda i, b=b: ("rt", b, i)
        SSQ, RSTD = ("ssq", b), ("rstd", b)
        pmm = pmm_m["wvT"]
        for half in range(2):
            ph.step("vector", lambda t=t, half=half, pmm=pmm: nc.vector.tensor_copy(
                st["v_all"].rearrange("p (tt h e) -> p tt h e", tt=NT, h=16)
                [:, t, 8 * half:8 * (half + 1), 0:64],
                pmm[half].rearrange("p (h e) -> p h e", h=8)),
                reads=[("pmm", "wvT", half)], writes=[("v", t)])
        for nm in ("wqT", "wkT"):
            qraw = qraw_b[nm][b]
            QRAW = ("qraw", nm, b)
            sb2 = 0 if nm == "wqT" else 16
            for half in range(2):
                ph.step("vector", lambda half=half, rt=rt, qraw=qraw: nc.vector.tensor_tensor(
                    rt[half][:], qraw[:, 512 * half:512 * (half + 1)],
                    qraw[:, 512 * half:512 * (half + 1)], ALU.mult),
                    reads=[QRAW], writes=[RT(half)])
                ph.step("vector", lambda half=half, ssq=ssq, rt=rt, sb2=sb2: nc.vector.tensor_reduce(
                    ssq[:, sb2 + 8 * half:sb2 + 8 * (half + 1)],
                    rt[half].rearrange("p (h e) -> p h e", h=8), AX.X, ALU.add),
                    reads=[RT(half)], writes=[SSQ])
        ph.step("vector", lambda rstd=rstd, ssq=ssq: nc.vector.tensor_scalar(
            rstd[:], ssq[:], 1.0 / 64.0, 1e-5, ALU.mult, ALU.add),
            reads=[SSQ], writes=[RSTD])
        ph.step("scalar", lambda rstd=rstd: nc.scalar.activation(rstd[:], rstd[:], AF.Sqrt),
                reads=[RSTD], writes=[RSTD])
        ph.step("vector", lambda rstd=rstd: nc.vector.reciprocal(rstd[:], rstd[:]),
                reads=[RSTD], writes=[RSTD])

    def stageBrest(t):
        b = t % 2
        qn = qn_b[b]
        rt = rt_b[b]
        qr = qr_b[b]
        cosb = cosb_b[b]
        sinb = sinb_b[b]
        rstd = rstd_b[b]
        QN, QR = ("qn", b), ("qr", b)
        RT = lambda i, b=b: ("rt", b, i)
        COSB, SINB, RSTD = ("cosb", b), ("sinb", b), ("rstd", b)
        for nm, dstT in (("wqT", qfmT), ("wkT", kfmT)):
            qraw = qraw_b[nm][b]
            QRAW = ("qraw", nm, b)
            rb = 0 if nm == "wqT" else 16
            ph.step("vector", lambda qn=qn, qraw=qraw, rstd=rstd, rb=rb: nc.vector.tensor_tensor(
                qn[:].rearrange("p (h f) -> p h f", h=16),
                qraw[:].rearrange("p (h f) -> p h f", h=16),
                rstd[:, rb:rb + 16].unsqueeze(2).to_broadcast([128, 16, 64]), ALU.mult),
                reads=[QRAW, RSTD], writes=[QN])
            ph.step("vector", lambda qn=qn: nc.vector.tensor_tensor(qn[:], qn[:], st["grb"][:], ALU.mult),
                    reads=[QN, "grb"], writes=[QN])
            qn4 = qn.rearrange("p (h u f) -> p h u f", h=16, u=2)
            qr4 = qr.rearrange("p (h u f) -> p h u f", h=16, u=2)
            c3 = cosb.rearrange("p (h f) -> p h f", h=16)
            s3 = sinb.rearrange("p (h f) -> p h f", h=16)
            ph.step("vector", lambda qn4=qn4, c3=c3, rt=rt: nc.vector.tensor_tensor(
                rt[0].rearrange("p (h f) -> p h f", h=16), qn4[:, :, 0, :], c3, ALU.mult),
                reads=[QN, COSB], writes=[RT(0)])
            ph.step("vector", lambda qn4=qn4, s3=s3, rt=rt: nc.vector.tensor_tensor(
                rt[1].rearrange("p (h f) -> p h f", h=16), qn4[:, :, 1, :], s3, ALU.mult),
                reads=[QN, SINB], writes=[RT(1)])
            ph.step("vector", lambda qr4=qr4, rt=rt: nc.vector.tensor_tensor(
                qr4[:, :, 0, :], rt[0].rearrange("p (h f) -> p h f", h=16),
                rt[1].rearrange("p (h f) -> p h f", h=16), ALU.add),
                reads=[RT(0), RT(1)], writes=[QR])
            ph.step("vector", lambda qn4=qn4, c3=c3, rt=rt: nc.vector.tensor_tensor(
                rt[0].rearrange("p (h f) -> p h f", h=16), qn4[:, :, 1, :], c3, ALU.mult),
                reads=[QN, COSB], writes=[RT(0)])
            ph.step("vector", lambda qn4=qn4, s3=s3, rt=rt: nc.vector.tensor_tensor(
                rt[1].rearrange("p (h f) -> p h f", h=16), qn4[:, :, 0, :], s3, ALU.mult),
                reads=[QN, SINB], writes=[RT(1)])
            ph.step("vector", lambda qr4=qr4, rt=rt: nc.vector.tensor_tensor(
                qr4[:, :, 1, :], rt[0].rearrange("p (h f) -> p h f", h=16),
                rt[1].rearrange("p (h f) -> p h f", h=16), ALU.subtract),
                reads=[RT(0), RT(1)], writes=[QR])
            ph.dma("gpsimd", lambda t=t, qr=qr, dstT=dstT: nc.gpsimd.dma_start(
                out=dstT[128 * t:128 * (t + 1), :], in_=qr[:]),
                reads=[QR], writes=[("tm", nm, t)])

    stageA1(0)
    stageA2(0)
    for t in range(NT):
        if t + 1 < NT:
            stageA1(t + 1)
        stageBhead(t)
        if t + 1 < NT:
            stageA2(t + 1)
        stageBrest(t)



def emit_D2(ph, reg):
    nc, prog = ph.nc, ph.prog
    st = prog.st
    qtm = reg.get("q_tm_hbm", (T, D), BF16)
    ktm = reg.get("k_tm_hbm", (T, D), BF16)
    afm = reg.get("afm_hbm", (NT * D, 128), BF16)

    qhead = [ph.sb(f"qhead{i}", [128, T], BF16) for i in range(2)]
    qpairT = [ph.sb(f"qpairT{i}", [128, T], BF16) for i in range(2)]
    kpair = [ph.sb(f"kpair{i}", [128, T], BF16) for i in range(2)]
    for b in range(2):
        zb = 64 * (1 - b)
        ph.step("vector", lambda b=b, zb=zb: nc.vector.memset(
            qhead[b][zb:zb + 64, :], 0.0), writes=[("qz", b)])
    PT = [ph.sb(f"PT{i}", [128, 1024], BF16) for i in range(4)]
    rec = ph.sb("rec", [1, 512], F32)
    recb = ph.sb("recb", [64, 512], F32)
    stage = ph.sb("stage", [128, T], BF16)
    pS = [ph.ps(f"pS{i}", [128, 1024], F32) for i in range(3)]
    pPV = [ph.ps(f"pPV{i}", [65, 512], F32) for i in range(2)]

    NQB = T // 1024
    for hp in range(HP):
        bufp = hp % 2
        for t in range(NT):
            ph.dma("sync", lambda hp=hp, bufp=bufp, t=t: nc.sync.dma_start(
                out=kpair[bufp][:, 128 * t:128 * (t + 1)],
                in_=ktm[128 * t:128 * (t + 1), 128 * hp:128 * (hp + 1)],
                transpose=True),
                writes=[("kpair", bufp, t)])
        for t in range(NT):
            ph.dma("sync", lambda hp=hp, bufp=bufp, t=t: nc.sync.dma_start(
                out=qpairT[bufp][:, 128 * t:128 * (t + 1)],
                in_=qtm[128 * t:128 * (t + 1), 128 * hp:128 * (hp + 1)],
                transpose=True),
                writes=[("qpT", bufp, t)])
        for sub in range(2):
            rb = 64 * sub
            for t in range(NT):
                ph.step("vector", lambda bufp=bufp, sub=sub, rb=rb, t=t: nc.vector.tensor_copy(
                    qhead[sub][rb:rb + 64, 128 * t:128 * (t + 1)],
                    qpairT[bufp][rb:rb + 64, 128 * t:128 * (t + 1)]),
                    reads=[("qpT", bufp, t), ("qz", sub)], writes=[("qhead", sub, t)])
        for qb in range(NQB):
            for sub in range(2):
                h = 2 * hp + sub
                bufq = sub
                for kt0 in range(0, NT + 2, 2):
                    for kt in (kt0, kt0 + 1):
                        if kt >= NT:
                            continue
                        par = kt % 3
                        for qc in range(2):
                            qt0 = (1024 * qb + 512 * qc) // 128
                            ph.step("tensor", lambda qb=qb, kt=kt, qc=qc, par=par,
                                    bufp=bufp, bufq=bufq: nc.tensor.matmul(
                                pS[par][:, 512 * qc:512 * (qc + 1)],
                                kpair[bufp][:, 128 * kt:128 * (kt + 1)],
                                qhead[bufq][:,
                                            1024 * qb + 512 * qc:1024 * qb + 512 * (qc + 1)],
                                start=True, stop=True),
                                reads=[("kpair", bufp, kt)] + [("qhead", bufq, qt0 + i)
                                                               for i in range(4)],
                                writes=[("pS", par, qc)])
                    for kt in (kt0, kt0 + 1):
                        if kt >= NT:
                            continue
                        par = kt % 3
                        par4 = kt % 4
                        ph.step("scalar", lambda kt=kt, par=par, par4=par4: nc.scalar.activation(
                            PT[par4][:], pS[par][:], AF.Exp, bias=st["Mb"][:], scale=0.125),
                            reads=[("pS", par, 0), ("pS", par, 1), "Mb"],
                            writes=[("PT", par4)])
                    for kp in (kt0 - 2, kt0 - 1):
                        if kp < 0 or kp >= NT:
                            continue
                        pp = kp % 4
                        for qc in range(2):
                            ph.step("tensor", lambda h=h, kp=kp, qc=qc, pp=pp: nc.tensor.matmul(
                                pPV[qc][:], st["v_all"][:, 1040 * kp + 65 * h:1040 * kp + 65 * (h + 1)],
                                PT[pp][:, 512 * qc:512 * (qc + 1)],
                                start=(kp == 0), stop=(kp == NT - 1)),
                                reads=["v_ones", ("v", kp), ("PT", pp)],
                                writes=[("pPV", qc)])
                for qc in range(2):
                    ph.step("scalar", lambda qc=qc: nc.scalar.activation(
                        rec[:], pPV[qc][64:65, :], AF.Ln),
                        reads=[("pPV", qc)], writes=["rec"])
                    ph.step("scalar", lambda: nc.scalar.activation(rec[:], rec[:], AF.Exp, scale=-1.0),
                            reads=["rec"], writes=["rec"])
                    ph.step("tensor", lambda qc=qc: nc.tensor.matmul(
                        pS[0][0:64, 0:512], st["onesfD"][0:1, 0:64], rec[:],
                        start=True, stop=True),
                        reads=["rec", "onesf"], writes=[("pS", 0, 0)])
                    ph.step("vector", lambda: nc.vector.tensor_copy(recb[:], pS[0][0:64, 0:512]),
                            reads=[("pS", 0, 0)], writes=["recb"])
                    ph.step("vector", lambda sub=sub, qb=qb, qc=qc: nc.vector.tensor_tensor(
                        stage[64 * sub:64 * (sub + 1),
                              1024 * qb + 512 * qc:1024 * qb + 512 * (qc + 1)],
                        pPV[qc][0:64, :], recb[:], ALU.mult),
                        reads=[("pPV", qc), "recb"], writes=[("stage", sub, qb)])
            for t in range(8 * qb, 8 * qb + 8):
                ph.dma("gpsimd", lambda hp=hp, t=t: nc.gpsimd.dma_start(
                    out=afm[D * t + 128 * hp:D * t + 128 * (hp + 1), :],
                    in_=stage[:, 128 * t:128 * (t + 1)]),
                    reads=[("stage", 0, qb), ("stage", 1, qb)], writes=[("afmh", hp, t)])



def emit_D3(ph, reg):
    nc, prog = ph.nc, ph.prog
    st = prog.st
    x2 = reg.get("x2", *STREAMS["x2"])
    x3 = reg.get("x3", *STREAMS["x3"])
    afm = reg.get("afm_hbm", (NT * D, 128), BF16)
    wproj = reg.get("L1_attn_wproj", (D, D), F32)

    wprojT = ph.sb("wprojT", [128, 8 * D], BF16)
    wrow = [ph.sb(f"wrow{i}", [128, D], BF16) for i in range(2)]
    ptd = ph.ps("ptd", [128, 128], BF16)
    for cc in range(8):
        b = cc % 2
        ph.dma("gpsimd", lambda cc=cc, b=b: nc.gpsimd.dma_start(
            out=wrow[b][:], in_=wproj[128 * cc:128 * (cc + 1), :]), writes=[("wrow", b)])
        for dc in range(8):
            ph.step("tensor", lambda b=b, dc=dc: nc.tensor.transpose(
                ptd[:], wrow[b][:, 128 * dc:128 * (dc + 1)], st["idbD"][:]),
                reads=[("wrow", b), "idb"], writes=["ptd"])
            ph.step("vector", lambda cc=cc, dc=dc: nc.vector.tensor_copy(
                wprojT[:, D * dc + 128 * cc:D * dc + 128 * (cc + 1)], ptd[:]),
                reads=["ptd"], writes=["wprojT"])

    at = [ph.sb(f"at{i}", [128, 8 * 128], BF16) for i in range(2)]
    xt = [ph.sb(f"xt{i}", [128, D], F32) for i in range(2)]
    x3t = [ph.sb(f"x3t{i}", [128, D], F32) for i in range(2)]
    pso = ph.ps("pso", [128, D], F32)

    for t in range(NT):
        b = t % 2
        ph.dma("sync", lambda t=t, b=b: nc.sync.dma_start(
            out=at[b][:].rearrange("p (cc tt) -> p cc tt", cc=8),
            in_=afm[D * t:D * (t + 1), :].rearrange("(cc p) tt -> p cc tt", p=128)),
            writes=[("at", b)])
        ph.dma("sync", lambda t=t, b=b: nc.sync.dma_start(
            out=xt[b][:], in_=x2[128 * t:128 * (t + 1), :]), writes=[("xt", b)])
        for cc in range(8):
            for half in range(2):
                ph.step("tensor", lambda b=b, cc=cc, half=half: nc.tensor.matmul(
                    pso[:, 512 * half:512 * (half + 1)],
                    at[b][:, 128 * cc:128 * (cc + 1)],
                    wprojT[:, D * cc + 512 * half:D * cc + 512 * (half + 1)],
                    start=(cc == 0), stop=(cc == 7)),
                    reads=[("at", b), "wprojT"], writes=[("pso", half)])
        ph.step("vector", lambda b=b: nc.vector.tensor_tensor(
            x3t[b][:], pso[:], xt[b][:], ALU.add),
            reads=[("pso", 0), ("pso", 1), ("xt", b)], writes=[("x3t", b)])
        ph.dma("gpsimd", lambda t=t, b=b: nc.gpsimd.dma_start(
            out=x3[128 * t:128 * (t + 1), :], in_=x3t[b][:]),
            reads=[("x3t", b)], writes=[("x3h", t)])



def emit_E(ph, reg):
    nc = ph.nc
    x3 = reg.get("x3", *STREAMS["x3"])
    out = reg.get("out", (T, D), F32)
    Wf = reg.get("L1_ffn_weight", (D, D), F32)
    ident = reg.get("ident_bf", (128, 128), BF16)

    idb = ph.sb("idb", [128, 128], BF16)
    ph.dma("sync", lambda: nc.sync.dma_start(out=idb[:], in_=ident), writes=["idb"])

    wrow = ph.sb("wrow", [128, D], F32)
    wsgn = ph.sb("wsgn", [128, D], BF16)
    sgnT = ph.sb("sgnT", [128, 8 * D], BF16)
    ptr = [ph.ps(f"ptr{i}", [128, 128], BF16) for i in range(2)]
    for oc in range(8):
        ph.dma("sync", lambda oc=oc: nc.sync.dma_start(out=wrow[:], in_=Wf[128 * oc:128 * (oc + 1), :]),
               writes=[("wrow",)])
        ph.step("scalar", lambda: nc.scalar.activation(wsgn[:], wrow[:], AF.Sign),
                reads=[("wrow",)], writes=[("wsgn",)])
        for dc in range(8):
            pb = dc % 2
            ph.step("tensor", lambda oc=oc, dc=dc, pb=pb: nc.tensor.transpose(
                ptr[pb][:], wsgn[:, 128 * dc:128 * (dc + 1)], idb[:]),
                reads=[("wsgn",), "idb"], writes=[("ptr", pb)])
            ph.step("vector", lambda oc=oc, dc=dc, pb=pb: nc.vector.tensor_copy(
                sgnT[:, dc * D + 128 * oc: dc * D + 128 * (oc + 1)], ptr[pb][:]),
                reads=[("ptr", pb)], writes=[("sgnT",)])

    ph.dbg = dict(sgnT=sgnT, wsgn=wsgn)

    nb = 2
    xt = [ph.sb(f"xt{i}", [128, D], F32) for i in range(nb)]
    yb = [ph.sb(f"yb{i}", [128, D], BF16) for i in range(nb)]
    yT = [ph.sb(f"yT{i}", [128, D], BF16) for i in range(nb)]
    stats2 = [ph.sb(f"stats{i}", [128, 12], F32) for i in range(nb)]
    mv2 = [ph.sb(f"mv{i}", [128, 2], F32) for i in range(nb)]
    rstd2 = [ph.sb(f"rstd{i}", [128, 1], F32) for i in range(nb)]
    res = [ph.sb(f"res{i}", [128, D], F32) for i in range(nb)]
    pmm = [ph.ps(f"pmm{i}", [128, D], F32) for i in range(nb)]
    ptr2 = [ph.ps(f"ptr2_{i}", [128, 128], BF16) for i in range(2)]

    def stageEA(t):
        b = t % nb
        stats, mv, rstd = stats2[b], mv2[b], rstd2[b]
        XT, YB, YT, PMM = ("xt", b), ("yb", b), ("yT", b), ("pmm", b)
        ST_, MV_, RS_ = ("st", b), ("mv", b), ("rstd", b)
        ph.dma("sync", lambda t=t, b=b: nc.sync.dma_start(out=xt[b][:], in_=x3[128 * t:128 * (t + 1), :]),
               writes=[XT])
        ph.step("vector", lambda b=b, stats=stats: nc.vector.bn_stats(stats[:, 0:6], xt[b][:, 0:512]),
                reads=[XT], writes=[ST_])
        ph.step("vector", lambda b=b, stats=stats: nc.vector.bn_stats(stats[:, 6:12], xt[b][:, 512:1024]),
                reads=[XT], writes=[ST_])
        ph.step("vector", lambda mv=mv, stats=stats: nc.vector.bn_aggr(mv[:], stats[:]), reads=[ST_], writes=[MV_])
        ph.step("vector", lambda rstd=rstd, mv=mv: nc.vector.tensor_scalar(rstd[:], mv[:, 1:2], 1e-5, None, ALU.add),
                reads=[MV_], writes=[RS_])
        ph.step("scalar", lambda rstd=rstd: nc.scalar.activation(rstd[:], rstd[:], AF.Sqrt),
                reads=[RS_], writes=[RS_])
        ph.step("vector", lambda rstd=rstd: nc.vector.reciprocal(rstd[:], rstd[:]),
                reads=[RS_], writes=[RS_])
        ph.step("vector", lambda b=b, mv=mv, rstd=rstd: nc.vector.tensor_scalar(yb[b][:], xt[b][:], mv[:, 0:1], rstd[:],
                                                              ALU.subtract, ALU.mult),
                reads=[XT, MV_, RS_], writes=[YB])
        for dc in range(8):
            pb = dc % 2
            ph.step("tensor", lambda b=b, dc=dc, pb=pb: nc.tensor.transpose(
                ptr2[pb][:], yb[b][:, 128 * dc:128 * (dc + 1)], idb[:]),
                reads=[YB, "idb"], writes=[("ptr2", pb)])
            ph.step("scalar", lambda b=b, dc=dc, pb=pb: nc.scalar.copy(
                yT[b][:, 128 * dc:128 * (dc + 1)], ptr2[pb][:]),
                reads=[("ptr2", pb)], writes=[YT])
        for half in range(2):
            for dc in range(8):
                ph.step("tensor", lambda b=b, dc=dc, half=half: nc.tensor.matmul(
                    pmm[b][:, 512 * half:512 * (half + 1)],
                    yT[b][:, 128 * dc:128 * (dc + 1)],
                    sgnT[:, dc * D + 512 * half: dc * D + 512 * (half + 1)],
                    start=(dc == 0), stop=(dc == 7)),
                    reads=[YT, "sgnT"], writes=[PMM])

    def stageEB(t):
        b = t % nb
        XT, RES, PMM = ("xt", b), ("res", b), ("pmm", b)
        ph.step("vector", lambda b=b: nc.vector.tensor_tensor(res[b][:], pmm[b][:], xt[b][:], ALU.add),
                reads=[PMM, XT], writes=[RES])
        ph.dma("gpsimd", lambda t=t, b=b: nc.gpsimd.dma_start(out=out[128 * t:128 * (t + 1), :], in_=res[b][:]),
               reads=[RES], writes=[("out_hbm", t)])

    stageEA(0)
    for t in range(1, NT):
        stageEA(t)
        stageEB(t - 1)
    stageEB(NT - 1)
    ph.dbg.update(yb=yb, xt=xt)


PHASE_EMITTERS = {"A": emit_A, "B": emit_B,
                  "C1": emit_C1, "C1b": emit_C1b, "C3": emit_C3,
                  "D0": emit_D0, "D1": emit_D1,
                  "D2": emit_D2, "D3": emit_D3, "E": emit_E}



GROUP_IO = {
    ("A", "B"): (["x0"], ["x1"]),
    ("C",): (["x1"], ["x2"]),
    ("D1", "D2", "D3"): (["x2"], ["x3"]),
    ("E",): (["x3"], ["out"]),
}

WEIGHT_NAMES = [
    "L0_attn_norm_g", "L0_attn_norm_b", "L0_attn_W_h", "L0_attn_b_h",
    "L0_attn_W_qk", "L0_attn_b_qk", "L0_attn_qk_gamma", "L0_attn_qk_beta",
    "L0_attn_rel_emb", "L0_attn_W_o", "L0_attn_b_o",
    "L0_ffn_w_gate", "L0_ffn_w1", "L0_ffn_w2", "L0_ffn_w3",
    "L0_ffn_sw1", "L0_ffn_sw2", "L0_ffn_sw3", "L0_ffn_expert_bias",
    "L1_attn_wq", "L1_attn_wk", "L1_attn_wv", "L1_attn_wproj", "L1_attn_g_rms",
    "L1_ffn_weight",
]


def kernel(**inputs):
    consts = host_constants()
    x_full = np.ascontiguousarray(inputs["L0_attn_x"])

    host_common = {k: np.ascontiguousarray(inputs[k]) for k in WEIGHT_NAMES}
    host_common["L0_attn_norm_g"] = host_common["L0_attn_norm_g"].reshape(1, D)
    host_common["L0_attn_norm_b"] = host_common["L0_attn_norm_b"].reshape(1, D)
    host_common["L0_attn_b_h"] = host_common["L0_attn_b_h"].reshape(1, 4096)
    host_common["L0_attn_b_qk"] = host_common["L0_attn_b_qk"].reshape(128, 1)
    host_common["L0_attn_b_o"] = host_common["L0_attn_b_o"].reshape(1, D)
    host_common["L1_attn_g_rms"] = host_common["L1_attn_g_rms"].reshape(1, 64)
    host_common["L0_ffn_expert_bias"] = host_common["L0_ffn_expert_bias"].reshape(1, 8)
    host_common["L0_ffn_w1"] = host_common["L0_ffn_w1"].reshape(8 * D, 512)
    host_common["L0_ffn_w2"] = host_common["L0_ffn_w2"].reshape(8 * 512, D)
    host_common["L0_ffn_w3"] = host_common["L0_ffn_w3"].reshape(8 * D, 512)
    host_common["grms_rep"] = np.ascontiguousarray(
        np.tile(inputs["L1_attn_g_rms"].reshape(1, 64), (1, 16)).astype(np.float32))
    host_common.update(consts)
    per_core = [dict() for _ in range(N_CORES)]
    for c in range(N_CORES):
        per_core[c]["x0"] = np.ascontiguousarray(x_full[c])
        per_core[c]["mask_core"] = np.full((8, 1), 1.0 if c == 0 else 0.0, np.float32)
        per_core[c]["posbase"] = np.full((1, 1), float(c * 2 * T), np.float32)
        oh = np.zeros((1, 8), np.float32); oh[0, c] = 1.0
        per_core[c]["slot_onehot"] = oh

    for group in LAUNCH_PLAN:
        gkey = tuple(group)
        ins, outs = GROUP_IO_FUSED(gkey)
        nc, reg, prog = build_program(group, ins, outs)
        in_maps = []
        for c in range(N_CORES):
            m = {}
            for name in reg.map:
                if name in reg.ext_in:
                    if name in per_core[c]:
                        m[name] = per_core[c][name]
                    else:
                        m[name] = host_common[name]
            in_maps.append(m)
        res = run_bass_kernel_spmd(nc, in_maps, core_ids=list(range(N_CORES)))
        for c in range(N_CORES):
            for name in reg.ext_out:
                if name in reg.map:
                    per_core[c][name] = res.results[c][name]

    return np.stack([per_core[c]["out"] for c in range(N_CORES)], axis=0)


def GROUP_IO_FUSED(gkey):
    produced = set()
    ins, outs = set(), set()
    for p in gkey:
        need, prod = PH_IO[p]
        for n in need:
            if n not in produced:
                ins.add(n)
        for o in prod:
            produced.add(o)
    all_later_needs = set()
    for grp, (need, prod) in PH_IO.items():
        if grp not in gkey:
            all_later_needs |= set(need)
    for o in produced:
        if o == "out" or o in all_later_needs:
            outs.add(o)
    ins |= set(WEIGHT_NAMES) | {"ident_bf", "ident_f32", "OH511", "cosrep",
                                "sinrep", "T8", "mask_core", "posbase", "slot_onehot",
                                "grms_rep"}
    ins -= outs
    return ins, outs
